# version 06
# baseline (speedup 1.0000x reference)
import math
from contextlib import ExitStack
import numpy as np
import concourse.bass as bass
import concourse.mybir as mybir
from concourse.bass_utils import run_bass_kernel_spmd

F32 = mybir.dt.float32
BF16 = mybir.dt.bfloat16
I32 = mybir.dt.int32
AF = mybir.ActivationFunctionType
ALU = mybir.AluOpType
AX = mybir.AxisListType

N_CORES = 8
B = 2
T = 2048
D = 1024
NCG = D // 128

ENGINES = ("sync", "tensor", "scalar", "vector", "gpsimd")


class Res:
    __slots__ = ("writer", "readers", "name")

    def __init__(self, name=""):
        self.writer = None
        self.readers = []
        self.name = name


def resv(n, name=""):
    return [Res(f"{name}{i}") for i in range(n)]


class Prog:
    def __init__(self, nc):
        self.nc = nc
        self.steps = []
        self.counts = {e: 0 for e in ENGINES}

    def step(self, engine, fn, reads=(), writes=(), after=(), dma=0, incs=None):
        deps = list(after)
        for r in reads:
            if r is not None and r.writer is not None:
                deps.append(r.writer)
        for w in writes:
            if w is None:
                continue
            if w.writer is not None:
                deps.append(w.writer)
            deps.extend(w.readers)
        waits = {}
        for h in deps:
            if h is None:
                continue
            e2, c2 = h
            if e2 == engine:
                continue
            if c2 > waits.get(e2, 0):
                waits[e2] = c2
        n = 16 * dma if dma else (1 if incs is None else incs)
        self.counts[engine] += n
        h = (engine, self.counts[engine])
        self.steps.append((engine, fn, waits, n, dma))
        for r in reads:
            if r is not None:
                r.readers.append(h)
        for w in writes:
            if w is not None:
                w.writer = h
                w.readers = []
        return h

    def fence(self):
        return [(e, c) for e, c in self.counts.items() if c > 0]

    def barrier(self):
        snap = [(e, c) for e, c in self.counts.items() if c > 0]
        for e in ENGINES:
            self.step(e, None, incs=0, after=snap)

    def emit(self):
        nc = self.nc
        sems = nc._phase_sems
        with nc.Block() as blk:
            for name in ENGINES:
                body = getattr(blk, name)

                def make(nm):
                    def run(eng):
                        self._emit_engine(nm, eng, sems)
                    return run
                body(make(name))

    def _emit_engine(self, name, eng, sems):
        waited = {e: 0 for e in ENGINES}
        sem = sems[name]
        for engine, fn, waits, incs, dma in self.steps:
            if engine != name:
                continue
            for e2, c2 in sorted(waits.items()):
                if c2 > waited[e2]:
                    eng.wait_ge(sems[e2], c2)
                    waited[e2] = c2
            if dma:
                fn(eng, sem)
            elif incs:
                last = fn(eng)
                last.then_inc(sem, incs)
            elif fn is not None:
                fn(eng)


def new_nc():
    nc = bass.Bass("TRN2", target_bir_lowering=False, debug=False)
    nc._phase_sems = {}
    nc._sem_ctxs = []
    nc._dram_cache = {}
    _orig = nc.dram_tensor

    def dram_cached(name, shape, dtype, kind="Internal"):
        if name in nc._dram_cache:
            return nc._dram_cache[name]
        h = _orig(name, shape, dtype, kind=kind)
        nc._dram_cache[name] = h
        return h
    nc.dram_tensor = dram_cached
    for e in ENGINES:
        ctx = nc.semaphore(f"sem_{e}")
        h = ctx.__enter__()
        nc._sem_ctxs.append(ctx)
        nc._phase_sems[e] = h
    return nc


def load_wb16(p, src_ap, wsb, wres, stage, stres, rows, cols, scale=None, dcol=0):
    if rows < 128:
        p.step("gpsimd", lambda eng, sem: eng.dma_start(
            out=wsb[0:rows, 0, dcol:dcol + cols], in_=src_ap).then_inc(sem, 16),
            writes=(wres,), dma=1)
        return
    k = rows // 128
    if scale is None:
        for i in range(k):
            p.step("gpsimd", lambda eng, sem, i=i: eng.dma_start(
                out=wsb[:, i, dcol:dcol + cols],
                in_=src_ap[i * 128:(i + 1) * 128, :]).then_inc(sem, 16),
                writes=(wres,), dma=1)
        return
    for i in range(k):
        sres = stres[i % len(stres)]
        st = stage[i % len(stres)]
        p.step("sync", lambda eng, sem, i=i, st=st: eng.dma_start(
            out=st[:, 0:cols], in_=src_ap[i * 128:(i + 1) * 128, :]).then_inc(sem, 16),
            writes=(sres,), dma=1)
        p.step("scalar", lambda en, i=i, st=st: en.activation(
            wsb[:, i, dcol:dcol + cols], st[:, 0:cols], AF.Copy, scale=scale),
            reads=(sres,), writes=(wres,))


def colload(p, src_ap, dst_slice, cres):
    p.step("sync", lambda eng, sem: eng.dma_start(
        out=dst_slice, in_=src_ap).then_inc(sem, 16), writes=(cres,), dma=1)


def _uname(nc, n):
    i = getattr(nc, "_uid", 0)
    nc._uid = i + 1
    return f"{n}_{i}"


def sball(nc, st, *specs):
    return [st.enter_context(nc.sbuf_tensor(_uname(nc, n), s, d)) for n, s, d in specs]


def psall(nc, st, *specs):
    return [st.enter_context(nc.psum_tensor(_uname(nc, n), s, d)) for n, s, d in specs]


def emit_ln(p, E, src, r_src, out, r_out, ng, gcol=None, bcol=None):
    if not callable(src):
        _s = src
        src = lambda g: _s[:, g, :]
    if not callable(out):
        _o = out
        out = lambda g: _o[:, g, :]
    """LayerNorm over ng*128 channels: out = (src - mu)*rstd [*g + b].
    E: env dict with c16/r_c16, sq16/r_sq, dd/r_dd, ps_s1/r_s1, ps_s2/r_s2,
    ps_mu/r_mu, ps_r/r_r, rows (2 row APs), r_row, ones128, onesb, r_ones,
    kc eps col, r_kc. src/out/dd slabs indexed [:, g, :]."""
    DN = ng * 128
    for g in range(ng):
        p.step("scalar", lambda e, g=g: e.copy(E["c16"][:, g, :], src(g)),
               reads=(r_src[g],), writes=(E["r_c16"][g],))

    def f_s1(e):
        for g in range(ng):
            ins = e.matmul(E["ps_s1"][:], E["ones128"][:], E["c16"][:, g, :],
                           start=(g == 0), stop=(g == ng - 1))
        return ins
    p.step("tensor", f_s1, reads=tuple(E["r_c16"]) + (E["r_ones"],), writes=(E["r_s1"],))
    p.step("scalar", lambda e: e.activation(E["rows"][0], E["ps_s1"][:], AF.Copy, scale=1.0 / DN),
           reads=(E["r_s1"],), writes=(E["r_row"][0],))
    p.step("tensor", lambda e: e.matmul(E["ps_mu"][:], E["onesb"][:], E["rows"][0],
                                        start=True, stop=True),
           reads=(E["r_row"][0], E["r_ones"]), writes=(E["r_mu"],))
    for g in range(ng):
        p.step("vector", lambda e, g=g: e.tensor_sub(E["dd"][:, g, :], src(g), E["ps_mu"][:]),
               reads=(r_src[g], E["r_mu"]), writes=(E["r_dd"][g],))
        p.step("scalar", lambda e, g=g: e.activation(E["sq16"][:, g, :], E["dd"][:, g, :], AF.Square),
               reads=(E["r_dd"][g],), writes=(E["r_sq"][g],))

    def f_s2(e):
        for g in range(ng):
            ins = e.matmul(E["ps_s2"][:], E["ones128"][:], E["sq16"][:, g, :],
                           start=(g == 0), stop=(g == ng - 1))
        return ins
    p.step("tensor", f_s2, reads=tuple(E["r_sq"]) + (E["r_ones"],), writes=(E["r_s2"],))
    p.step("scalar", lambda e: e.activation(E["rows"][1], E["ps_s2"][:], AF.Sqrt,
                                            scale=1.0 / DN, bias=E["kc"]),
           reads=(E["r_s2"], E["r_kc"]), writes=(E["r_row"][1],))
    p.step("vector", lambda e: e.reciprocal(E["rows"][1], E["rows"][1]),
           reads=(E["r_row"][1],), writes=(E["r_row"][1],))
    p.step("tensor", lambda e: e.matmul(E["ps_r"][:], E["onesb"][:], E["rows"][1],
                                        start=True, stop=True),
           reads=(E["r_row"][1], E["r_ones"]), writes=(E["r_r"],))
    for g in range(ng):
        if gcol is None:
            p.step("vector", lambda e, g=g: e.tensor_mul(out(g), E["dd"][:, g, :], E["ps_r"][:]),
                   reads=(E["r_dd"][g], E["r_r"]), writes=(r_out[g],))
        else:
            p.step("vector", lambda e, g=g: e.tensor_mul(E["dd"][:, g, :], E["dd"][:, g, :], E["ps_r"][:]),
                   reads=(E["r_dd"][g], E["r_r"]), writes=(E["r_dd"][g],))
            p.step("vector", lambda e, g=g: e.tensor_scalar(
                out(g), E["dd"][:, g, :], gcol(g), bcol(g), ALU.mult, ALU.add),
                reads=(E["r_dd"][g], E["r_cols"]), writes=(r_out[g],))


def build_p0a(nc, p, kinds):
    TC = 512
    x0 = nc.dram_tensor("x0", [B, NCG, 128, T], F32, kind=kinds.get("x0", "ExternalInput")).ap()
    pwT = nc.dram_tensor("l0_pwT", [D, D], F32, kind="ExternalInput").ap()
    gruT = nc.dram_tensor("l0_gruT", [D, 2 * D], F32, kind="ExternalInput").ap()
    b_pw = nc.dram_tensor("l0_b_pw", [128, NCG], F32, kind="ExternalInput").ap()
    w_dw = nc.dram_tensor("l0_w_dw", [128, 3 * NCG], F32, kind="ExternalInput").ap()
    b_dw = nc.dram_tensor("l0_b_dw", [128, NCG], F32, kind="ExternalInput").ap()
    g_conv = nc.dram_tensor("l0_g_conv", [128, NCG], F32, kind="ExternalInput").ap()
    g_gru = nc.dram_tensor("l0_g_gru", [128, NCG], F32, kind="ExternalInput").ap()
    y2o = nc.dram_tensor("y2s", [B, NCG, 128, T], F32, kind=kinds.get("y2s", "ExternalOutput")).ap()

    nch = T // TC
    with ExitStack() as est:
        (stage_t, w_pw, w_gru, ones128, onesb, cols, xb, xb16, dd, sq16, u1b,
         cvt, cv16, ev, u2b, ga, gs, ggt, gb, hsc, rowt, carry, kc) = sball(nc, est,
            ("stage", [128, 2, 2048], F32), ("w_pw", [128, NCG, D], BF16),
            ("w_gru", [128, NCG, 2 * D], BF16), ("ones128", [128, 1], BF16),
            ("onesb", [1, 128], F32), ("cols", [128, 7 * NCG], F32),
            ("xb", [128, NCG, 512], F32), ("xb16", [128, NCG, 512], BF16),
            ("dd", [128, NCG, 512], F32), ("sq16", [128, NCG, 512], BF16),
            ("u1b", [128, 2, NCG, 514], BF16), ("cvt", [128, 2, 512], F32),
            ("cv16", [128, NCG, 512], BF16), ("ev", [128, 2, 512], F32),
            ("u2b", [128, NCG, 512], BF16), ("ga", [128, 2, 512], F32),
            ("gs", [128, 2, 512], F32), ("ggt", [128, 2, 512], F32),
            ("gb", [128, 2, 512], F32), ("hsc", [128, NCG, 512], F32),
            ("rowt", [1, 2, 512], F32), ("carry", [128, NCG], F32),
            ("kc", [128, 1], F32))
        (ps_s1, ps_s2, ps_mu, ps_r, ps_m0, ps_m1) = psall(nc, est,
            ("ps_s1", [1, 512], F32), ("ps_s2", [1, 512], F32),
            ("ps_mu", [128, 512], F32), ("ps_r", [128, 512], F32),
            ("ps_m0", [128, 512], F32), ("ps_m1", [128, 512], F32))
        stage = [stage_t[:, 0, :], stage_t[:, 1, :]]
        r_stage = resv(2, "stage")
        r_wpw, r_wgru, r_cols, r_ones = Res("wpw"), Res("wgru"), Res("cols"), Res("ones")
        r_xb = resv(NCG, "xb")
        r_xb16 = resv(NCG, "xb16")
        r_dd = resv(NCG, "dd")
        r_sq = resv(NCG, "sq")
        r_u1 = [resv(NCG, f"u1s{s}") for s in range(2)]
        r_cvt = resv(2, "cvt")
        r_cv16 = resv(NCG, "cv16")
        r_ev = resv(2, "ev")
        r_u2 = resv(NCG, "u2")
        r_ga = resv(2, "ga")
        r_gs = resv(2, "gs")
        r_gg = resv(2, "gg")
        r_gb = resv(2, "gb")
        r_hsc = resv(NCG, "hsc")
        r_carry = resv(NCG, "carry")
        r_row = resv(2, "row")
        r_s1, r_s2, r_mu, r_r, r_m0, r_m1 = (Res("s1"), Res("s2"), Res("mu"),
                                             Res("r"), Res("m0"), Res("m1"))

        load_wb16(p, pwT, w_pw, r_wpw, stage, r_stage, D, D)
        load_wb16(p, gruT, w_gru, r_wgru, stage, r_stage, D, 2 * D)
        p.step("vector", lambda e: e.memset(ones128[:], 1.0), writes=(r_ones,))
        p.step("vector", lambda e: e.memset(onesb[:], 1.0), writes=(r_ones,))
        colload(p, b_pw, cols[:, 0:NCG], r_cols)
        colload(p, b_dw, cols[:, NCG:2 * NCG], r_cols)
        colload(p, g_conv, cols[:, 2 * NCG:3 * NCG], r_cols)
        colload(p, g_gru, cols[:, 3 * NCG:4 * NCG], r_cols)
        colload(p, w_dw, cols[:, 4 * NCG:7 * NCG], r_cols)
        p.step("vector", lambda e: e.tensor_scalar_add(
            cols[:, 2 * NCG:4 * NCG], cols[:, 2 * NCG:4 * NCG], 1.0),
            reads=(r_cols,), writes=(r_cols,))
        p.step("vector", lambda e: e.memset(kc[:], 1e-5), writes=(r_cols,))

        def ccol(j):
            return cols[:, j:j + 1]

        rows = [rowt[0:1, 0, :], rowt[0:1, 1, :]]

        def stats_mm(src, srcres, dst, dstres, extra=()):
            def f(e):
                for g in range(NCG):
                    ins = e.matmul(dst[:], ones128[:], src[:, g, :],
                                   start=(g == 0), stop=(g == NCG - 1))
                return ins
            return p.step("tensor", f, reads=tuple(srcres) + (r_ones,) + tuple(extra),
                          writes=(dstres,))

        def bcast(rowap, rres, dst, dres):
            return p.step("tensor", lambda e: e.matmul(dst[:], onesb[:], rowap,
                                                       start=True, stop=True),
                          reads=(rres, r_ones), writes=(dres,))

        for b in range(B):
            for ci in range(nch):
                it = b * nch + ci
                t0 = ci * TC
                sl = it % 2

                p.step("sync", lambda eng, sem, b=b, t0=t0: eng.dma_start(
                    out=xb[:], in_=x0[b, :, :, t0:t0 + TC].rearrange("g p t -> p g t")
                ).then_inc(sem, 16), writes=tuple(r_xb), reads=(r_wpw, r_wgru, r_cols), dma=1)

                for g in range(NCG):
                    p.step("gpsimd", lambda e, g=g: e.tensor_copy(xb16[:, g, :], xb[:, g, :]),
                           reads=(r_xb[g],), writes=(r_xb16[g],))
                stats_mm(xb16, r_xb16, ps_s1, r_s1)
                p.step("scalar", lambda e: e.activation(rows[0], ps_s1[:], AF.Copy, scale=1.0 / D),
                       reads=(r_s1,), writes=(r_row[0],))
                bcast(rows[0], r_row[0], ps_mu, r_mu)
                for g in range(NCG):
                    p.step("vector", lambda e, g=g: e.tensor_sub(dd[:, g, :], xb[:, g, :], ps_mu[:]),
                           reads=(r_xb[g], r_mu), writes=(r_dd[g],))
                    p.step("scalar", lambda e, g=g: e.activation(sq16[:, g, :], dd[:, g, :], AF.Square),
                           reads=(r_dd[g],), writes=(r_sq[g],))
                stats_mm(sq16, r_sq, ps_s2, r_s2)
                p.step("scalar", lambda e: e.activation(rows[1], ps_s2[:], AF.Sqrt,
                                                        scale=1.0 / D, bias=kc[0:1, 0:1]),
                       reads=(r_s2, r_cols), writes=(r_row[1],))
                p.step("vector", lambda e: e.reciprocal(rows[1], rows[1]),
                       reads=(r_row[1],), writes=(r_row[1],))
                bcast(rows[1], r_row[1], ps_r, r_r)
                for g in range(NCG):
                    p.step("vector", lambda e, g=g: e.tensor_mul(dd[:, g, :], dd[:, g, :], ps_r[:]),
                           reads=(r_dd[g], r_r), writes=(r_dd[g],))

                for g in range(NCG):
                    p.step("scalar", lambda e, g=g: e.activation(sq16[:, g, :], dd[:, g, :], AF.Square),
                           reads=(r_dd[g],), writes=(r_sq[g],))
                stats_mm(sq16, r_sq, ps_s1, r_s1)
                p.step("scalar", lambda e: e.activation(rows[0], ps_s1[:], AF.Sqrt),
                       reads=(r_s1,), writes=(r_row[0],))
                p.step("vector", lambda e: e.tensor_scalar(
                    rows[0], rows[0], 1.0 / math.sqrt(D), 1e-12 / math.sqrt(D), ALU.mult, ALU.max),
                    reads=(r_row[0],), writes=(r_row[0],))
                p.step("vector", lambda e: e.reciprocal(rows[0], rows[0]),
                       reads=(r_row[0],), writes=(r_row[0],))
                bcast(rows[0], r_row[0], ps_r, r_r)
                for g in range(NCG):
                    if ci == 0:
                        p.step("gpsimd", lambda e, g=g, sl=sl: e.memset(u1b[:, sl, g, 0:2], 0.0),
                               writes=(r_u1[sl][g],))
                    else:
                        p.step("gpsimd", lambda e, g=g, sl=sl: e.tensor_copy(
                            u1b[:, sl, g, 0:2], u1b[:, 1 - sl, g, TC:TC + 2]),
                            reads=(r_u1[1 - sl][g],), writes=(r_u1[sl][g],))
                    p.step("vector", lambda e, g=g, sl=sl: e.scalar_tensor_tensor(
                        u1b[:, sl, g, 2:2 + TC], dd[:, g, :], ccol(2 * NCG + g), ps_r[:],
                        ALU.mult, ALU.mult),
                        reads=(r_dd[g], r_r, r_cols), writes=(r_u1[sl][g],))

                for g in range(NCG):
                    cs = g % 2
                    p.step("vector", lambda e, g=g, sl=sl, cs=cs: e.tensor_scalar_mul(
                        cvt[:, cs, :], u1b[:, sl, g, 0:TC], ccol(4 * NCG + g)),
                        reads=(r_u1[sl][g], r_cols), writes=(r_cvt[cs],))
                    p.step("vector", lambda e, g=g, sl=sl, cs=cs: e.scalar_tensor_tensor(
                        cvt[:, cs, :], u1b[:, sl, g, 1:1 + TC], ccol(5 * NCG + g), cvt[:, cs, :],
                        ALU.mult, ALU.add),
                        reads=(r_u1[sl][g], r_cols, r_cvt[cs]), writes=(r_cvt[cs],))
                    p.step("vector", lambda e, g=g, sl=sl, cs=cs: e.scalar_tensor_tensor(
                        cvt[:, cs, :], u1b[:, sl, g, 2:2 + TC], ccol(6 * NCG + g), cvt[:, cs, :],
                        ALU.mult, ALU.add),
                        reads=(r_u1[sl][g], r_cols, r_cvt[cs]), writes=(r_cvt[cs],))
                    p.step("gpsimd", lambda e, g=g, cs=cs: e.tensor_scalar_add(
                        cv16[:, g, :], cvt[:, cs, :], ccol(NCG + g)),
                        reads=(r_cvt[cs], r_cols), writes=(r_cv16[g],))

                for og in range(NCG):
                    psb, psr = (ps_m0, r_m0) if og % 2 == 0 else (ps_m1, r_m1)
                    es = og % 2

                    def mmpw(e, og=og, psb=psb):
                        for g in range(NCG):
                            ins = e.matmul(psb[:], w_pw[:, g, og * 128:(og + 1) * 128],
                                           cv16[:, g, :], start=(g == 0), stop=(g == NCG - 1))
                        return ins
                    p.step("tensor", mmpw, reads=tuple(r_cv16) + (r_wpw,), writes=(psr,))
                    p.step("scalar", lambda e, og=og, psb=psb, es=es: e.activation(
                        ev[:, es, :], psb[:], AF.Identity, bias=ccol(og)),
                        reads=(psr, r_cols), writes=(r_ev[es],))
                    p.step("vector", lambda e, og=og, es=es: e.tensor_add(
                        dd[:, og, :], dd[:, og, :], ev[:, es, :]),
                        reads=(r_dd[og], r_ev[es]), writes=(r_dd[og],))

                for g in range(NCG):
                    p.step("scalar", lambda e, g=g: e.activation(sq16[:, g, :], dd[:, g, :], AF.Square),
                           reads=(r_dd[g],), writes=(r_sq[g],))
                stats_mm(sq16, r_sq, ps_s1, r_s1)
                p.step("scalar", lambda e: e.activation(rows[0], ps_s1[:], AF.Sqrt),
                       reads=(r_s1,), writes=(r_row[0],))
                p.step("vector", lambda e: e.tensor_scalar(
                    rows[0], rows[0], 1.0 / math.sqrt(D), 1e-12 / math.sqrt(D), ALU.mult, ALU.max),
                    reads=(r_row[0],), writes=(r_row[0],))
                p.step("vector", lambda e: e.reciprocal(rows[0], rows[0]),
                       reads=(r_row[0],), writes=(r_row[0],))
                bcast(rows[0], r_row[0], ps_r, r_r)
                for g in range(NCG):
                    p.step("vector", lambda e, g=g: e.scalar_tensor_tensor(
                        u2b[:, g, :], dd[:, g, :], ccol(3 * NCG + g), ps_r[:], ALU.mult, ALU.mult),
                        reads=(r_dd[g], r_r, r_cols), writes=(r_u2[g],))

                for g in range(NCG):
                    def mmh(e, g=g):
                        for kk in range(NCG):
                            ins = e.matmul(ps_m0[:], w_gru[:, kk, g * 128:(g + 1) * 128],
                                           u2b[:, kk, :], start=(kk == 0), stop=(kk == NCG - 1))
                        return ins

                    def mmg(e, g=g):
                        for kk in range(NCG):
                            ins = e.matmul(ps_m1[:], w_gru[:, kk, D + g * 128:D + (g + 1) * 128],
                                           u2b[:, kk, :], start=(kk == 0), stop=(kk == NCG - 1))
                        return ins
                    p.step("tensor", mmh, reads=tuple(r_u2) + (r_wgru,), writes=(r_m0,))
                    p.step("tensor", mmg, reads=tuple(r_u2) + (r_wgru,), writes=(r_m1,))
                    es = g % 2
                    p.step("scalar", lambda e, es=es: e.activation(
                        ga[:, es, :], ps_m1[:], AF.Sigmoid, scale=-1.0),
                        reads=(r_m1,), writes=(r_ga[es],))
                    p.step("scalar", lambda e, es=es: e.activation(
                        gs[:, es, :], ps_m1[:], AF.Sigmoid),
                        reads=(r_m1,), writes=(r_gs[es],))
                    p.step("scalar", lambda e, es=es: e.activation(
                        ggt[:, es, :], ps_m0[:], AF.Sigmoid),
                        reads=(r_m0,), writes=(r_gg[es],))
                    p.step("vector", lambda e, es=es: e.scalar_tensor_tensor(
                        ggt[:, es, :], ps_m0[:], 0.5, ggt[:, es, :], ALU.add, ALU.max),
                        reads=(r_m0, r_gg[es]), writes=(r_gg[es],))
                    p.step("gpsimd", lambda e, es=es: e.tensor_mul(
                        gb[:, es, :], gs[:, es, :], ggt[:, es, :]),
                        reads=(r_gs[es], r_gg[es]), writes=(r_gb[es],))
                    init = 0.0 if ci == 0 else carry[:, g:g + 1]
                    initdeps = () if ci == 0 else (r_carry[g],)
                    p.step("vector", lambda e, g=g, es=es, init=init: e.tensor_tensor_scan(
                        hsc[:, g, :], ga[:, es, :], gb[:, es, :], init, ALU.mult, ALU.add),
                        reads=(r_ga[es], r_gb[es]) + initdeps, writes=(r_hsc[g],))
                    p.step("gpsimd", lambda e, g=g: e.tensor_copy(
                        carry[:, g:g + 1], hsc[:, g, TC - 1:TC]),
                        reads=(r_hsc[g],), writes=(r_carry[g],))
                    p.step("vector", lambda e, g=g: e.tensor_add(
                        dd[:, g, :], dd[:, g, :], hsc[:, g, :]),
                        reads=(r_dd[g], r_hsc[g]), writes=(r_dd[g],))

                p.step("sync", lambda eng, sem, b=b, t0=t0: eng.dma_start(
                    out=y2o[b, :, :, t0:t0 + TC].rearrange("g p t -> p g t"),
                    in_=xb16.tensor.ap() if False else dd[:, :, :]).then_inc(sem, 16),
                    reads=tuple(r_dd), dma=1)

        p.step("gpsimd", None, incs=0,
               after=[(e, c) for e, c in p.counts.items()])
    return ["x0", "l0_pwT", "l0_gruT", "l0_b_pw", "l0_w_dw", "l0_b_dw",
            "l0_g_conv", "l0_g_gru"], ["y2s"]


def build_p0b(nc, p, kinds):
    TC = 256
    x0 = nc.dram_tensor("x0", [B, NCG, 128, T], F32, kind=kinds.get("x0", "ExternalInput")).ap()
    y2i = nc.dram_tensor("y2s", [B, NCG, 128, T], F32, kind=kinds.get("y2s", "ExternalInput")).ap()
    ff1T = nc.dram_tensor("l0_ff1T", [D, 4 * D], F32, kind="ExternalInput").ap()
    ff2T = nc.dram_tensor("l0_ff2T", [4 * D, D], F32, kind="ExternalInput").ap()
    b1 = nc.dram_tensor("l0_b_ff1", [128, 32], F32, kind="ExternalInput").ap()
    b2 = nc.dram_tensor("l0_b_ff2", [128, NCG], F32, kind="ExternalInput").ap()
    g_ff = nc.dram_tensor("l0_g_ff", [128, NCG], F32, kind="ExternalInput").ap()
    x1o = nc.dram_tensor("x1s", [B, NCG, 128, T], F32, kind=kinds.get("x1s", "ExternalOutput")).ap()

    nch = T // TC
    with ExitStack() as est:
        (stage_t, w_f1, w_f2, ones128, onesb, cols, yb, xr, sq16, u3b, hb, ev,
         rowt) = sball(nc, est,
            ("stage", [128, 1, 4096], F32), ("w_f1", [128, NCG, 4 * D], BF16),
            ("w_f2", [128, 32, D], BF16), ("ones128", [128, 1], BF16),
            ("onesb", [1, 128], F32), ("cols", [128, 2 * NCG + 32], F32),
            ("yb", [128, NCG, 256], F32), ("xr", [128, NCG, 256], F32),
            ("sq16", [128, NCG, 256], BF16), ("u3b", [128, NCG, 256], BF16),
            ("hb", [128, 32, 256], BF16), ("ev", [128, 2, 256], F32),
            ("rowt", [1, 2, 256], F32))
        (ps_s1, ps_r, ps_m0, ps_m1) = psall(nc, est,
            ("ps_s1", [1, 256], F32), ("ps_r", [128, 256], F32),
            ("ps_m0", [128, 256], F32), ("ps_m1", [128, 256], F32))
        stage = [stage_t[:, 0, :]]
        r_stage = resv(1, "stage")
        r_w1, r_w2, r_cols, r_ones = Res(), Res(), Res(), Res()
        r_yb = resv(NCG, "yb")
        r_xr = resv(NCG, "xr")
        r_sq = resv(NCG, "sq")
        r_u3 = resv(NCG, "u3")
        r_hb = resv(32, "hb")
        r_ev = resv(2, "ev")
        r_row = resv(2, "row")
        r_s1, r_r, r_m0, r_m1 = Res(), Res(), Res(), Res()

        load_wb16(p, ff1T, w_f1, r_w1, stage, r_stage, D, 4 * D)
        load_wb16(p, ff2T, w_f2, r_w2, stage, r_stage, 4 * D, D)
        p.step("vector", lambda e: e.memset(ones128[:], 1.0), writes=(r_ones,))
        p.step("vector", lambda e: e.memset(onesb[:], 1.0), writes=(r_ones,))
        colload(p, g_ff, cols[:, 0:NCG], r_cols)
        colload(p, b2, cols[:, NCG:2 * NCG], r_cols)
        colload(p, b1, cols[:, 2 * NCG:2 * NCG + 32], r_cols)
        p.step("vector", lambda e: e.tensor_scalar_add(
            cols[:, 0:NCG], cols[:, 0:NCG], 1.0), reads=(r_cols,), writes=(r_cols,))

        def ccol(j):
            return cols[:, j:j + 1]

        rows = [rowt[0:1, 0, :], rowt[0:1, 1, :]]

        for b in range(B):
            for ci in range(nch):
                t0 = ci * TC
                p.step("sync", lambda eng, sem, b=b, t0=t0: eng.dma_start(
                    out=yb[:], in_=y2i[b, :, :, t0:t0 + TC].rearrange("g p t -> p g t")
                ).then_inc(sem, 16), writes=tuple(r_yb), reads=(r_w1, r_w2, r_cols), dma=1)
                p.step("sync", lambda eng, sem, b=b, t0=t0: eng.dma_start(
                    out=xr[:], in_=x0[b, :, :, t0:t0 + TC].rearrange("g p t -> p g t")
                ).then_inc(sem, 16), writes=tuple(r_xr), dma=1)

                for g in range(NCG):
                    p.step("scalar", lambda e, g=g: e.activation(sq16[:, g, :], yb[:, g, :], AF.Square),
                           reads=(r_yb[g],), writes=(r_sq[g],))

                def f_stats(e):
                    for g in range(NCG):
                        ins = e.matmul(ps_s1[:], ones128[:], sq16[:, g, :],
                                       start=(g == 0), stop=(g == NCG - 1))
                    return ins
                p.step("tensor", f_stats, reads=tuple(r_sq) + (r_ones,), writes=(r_s1,))
                p.step("scalar", lambda e: e.activation(rows[0], ps_s1[:], AF.Sqrt),
                       reads=(r_s1,), writes=(r_row[0],))
                p.step("vector", lambda e: e.tensor_scalar(
                    rows[0], rows[0], 1.0 / math.sqrt(D), 1e-12 / math.sqrt(D), ALU.mult, ALU.max),
                    reads=(r_row[0],), writes=(r_row[0],))
                p.step("vector", lambda e: e.reciprocal(rows[0], rows[0]),
                       reads=(r_row[0],), writes=(r_row[0],))
                p.step("tensor", lambda e: e.matmul(ps_r[:], onesb[:], rows[0],
                                                    start=True, stop=True),
                       reads=(r_row[0], r_ones), writes=(r_r,))
                for g in range(NCG):
                    p.step("vector", lambda e, g=g: e.scalar_tensor_tensor(
                        u3b[:, g, :], yb[:, g, :], ccol(g), ps_r[:], ALU.mult, ALU.mult),
                        reads=(r_yb[g], r_r, r_cols), writes=(r_u3[g],))

                for og in range(32):
                    psb, psr = (ps_m0, r_m0) if og % 2 == 0 else (ps_m1, r_m1)

                    def mm1(e, og=og, psb=psb):
                        for g in range(NCG):
                            ins = e.matmul(psb[:], w_f1[:, g, og * 128:(og + 1) * 128],
                                           u3b[:, g, :], start=(g == 0), stop=(g == NCG - 1))
                        return ins
                    p.step("tensor", mm1, reads=tuple(r_u3) + (r_w1,), writes=(psr,))
                    p.step("scalar", lambda e, og=og, psb=psb: e.activation(
                        hb[:, og, :], psb[:], AF.Gelu, bias=ccol(2 * NCG + og)),
                        reads=(psr, r_cols), writes=(r_hb[og],))

                for og in range(NCG):
                    psb, psr = (ps_m0, r_m0) if og % 2 == 0 else (ps_m1, r_m1)
                    es = og % 2

                    def mm2(e, og=og, psb=psb):
                        for kk in range(32):
                            ins = e.matmul(psb[:], w_f2[:, kk, og * 128:(og + 1) * 128],
                                           hb[:, kk, :], start=(kk == 0), stop=(kk == 31))
                        return ins
                    p.step("tensor", mm2, reads=tuple(r_hb) + (r_w2,), writes=(psr,))
                    p.step("scalar", lambda e, og=og, psb=psb, es=es: e.activation(
                        ev[:, es, :], psb[:], AF.Identity, bias=ccol(NCG + og)),
                        reads=(psr, r_cols), writes=(r_ev[es],))
                    p.step("vector", lambda e, og=og, es=es: e.tensor_add(
                        ev[:, es, :], ev[:, es, :], yb[:, og, :]),
                        reads=(r_ev[es], r_yb[og]), writes=(r_ev[es],))
                    p.step("gpsimd", lambda e, og=og, es=es: e.tensor_add(
                        xr[:, og, :], xr[:, og, :], ev[:, es, :]),
                        reads=(r_xr[og], r_ev[es]), writes=(r_xr[og],))

                p.step("sync", lambda eng, sem, b=b, t0=t0: eng.dma_start(
                    out=x1o[b, :, :, t0:t0 + TC].rearrange("g p t -> p g t"),
                    in_=xr[:, :, :]).then_inc(sem, 16),
                    reads=tuple(r_xr), dma=1)

        p.step("gpsimd", None, incs=0, after=[(e, c) for e, c in p.counts.items()])
    return ["x0", "y2s", "l0_ff1T", "l0_ff2T", "l0_b_ff1", "l0_b_ff2", "l0_g_ff"], ["x1s"]


def build_p1a(nc, p, kinds):
    TC = 512
    NH = 24
    x1 = nc.dram_tensor("x1s", [B, NCG, 128, T], F32, kind=kinds.get("x1s", "ExternalInput")).ap()
    wgT = nc.dram_tensor("l0f_wgT", [D, 3072], I32, kind="ExternalInput").ap()
    wuT = nc.dram_tensor("l0f_wuT", [D, 3072], I32, kind="ExternalInput").ap()
    ho = nc.dram_tensor("hs", [B, NH, 128, T], BF16, kind=kinds.get("hs", "ExternalOutput")).ap()

    nch = T // TC
    with ExitStack() as est:
        (stage_t, w_g, w_u, ones128, onesb, xb, xb16, dd, sq16, xnb, hb, ev,
         rowt, kc) = sball(nc, est,
            ("stage", [128, 3072], I32), ("w_g", [128, NCG, 3072], BF16),
            ("w_u", [128, NCG, 3072], BF16), ("ones128", [128, 1], BF16),
            ("onesb", [1, 128], F32), ("xb", [128, NCG, TC], F32),
            ("xb16", [128, NCG, TC], BF16), ("dd", [128, NCG, TC], F32),
            ("sq16", [128, NCG, TC], BF16), ("xnb", [128, NCG, TC], BF16),
            ("hb", [128, NH, TC], BF16), ("ev", [128, 2, TC], F32),
            ("rowt", [1, 2, TC], F32), ("kc", [128, 1], F32))
        (ps_s1, ps_s2, ps_mu, ps_r, ps_m0, ps_m1) = psall(nc, est,
            ("ps_s1", [1, TC], F32), ("ps_s2", [1, TC], F32),
            ("ps_mu", [128, TC], F32), ("ps_r", [128, TC], F32),
            ("ps_m0", [128, TC], F32), ("ps_m1", [128, TC], F32))

        r_stage = resv(1, "stage")
        r_wg, r_wu, r_ones = Res(), Res(), Res()
        r_xb = resv(NCG, "xb")
        r_xb16 = resv(NCG, "xb16")
        r_dd = resv(NCG, "dd")
        r_sq = resv(NCG, "sq")
        r_xn = resv(NCG, "xn")
        r_hb = resv(NH, "hb")
        r_ev = resv(2, "ev")
        r_row = resv(2, "row")
        r_s1, r_s2, r_mu, r_r, r_m0, r_m1 = Res(), Res(), Res(), Res(), Res(), Res()

        load_wb16(p, wgT, w_g, r_wg, [stage_t], r_stage, D, 3072, scale=1.0 / 127.0)
        load_wb16(p, wuT, w_u, r_wu, [stage_t], r_stage, D, 3072, scale=1.0 / 127.0)
        p.step("vector", lambda e: e.memset(ones128[:], 1.0), writes=(r_ones,))
        p.step("vector", lambda e: e.memset(onesb[:], 1.0), writes=(r_ones,))
        p.step("vector", lambda e: e.memset(kc[:], 1e-5), writes=(r_ones,))
        rows = [rowt[0:1, 0, :], rowt[0:1, 1, :]]

        for b in range(B):
            for ci in range(nch):
                t0 = ci * TC
                p.step("sync", lambda eng, sem, b=b, t0=t0: eng.dma_start(
                    out=xb[:], in_=x1[b, :, :, t0:t0 + TC].rearrange("g p t -> p g t")
                ).then_inc(sem, 16), writes=tuple(r_xb), reads=(r_wg, r_wu), dma=1)
                for g in range(NCG):
                    p.step("gpsimd", lambda e, g=g: e.tensor_copy(xb16[:, g, :], xb[:, g, :]),
                           reads=(r_xb[g],), writes=(r_xb16[g],))

                def f_s1(e):
                    for g in range(NCG):
                        ins = e.matmul(ps_s1[:], ones128[:], xb16[:, g, :],
                                       start=(g == 0), stop=(g == NCG - 1))
                    return ins
                p.step("tensor", f_s1, reads=tuple(r_xb16) + (r_ones,), writes=(r_s1,))
                p.step("scalar", lambda e: e.activation(rows[0], ps_s1[:], AF.Copy, scale=1.0 / D),
                       reads=(r_s1,), writes=(r_row[0],))
                p.step("tensor", lambda e: e.matmul(ps_mu[:], onesb[:], rows[0],
                                                    start=True, stop=True),
                       reads=(r_row[0], r_ones), writes=(r_mu,))
                for g in range(NCG):
                    p.step("vector", lambda e, g=g: e.tensor_sub(dd[:, g, :], xb[:, g, :], ps_mu[:]),
                           reads=(r_xb[g], r_mu), writes=(r_dd[g],))
                    p.step("scalar", lambda e, g=g: e.activation(sq16[:, g, :], dd[:, g, :], AF.Square),
                           reads=(r_dd[g],), writes=(r_sq[g],))

                def f_s2(e):
                    for g in range(NCG):
                        ins = e.matmul(ps_s2[:], ones128[:], sq16[:, g, :],
                                       start=(g == 0), stop=(g == NCG - 1))
                    return ins
                p.step("tensor", f_s2, reads=tuple(r_sq) + (r_ones,), writes=(r_s2,))
                p.step("scalar", lambda e: e.activation(rows[1], ps_s2[:], AF.Sqrt,
                                                        scale=1.0 / D, bias=kc[0:1, 0:1]),
                       reads=(r_s2, r_ones), writes=(r_row[1],))
                p.step("vector", lambda e: e.reciprocal(rows[1], rows[1]),
                       reads=(r_row[1],), writes=(r_row[1],))
                p.step("tensor", lambda e: e.matmul(ps_r[:], onesb[:], rows[1],
                                                    start=True, stop=True),
                       reads=(r_row[1], r_ones), writes=(r_r,))
                for g in range(NCG):
                    p.step("vector", lambda e, g=g: e.tensor_mul(xnb[:, g, :], dd[:, g, :], ps_r[:]),
                           reads=(r_dd[g], r_r), writes=(r_xn[g],))

                for og in range(NH):
                    def mmg_(e, og=og):
                        for g in range(NCG):
                            ins = e.matmul(ps_m0[:], w_g[:, g, og * 128:(og + 1) * 128],
                                           xnb[:, g, :], start=(g == 0), stop=(g == NCG - 1))
                        return ins

                    def mmu_(e, og=og):
                        for g in range(NCG):
                            ins = e.matmul(ps_m1[:], w_u[:, g, og * 128:(og + 1) * 128],
                                           xnb[:, g, :], start=(g == 0), stop=(g == NCG - 1))
                        return ins
                    es = og % 2
                    p.step("tensor", mmg_, reads=tuple(r_xn) + (r_wg,), writes=(r_m0,))
                    p.step("tensor", mmu_, reads=tuple(r_xn) + (r_wu,), writes=(r_m1,))
                    p.step("scalar", lambda e, es=es: e.activation(ev[:, es, :], ps_m0[:], AF.Silu),
                           reads=(r_m0,), writes=(r_ev[es],))
                    p.step("vector", lambda e, og=og, es=es: e.tensor_mul(
                        hb[:, og, :], ev[:, es, :], ps_m1[:]),
                        reads=(r_ev[es], r_m1), writes=(r_hb[og],))
                p.step("sync", lambda eng, sem, b=b, t0=t0: eng.dma_start(
                    out=ho[b, :, :, t0:t0 + TC].rearrange("g p t -> p g t"),
                    in_=hb[:, :, :]).then_inc(sem, 16),
                    reads=tuple(r_hb), dma=1)

        p.step("gpsimd", None, incs=0, after=[(e, c) for e, c in p.counts.items()])
    return ["x1s", "l0f_wgT", "l0f_wuT"], ["hs"]


def build_p1b(nc, p, kinds):
    TC = 512
    NH = 24
    x1 = nc.dram_tensor("x1s", [B, NCG, 128, T], F32, kind=kinds.get("x1s", "ExternalInput")).ap()
    hi = nc.dram_tensor("hs", [B, NH, 128, T], BF16, kind=kinds.get("hs", "ExternalInput")).ap()
    wdT = nc.dram_tensor("l0f_wdT", [3072, D], I32, kind="ExternalInput").ap()
    x2o = nc.dram_tensor("x2s", [B, NCG, 128, T], F32, kind=kinds.get("x2s", "ExternalOutput")).ap()

    nch = T // TC
    with ExitStack() as est:
        (stage_t, w_d, xb, hb, ev) = sball(nc, est,
            ("stage", [128, 1024], I32), ("w_d", [128, NH, D], BF16),
            ("xb", [128, NCG, TC], F32), ("hb", [128, NH, TC], BF16),
            ("ev", [128, 2, TC], F32))
        (ps_m0, ps_m1) = psall(nc, est,
            ("ps_m0", [128, TC], F32), ("ps_m1", [128, TC], F32))
        r_stage = resv(1, "stage")
        r_wd = Res()
        r_xb = resv(NCG, "xb")
        r_hb = resv(NH, "hb")
        r_ev = resv(2, "ev")
        r_m0, r_m1 = Res(), Res()

        load_wb16(p, wdT, w_d, r_wd, [stage_t], r_stage, 3072, D, scale=1.0 / 127.0)

        for b in range(B):
            for ci in range(nch):
                t0 = ci * TC
                p.step("sync", lambda eng, sem, b=b, t0=t0: eng.dma_start(
                    out=hb[:], in_=hi[b, :, :, t0:t0 + TC].rearrange("g p t -> p g t")
                ).then_inc(sem, 16), writes=tuple(r_hb), reads=(r_wd,), dma=1)
                p.step("sync", lambda eng, sem, b=b, t0=t0: eng.dma_start(
                    out=xb[:], in_=x1[b, :, :, t0:t0 + TC].rearrange("g p t -> p g t")
                ).then_inc(sem, 16), writes=tuple(r_xb), dma=1)
                for og in range(NCG):
                    psb, psr = (ps_m0, r_m0) if og % 2 == 0 else (ps_m1, r_m1)
                    es = og % 2

                    def mmd(e, og=og, psb=psb):
                        for kk in range(NH):
                            ins = e.matmul(psb[:], w_d[:, kk, og * 128:(og + 1) * 128],
                                           hb[:, kk, :], start=(kk == 0), stop=(kk == NH - 1))
                        return ins
                    p.step("tensor", mmd, reads=tuple(r_hb) + (r_wd,), writes=(psr,))
                    p.step("vector", lambda e, og=og, psb=psb: e.tensor_add(
                        xb[:, og, :], xb[:, og, :], psb[:]),
                        reads=(r_xb[og], psr), writes=(r_xb[og],))
                p.step("sync", lambda eng, sem, b=b, t0=t0: eng.dma_start(
                    out=x2o[b, :, :, t0:t0 + TC].rearrange("g p t -> p g t"),
                    in_=xb[:, :, :]).then_inc(sem, 16),
                    reads=tuple(r_xb), dma=1)

        p.step("gpsimd", None, incs=0, after=[(e, c) for e, c in p.counts.items()])
    return ["x1s", "hs", "l0f_wdT"], ["x2s"]


def build_p2a(nc, p, kinds):
    TC = 256
    ND = 16
    x2 = nc.dram_tensor("x2s", [B, NCG, 128, T], F32, kind=kinds.get("x2s", "ExternalInput")).ap()
    w_in = nc.dram_tensor("m_w_in", [D, 4 * D], F32, kind="ExternalInput").ap()
    conv_w = nc.dram_tensor("m_conv_w", [128, 4 * ND], F32, kind="ExternalInput").ap()
    conv_b = nc.dram_tensor("m_conv_b", [128, ND], F32, kind="ExternalInput").ap()
    uo = nc.dram_tensor("us", [B, ND, 128, T], BF16, kind=kinds.get("us", "ExternalOutput")).ap()
    zo = nc.dram_tensor("zs", [B, ND, 128, T], BF16, kind=kinds.get("zs", "ExternalOutput")).ap()

    nch = T // TC
    with ExitStack() as est:
        (stage_t, w_i, ones128, onesb, cols, xb, c16, dd, sq16, xnb, xinb, zb,
         ub, cvt, rowt, kc) = sball(nc, est,
            ("stage", [128, 4096], F32), ("w_i", [128, NCG, 4 * D], BF16),
            ("ones128", [128, 1], BF16), ("onesb", [1, 128], F32),
            ("cols", [128, 5 * ND], F32), ("xb", [128, NCG, TC], F32),
            ("c16", [128, NCG, TC], BF16), ("dd", [128, NCG, TC], F32),
            ("sq16", [128, NCG, TC], BF16), ("xnb", [128, NCG, TC], BF16),
            ("xinb", [128, ND, TC + 3], F32), ("zb", [128, ND, TC], BF16),
            ("ub", [128, ND, TC], BF16), ("cvt", [128, 2, TC], F32),
            ("rowt", [1, 2, TC], F32), ("kc", [128, 1], F32))
        (ps_s1, ps_s2, ps_mu, ps_r, ps_m0, ps_m1) = psall(nc, est,
            ("ps_s1", [1, TC], F32), ("ps_s2", [1, TC], F32),
            ("ps_mu", [128, TC], F32), ("ps_r", [128, TC], F32),
            ("ps_m0", [128, TC], F32), ("ps_m1", [128, TC], F32))

        r_stage = resv(1, "stage")
        r_wi, r_cols, r_ones, r_kc = Res(), Res(), Res(), Res()
        r_xb = resv(NCG, "xb")
        r_c16 = resv(NCG, "c16")
        r_dd = resv(NCG, "dd")
        r_sq = resv(NCG, "sq")
        r_xn = resv(NCG, "xn")
        r_xin = resv(ND, "xin")
        r_zb = resv(ND, "zb")
        r_ub = resv(ND, "ub")
        r_cvt = resv(2, "cvt")
        r_row = resv(2, "row")
        r_s1, r_s2, r_mu, r_r, r_m0, r_m1 = Res(), Res(), Res(), Res(), Res(), Res()

        load_wb16(p, w_in, w_i, r_wi, [stage_t], r_stage, D, 4 * D)
        p.step("vector", lambda e: e.memset(ones128[:], 1.0), writes=(r_ones,))
        p.step("vector", lambda e: e.memset(onesb[:], 1.0), writes=(r_ones,))
        p.step("vector", lambda e: e.memset(kc[:], 1e-5), writes=(r_kc,))
        colload(p, conv_w, cols[:, 0:4 * ND], r_cols)
        colload(p, conv_b, cols[:, 4 * ND:5 * ND], r_cols)

        def ccw(k, dg):
            return cols[:, k * ND + dg:k * ND + dg + 1]

        E = {"c16": c16, "r_c16": r_c16, "sq16": sq16, "r_sq": r_sq, "dd": dd,
             "r_dd": r_dd, "ps_s1": ps_s1, "r_s1": r_s1, "ps_s2": ps_s2, "r_s2": r_s2,
             "ps_mu": ps_mu, "r_mu": r_mu, "ps_r": ps_r, "r_r": r_r,
             "rows": [rowt[0:1, 0, :], rowt[0:1, 1, :]], "r_row": r_row,
             "ones128": ones128, "onesb": onesb, "r_ones": r_ones,
             "kc": kc[0:1, 0:1], "r_kc": r_kc, "r_cols": r_cols}

        for b in range(B):
            for ci in range(nch):
                t0 = ci * TC
                p.step("sync", lambda eng, sem, b=b, t0=t0: eng.dma_start(
                    out=xb[:], in_=x2[b, :, :, t0:t0 + TC].rearrange("g p t -> p g t")
                ).then_inc(sem, 16), writes=tuple(r_xb), reads=(r_wi, r_cols), dma=1)
                emit_ln(p, E, xb, r_xb, dd, r_dd, NCG)
                for g in range(NCG):
                    p.step("gpsimd", lambda e, g=g: e.tensor_copy(xnb[:, g, :], dd[:, g, :]),
                           reads=(r_dd[g],), writes=(r_xn[g],))

                for og in range(32):
                    psb, psr = (ps_m0, r_m0) if og % 2 == 0 else (ps_m1, r_m1)

                    def mmw(e, og=og, psb=psb):
                        for g in range(NCG):
                            ins = e.matmul(psb[:], w_i[:, g, og * 128:(og + 1) * 128],
                                           xnb[:, g, :], start=(g == 0), stop=(g == NCG - 1))
                        return ins
                    p.step("tensor", mmw, reads=tuple(r_xn) + (r_wi,), writes=(psr,))
                    if og < ND:
                        if ci == 0:
                            p.step("gpsimd", lambda e, og=og: e.memset(xinb[:, og, 0:3], 0.0),
                                   writes=(r_xin[og],))
                        else:
                            p.step("gpsimd", lambda e, og=og: e.tensor_copy(
                                xinb[:, og, 0:3], xinb[:, og, TC:TC + 3]),
                                reads=(r_xin[og],), writes=(r_xin[og],))
                        p.step("scalar", lambda e, og=og, psb=psb: e.copy(
                            xinb[:, og, 3:3 + TC], psb[:]),
                            reads=(psr,), writes=(r_xin[og],))
                    else:
                        p.step("scalar", lambda e, og=og, psb=psb: e.copy(
                            zb[:, og - ND, :], psb[:]),
                            reads=(psr,), writes=(r_zb[og - ND],))

                for dg in range(ND):
                    cs = dg % 2
                    p.step("vector", lambda e, dg=dg, cs=cs: e.tensor_scalar_mul(
                        cvt[:, cs, :], xinb[:, dg, 0:TC], ccw(0, dg)),
                        reads=(r_xin[dg], r_cols), writes=(r_cvt[cs],))
                    for k in range(1, 4):
                        p.step("vector", lambda e, dg=dg, cs=cs, k=k: e.scalar_tensor_tensor(
                            cvt[:, cs, :], xinb[:, dg, k:k + TC], ccw(k, dg), cvt[:, cs, :],
                            ALU.mult, ALU.add),
                            reads=(r_xin[dg], r_cols, r_cvt[cs]), writes=(r_cvt[cs],))
                    p.step("scalar", lambda e, dg=dg, cs=cs: e.activation(
                        ub[:, dg, :], cvt[:, cs, :], AF.Silu, bias=ccw(4, dg)),
                        reads=(r_cvt[cs], r_cols), writes=(r_ub[dg],))

                p.step("sync", lambda eng, sem, b=b, t0=t0: eng.dma_start(
                    out=uo[b, :, :, t0:t0 + TC].rearrange("g p t -> p g t"),
                    in_=ub[:, :, :]).then_inc(sem, 16), reads=tuple(r_ub), dma=1)
                p.step("sync", lambda eng, sem, b=b, t0=t0: eng.dma_start(
                    out=zo[b, :, :, t0:t0 + TC].rearrange("g p t -> p g t"),
                    in_=zb[:, :, :]).then_inc(sem, 16), reads=tuple(r_zb), dma=1)

        p.step("gpsimd", None, incs=0, after=[(e, c) for e, c in p.counts.items()])
    return ["x2s", "m_w_in", "m_conv_w", "m_conv_b"], ["us", "zs"]


def build_p2b(nc, p, kinds):
    TC = 512
    ND = 16
    NS = 16
    ui = nc.dram_tensor("us", [B, ND, 128, T], BF16, kind=kinds.get("us", "ExternalInput")).ap()
    zi = nc.dram_tensor("zs", [B, ND, 128, T], BF16, kind=kinds.get("zs", "ExternalInput")).ap()
    wxp1 = nc.dram_tensor("m_wxp1", [2 * D, 64], F32, kind="ExternalInput").ap()
    wxp2 = nc.dram_tensor("m_wxp2", [2 * D, 64], F32, kind="ExternalInput").ap()
    wdt = nc.dram_tensor("m_wdt", [64, 2 * D], F32, kind="ExternalInput").ap()
    selb_d = nc.dram_tensor("m_selb", [64, 2 * NS * 128], F32, kind="ExternalInput").ap()
    alog_d = nc.dram_tensor("m_alog", [128, ND * NS], F32, kind="ExternalInput").ap()
    bdt_d = nc.dram_tensor("m_bdt", [128, ND], F32, kind="ExternalInput").ap()
    dvec_d = nc.dram_tensor("m_dvec", [128, ND], F32, kind="ExternalInput").ap()
    ybo = nc.dram_tensor("ybs", [B, ND, 128, T], BF16, kind=kinds.get("ybs", "ExternalOutput")).ap()

    nch = T // TC
    with ExitStack() as est:
        (stage_t, w_x1, w_x2, w_dt, selb, a_sb, ones128, onesb, cols, kc,
         u_s, z_s, dtb, dub, dAs, bsc, hbuf, yacc, bbs, dl16, bc16, cbs,
         sp, state) = sball(nc, est,
            ("stage", [128, 2048], F32), ("w_x1", [128, ND, 64], BF16),
            ("w_x2", [128, ND, 64], BF16), ("w_dt", [64, 1, 2 * D], BF16),
            ("selb", [64, 2 * NS, 128], BF16),
            ("a_sb", [128, ND * NS], F32), ("ones128", [128, 1], BF16),
            ("onesb", [1, 128], F32), ("cols", [128, 2 * ND], F32),
            ("kc", [128, 2], F32),
            ("u_s", [128, ND, TC], BF16),
            ("z_s", [128, ND, TC], BF16), ("dtb", [128, ND, TC], BF16),
            ("dub", [128, ND, TC], BF16), ("dAs", [128, ND, TC], BF16),
            ("bsc", [128, 2, ND, TC], BF16), ("hbuf", [128, ND, TC], BF16),
            ("yacc", [128, ND, TC], F32), ("bbs", [128, 2, TC], BF16),
            ("dl16", [64, TC], BF16), ("bc16", [64, TC], BF16),
            ("cbs", [128, 2, TC], BF16),
            ("sp", [128, 4, TC], F32),
            ("state", [128, ND * NS], F32))
        (ps_m0, ps_m1, ps_dt, ps_bb, ps_cb0, ps_cb1) = psall(nc, est,
            ("ps_m0", [128, TC], F32), ("ps_m1", [128, TC], F32),
            ("ps_dt", [64, TC], F32), ("ps_bb", [128, TC], F32),
            ("ps_cb0", [128, TC], F32), ("ps_cb1", [128, TC], F32))

        r_stage = resv(1, "stage")
        r_wx1, r_wx2, r_wdt, r_selb, r_asb = (Res(), Res(), Res(), Res(), Res())
        r_cols, r_ones, r_kc = Res(), Res(), Res()
        r_u = resv(ND, "u")
        r_z = resv(ND, "z")
        r_dt = resv(ND, "dt")
        r_du = resv(ND, "du")
        r_dA = resv(ND, "dA")
        r_bsc = [resv(ND, "bscA"), resv(ND, "bscB")]
        r_h = resv(ND, "h")
        r_y = resv(ND, "y")
        r_bbs = resv(2, "bbs")
        r_cbs = resv(2, "cbs")
        r_dl, r_bc = Res(), Res()
        r_sp = resv(4, "sp")
        r_state = [resv(ND, f"st{n}") for n in range(NS)]
        (r_m0, r_m1, r_psdt, r_psbb, r_pscb0, r_pscb1) = (Res(), Res(), Res(), Res(), Res(), Res())

        load_wb16(p, wxp1, w_x1, r_wx1, [stage_t], r_stage, 2 * D, 64)
        load_wb16(p, wxp2, w_x2, r_wx2, [stage_t], r_stage, 2 * D, 64)
        load_wb16(p, wdt, w_dt, r_wdt, [stage_t], r_stage, 64, 2 * D)
        for sh_ in range(2):
            p.step("sync", lambda eng, sem, sh_=sh_: eng.dma_start(
                out=stage_t[0:64, 0:NS * 128],
                in_=selb_d[:, sh_ * NS * 128:(sh_ + 1) * NS * 128]).then_inc(sem, 16),
                writes=(r_stage[0],), dma=1)
            p.step("vector", lambda e, sh_=sh_: e.tensor_copy(
                selb[:, sh_ * NS:(sh_ + 1) * NS, :].rearrange("p a b -> p (a b)"),
                stage_t[0:64, 0:NS * 128]),
                reads=(r_stage[0],), writes=(r_selb,))
        p.step("sync", lambda eng, sem: eng.dma_start(
            out=stage_t[:, 0:ND * NS], in_=alog_d).then_inc(sem, 16),
            writes=(r_stage[0],), dma=1)
        p.step("scalar", lambda e: e.activation(a_sb[:], stage_t[:, 0:ND * NS], AF.Exp),
               reads=(r_stage[0],), writes=(r_asb,))
        p.step("vector", lambda e: e.tensor_scalar_mul(a_sb[:], a_sb[:], -1.0),
               reads=(r_asb,), writes=(r_asb,))
        colload(p, bdt_d, cols[:, 0:ND], r_cols)
        colload(p, dvec_d, cols[:, ND:2 * ND], r_cols)
        p.step("vector", lambda e: e.memset(ones128[:], 1.0), writes=(r_ones,))
        p.step("vector", lambda e: e.memset(onesb[:], 1.0), writes=(r_ones,))
        p.step("vector", lambda e: e.memset(kc[:, 0:1], 1e-5), writes=(r_kc,))
        p.step("vector", lambda e: e.memset(kc[:, 1:2], 1.0), writes=(r_kc,))

        def bdtc(dg):
            return cols[:, dg:dg + 1]

        def dvc(dg):
            return cols[:, ND + dg:ND + dg + 1]

        for b in range(B):
            for ci in range(nch):
                t0 = ci * TC
                p.step("sync", lambda eng, sem, b=b, t0=t0: eng.dma_start(
                    out=u_s[:], in_=ui[b, :, :, t0:t0 + TC].rearrange("g p t -> p g t")
                ).then_inc(sem, 16), writes=tuple(r_u),
                    reads=(r_wx1, r_wx2, r_wdt, r_selb, r_asb, r_cols), dma=1)
                p.step("sync", lambda eng, sem, b=b, t0=t0: eng.dma_start(
                    out=z_s[:], in_=zi[b, :, :, t0:t0 + TC].rearrange("g p t -> p g t")
                ).then_inc(sem, 16), writes=tuple(r_z), dma=1)

                def mmx1(e):
                    for dg in range(ND):
                        ins = e.matmul(ps_dt[:], w_x1[:, dg, :], u_s[:, dg, :],
                                       start=(dg == 0), stop=(dg == ND - 1))
                    return ins
                p.step("tensor", mmx1, reads=tuple(r_u) + (r_wx1,), writes=(r_psdt,))
                p.step("scalar", lambda e: e.copy(dl16[:], ps_dt[:]),
                       reads=(r_psdt,), writes=(r_dl,))

                def mmx2(e):
                    for dg in range(ND):
                        ins = e.matmul(ps_dt[:], w_x2[:, dg, :], u_s[:, dg, :],
                                       start=(dg == 0), stop=(dg == ND - 1))
                    return ins
                p.step("tensor", mmx2, reads=tuple(r_u) + (r_wx2, r_dl), writes=(r_psdt,))
                p.step("scalar", lambda e: e.copy(bc16[:], ps_dt[:]),
                       reads=(r_psdt,), writes=(r_bc,))

                for dg in range(ND):
                    psb, psr = (ps_m0, r_m0) if dg % 2 == 0 else (ps_m1, r_m1)
                    p.step("tensor", lambda e, dg=dg, psb=psb: e.matmul(
                        psb[:], w_dt[0:64, 0, dg * 128:(dg + 1) * 128], dl16[:],
                        start=True, stop=True),
                        reads=(r_dl, r_wdt), writes=(psr,))
                    s0, s1 = dg % 2, 2 + dg % 2
                    p.step("scalar", lambda e, dg=dg, psb=psb, s0=s0: e.activation(
                        sp[:, s0, :], psb[:], AF.Abs, bias=bdtc(dg)),
                        reads=(psr, r_cols), writes=(r_sp[s0],))
                    p.step("scalar", lambda e, s0=s0: e.activation(
                        sp[:, s0, :], sp[:, s0, :], AF.Exp, scale=-1.0),
                        reads=(r_sp[s0],), writes=(r_sp[s0],))
                    p.step("scalar", lambda e, s0=s0: e.activation(
                        sp[:, s0, :], sp[:, s0, :], AF.Ln, bias=kc[:, 1:2]),
                        reads=(r_sp[s0], r_kc), writes=(r_sp[s0],))
                    p.step("scalar", lambda e, dg=dg, psb=psb, s1=s1: e.activation(
                        sp[:, s1, :], psb[:], AF.Relu, bias=bdtc(dg)),
                        reads=(psr, r_cols), writes=(r_sp[s1],))
                    p.step("vector", lambda e, dg=dg, s0=s0, s1=s1: e.tensor_add(
                        dtb[:, dg, :], sp[:, s0, :], sp[:, s1, :]),
                        reads=(r_sp[s0], r_sp[s1]), writes=(r_dt[dg],))

                p.step("vector", lambda e: e.tensor_mul(
                    dub[:].rearrange("p a t -> p (a t)"), dtb[:].rearrange("p a t -> p (a t)"),
                    u_s[:].rearrange("p a t -> p (a t)")),
                    reads=tuple(r_dt) + tuple(r_u), writes=tuple(r_du))

                for n in range(NS):
                    bs = n % 2
                    ps_cb, r_pscb = (ps_cb0, r_pscb0) if bs == 0 else (ps_cb1, r_pscb1)
                    ps_bbx, r_psbbx = (ps_bb, r_psbb) if bs == 0 else (ps_m0, r_m0)
                    p.step("tensor", lambda e, n=n, ps_bbx=ps_bbx: e.matmul(
                        ps_bbx[:], selb[:, n, :], bc16[:], start=True, stop=True),
                        reads=(r_selb, r_bc), writes=(r_psbbx,))
                    p.step("tensor", lambda e, n=n, ps_cb=ps_cb: e.matmul(
                        ps_cb[:], selb[:, NS + n, :], bc16[:], start=True, stop=True),
                        reads=(r_selb, r_bc), writes=(r_pscb,))
                    p.step("scalar", lambda e, bs=bs, ps_bbx=ps_bbx: e.copy(
                        bbs[:, bs, :], ps_bbx[:]),
                        reads=(r_psbbx,), writes=(r_bbs[bs],))
                    p.step("vector", lambda e, bs=bs: e.tensor_mul(
                        bsc[:, bs], dub[:],
                        bbs[:, bs, :].rearrange("p (a t) -> p a t", a=1)
                        .broadcast_to([128, ND, TC])),
                        reads=tuple(r_du) + (r_bbs[bs],), writes=tuple(r_bsc[bs]))
                    for dg in range(ND):
                        p.step("scalar", lambda e, n=n, dg=dg: e.activation(
                            dAs[:, dg, :], dtb[:, dg, :], AF.Exp,
                            scale=a_sb[:, dg * NS + n:dg * NS + n + 1]),
                            reads=(r_dt[dg], r_asb), writes=(r_dA[dg],))
                    for dg in range(ND):
                        if ci == 0:
                            init = 0.0
                            ideps = ()
                        else:
                            init = state[:, n * ND + dg:n * ND + dg + 1]
                            ideps = (r_state[n][dg],)
                        p.step("vector", lambda e, n=n, dg=dg, init=init, bs=bs: e.tensor_tensor_scan(
                            hbuf[:, dg, :], dAs[:, dg, :], bsc[:, bs, dg, :], init,
                            ALU.mult, ALU.add),
                            reads=(r_dA[dg], r_bsc[bs][dg]) + ideps, writes=(r_h[dg],))
                    p.step("vector", lambda e, n=n: e.tensor_copy(
                        state[:, n * ND:(n + 1) * ND],
                        hbuf[:, 0:ND, TC - 1:TC].rearrange("p a b -> p (a b)")),
                        reads=tuple(r_h), writes=tuple(r_state[n]))
                    p.step("scalar", lambda e, bs=bs, ps_cb=ps_cb: e.copy(
                        cbs[:, bs, :], ps_cb[:]),
                        reads=(r_pscb,), writes=(r_cbs[bs],))
                    p.step("vector", lambda e, bs=bs: e.tensor_mul(
                        bsc[:, bs], hbuf[:],
                        cbs[:, bs, :].rearrange("p (a t) -> p a t", a=1)
                        .broadcast_to([128, ND, TC])),
                        reads=tuple(r_h) + (r_cbs[bs],), writes=tuple(r_bsc[bs]))
                    if n == 0:
                        p.step("gpsimd", lambda e, bs=bs: e.tensor_copy(
                            yacc[:].rearrange("p a t -> p (a t)"),
                            bsc[:, bs].rearrange("p a t -> p (a t)")),
                            reads=tuple(r_bsc[bs]), writes=tuple(r_y))
                    else:
                        p.step("gpsimd", lambda e, bs=bs: e.tensor_add(
                            yacc[:].rearrange("p a t -> p (a t)"),
                            yacc[:].rearrange("p a t -> p (a t)"),
                            bsc[:, bs].rearrange("p a t -> p (a t)")),
                            reads=tuple(r_bsc[bs]) + tuple(r_y), writes=tuple(r_y))

                for dg in range(ND):
                    p.step("vector", lambda e, dg=dg: e.scalar_tensor_tensor(
                        yacc[:, dg, :], u_s[:, dg, :], dvc(dg), yacc[:, dg, :],
                        ALU.mult, ALU.add),
                        reads=(r_u[dg], r_cols, r_y[dg]), writes=(r_y[dg],))
                p.step("scalar", lambda e: e.activation(
                    dub[:].rearrange("p a t -> p (a t)"),
                    z_s[:].rearrange("p a t -> p (a t)"), AF.Silu),
                    reads=tuple(r_z), writes=tuple(r_du))
                p.step("gpsimd", lambda e: e.tensor_mul(
                    u_s[:].rearrange("p a t -> p (a t)"),
                    yacc[:].rearrange("p a t -> p (a t)"),
                    dub[:].rearrange("p a t -> p (a t)")),
                    reads=tuple(r_y) + tuple(r_du), writes=tuple(r_u))

                p.step("sync", lambda eng, sem, b=b, t0=t0: eng.dma_start(
                    out=ybo[b, :, :, t0:t0 + TC].rearrange("g p t -> p g t"),
                    in_=u_s[:, :, :]).then_inc(sem, 16), reads=tuple(r_u), dma=1)

        p.step("gpsimd", None, incs=0, after=[(e, c) for e, c in p.counts.items()])
    return ["us", "zs", "m_wxp1", "m_wxp2", "m_wdt", "m_selb",
            "m_alog", "m_bdt", "m_dvec"], ["ybs"]




def build_p2c(nc, p, kinds):
    TC = 512
    ND = 16
    x2 = nc.dram_tensor("x2s", [B, NCG, 128, T], F32, kind=kinds.get("x2s", "ExternalInput")).ap()
    ybi = nc.dram_tensor("ybs", [B, ND, 128, T], BF16, kind=kinds.get("ybs", "ExternalInput")).ap()
    wout = nc.dram_tensor("m_wout", [2 * D, D], F32, kind="ExternalInput").ap()
    gam_d = nc.dram_tensor("m_gamma", [128, NCG], F32, kind="ExternalInput").ap()
    bet_d = nc.dram_tensor("m_beta", [128, NCG], F32, kind="ExternalInput").ap()
    x3o = nc.dram_tensor("x3s", [B, NCG, 128, T], F32, kind=kinds.get("x3s", "ExternalOutput")).ap()
    y0o = nc.dram_tensor("y0s", [B, NCG, 128, T], F32, kind=kinds.get("y0s", "ExternalOutput")).ap()

    nch = T // TC
    with ExitStack() as est:
        (stage_t, w_o, ones128, onesb, cols, kc, ybt, xb, c16, dd, sq16, resb,
         rowt) = sball(nc, est,
            ("stage", [128, 1024], F32), ("w_o", [128, ND, D], BF16),
            ("ones128", [128, 1], BF16), ("onesb", [1, 128], F32),
            ("cols", [128, 2 * NCG], F32), ("kc", [128, 1], F32),
            ("ybt", [128, ND, TC], BF16), ("xb", [128, NCG, TC], F32),
            ("c16", [128, NCG, TC], BF16), ("dd", [128, NCG, TC], F32),
            ("sq16", [128, NCG, TC], BF16), ("resb", [128, NCG, TC], F32),
            ("rowt", [1, 2, TC], F32))
        (ps_s1, ps_s2, ps_mu, ps_r, ps_m0, ps_m1) = psall(nc, est,
            ("ps_s1", [1, TC], F32), ("ps_s2", [1, TC], F32),
            ("ps_mu", [128, TC], F32), ("ps_r", [128, TC], F32),
            ("ps_m0", [128, TC], F32), ("ps_m1", [128, TC], F32))
        r_stage = resv(1, "stage")
        r_wo, r_cols, r_ones, r_kc = Res(), Res(), Res(), Res()
        r_yb = resv(ND, "yb")
        r_xb = resv(NCG, "xb")
        r_c16 = resv(NCG, "c16")
        r_dd = resv(NCG, "dd")
        r_sq = resv(NCG, "sq")
        r_res = resv(NCG, "res")
        r_row = resv(2, "row")
        r_s1, r_s2, r_mu, r_r, r_m0, r_m1 = Res(), Res(), Res(), Res(), Res(), Res()

        load_wb16(p, wout, w_o, r_wo, [stage_t], r_stage, 2 * D, D)
        colload(p, gam_d, cols[:, 0:NCG], r_cols)
        colload(p, bet_d, cols[:, NCG:2 * NCG], r_cols)
        p.step("vector", lambda e: e.memset(ones128[:], 1.0), writes=(r_ones,))
        p.step("vector", lambda e: e.memset(onesb[:], 1.0), writes=(r_ones,))
        p.step("vector", lambda e: e.memset(kc[:], 1e-5), writes=(r_kc,))

        def gamc(g):
            return cols[:, g:g + 1]

        def betc(g):
            return cols[:, NCG + g:NCG + g + 1]

        E = {"c16": c16, "r_c16": r_c16, "sq16": sq16, "r_sq": r_sq, "dd": dd,
             "r_dd": r_dd, "ps_s1": ps_s1, "r_s1": r_s1, "ps_s2": ps_s2, "r_s2": r_s2,
             "ps_mu": ps_mu, "r_mu": r_mu, "ps_r": ps_r, "r_r": r_r,
             "rows": [rowt[0:1, 0, :], rowt[0:1, 1, :]], "r_row": r_row,
             "ones128": ones128, "onesb": onesb, "r_ones": r_ones,
             "kc": kc[0:1, 0:1], "r_kc": r_kc, "r_cols": r_cols}

        for b in range(B):
            for ci in range(nch):
                t0 = ci * TC
                p.step("sync", lambda eng, sem, b=b, t0=t0: eng.dma_start(
                    out=ybt[:], in_=ybi[b, :, :, t0:t0 + TC].rearrange("g p t -> p g t")
                ).then_inc(sem, 16), writes=tuple(r_yb), reads=(r_wo, r_cols), dma=1)
                p.step("sync", lambda eng, sem, b=b, t0=t0: eng.dma_start(
                    out=xb[:], in_=x2[b, :, :, t0:t0 + TC].rearrange("g p t -> p g t")
                ).then_inc(sem, 16), writes=tuple(r_xb), dma=1)

                emit_ln(p, E, xb, r_xb, dd, r_dd, NCG)
                for og in range(NCG):
                    psb, psr = (ps_m0, r_m0) if og % 2 == 0 else (ps_m1, r_m1)

                    def mmo(e, og=og, psb=psb):
                        for kk in range(ND):
                            ins = e.matmul(psb[:], w_o[:, kk, og * 128:(og + 1) * 128],
                                           ybt[:, kk, :], start=(kk == 0), stop=(kk == ND - 1))
                        return ins
                    p.step("tensor", mmo, reads=tuple(r_yb) + (r_wo,), writes=(psr,))
                    p.step("vector", lambda e, og=og, psb=psb: e.tensor_add(
                        resb[:, og, :], dd[:, og, :], psb[:]),
                        reads=(r_dd[og], psr), writes=(r_res[og],))

                emit_ln(p, E, resb, r_res, resb, r_res, NCG, gcol=gamc, bcol=betc)
                p.step("gpsimd", lambda e: e.tensor_add(
                    xb[:].rearrange("p a t -> p (a t)"), xb[:].rearrange("p a t -> p (a t)"),
                    resb[:].rearrange("p a t -> p (a t)")),
                    reads=tuple(r_res) + tuple(r_xb), writes=tuple(r_xb))
                p.step("sync", lambda eng, sem, b=b, t0=t0: eng.dma_start(
                    out=x3o[b, :, :, t0:t0 + TC].rearrange("g p t -> p g t"),
                    in_=xb[:, :, :]).then_inc(sem, 16), reads=tuple(r_xb), dma=1)
                emit_ln(p, E, xb, r_xb, dd, r_dd, NCG)
                p.step("sync", lambda eng, sem, b=b, t0=t0: eng.dma_start(
                    out=y0o[b, :, :, t0:t0 + TC].rearrange("g p t -> p g t"),
                    in_=dd[:, :, :]).then_inc(sem, 16), reads=tuple(r_dd), dma=1)

        p.step("gpsimd", None, incs=0, after=[(e, c) for e, c in p.counts.items()])
    return ["x2s", "ybs", "m_wout", "m_gamma", "m_beta"], ["x3s", "y0s"]

def build_l1f_attn(nc, p, kinds, layer, yin, yout):
    TC = 512
    x_in = nc.dram_tensor(yin, [B, NCG, 128, T], F32, kind=kinds.get(yin, "ExternalInput")).ap()
    wqkv_d = nc.dram_tensor(f"f_wqkv{layer}", [D, 3 * D], F32, kind="ExternalInput").ap()
    wo_d = nc.dram_tensor(f"f_wo{layer}", [64, 16 * D], F32, kind="ExternalInput").ap()
    qs_d = nc.dram_tensor(f"f_qs{layer}", [128, 2], F32, kind="ExternalInput").ap()
    rot_d = nc.dram_tensor("f_rot", [128, 4, T], F32, kind="ExternalInput").ap()
    onesh_d = nc.dram_tensor("f_onesh", [128, NCG * 16], F32, kind="ExternalInput").ap()
    exph_d = nc.dram_tensor("f_exph", [16, NCG * 128], F32, kind="ExternalInput").ap()
    id_d = nc.dram_tensor("f_ident", [128, 128], F32, kind="ExternalInput").ap()
    msk_d = nc.dram_tensor("f_masks", [128, 3, 384], F32, kind="ExternalInput").ap()
    x_out = nc.dram_tensor(yout, [B, NCG, 128, T], F32, kind=kinds.get(yout, "ExternalOutput")).ap()
    nch = T // TC
    with ExitStack() as est:
        (wqkv, wo64, onesh, exph, identb, qscol, xr, xsb, q_s,
         k_s, qf, sq16, kbuf, vbuf, rot, sbs, pfs, pTs, maskb, av, zsl,
         rn, mcol) = sball(nc, est,
            ("wqkv", [128, NCG, 3 * D], BF16),
            ("wo64", [64, 16, D], BF16), ("onesh", [128, NCG, 16], BF16),
            ("exph", [16, NCG, 128], BF16),
            ("identb", [128, 128], BF16), ("qscol", [128, 2], F32),
            ("xr", [128, NCG, 256], F32), ("xsb", [128, NCG, TC + 1], BF16),
            ("q_s", [128, NCG, TC], BF16), ("k_s", [128, NCG, TC], BF16),
            ("qf", [128, NCG, TC], BF16),
            ("sq16", [128, NCG, TC], BF16), ("kbuf", [128, NCG, 256 + TC], BF16),
            ("vbuf", [128, 6, 1024], BF16), ("rot", [128, 2, TC], BF16),
            ("sbs", [128, 16, 384], BF16), ("pfs", [128, 16, 384], BF16),
            ("pTs", [128, 16, 384], BF16), ("maskb", [128, 3, 384], BF16),
            ("av", [64, 16, 256], BF16), ("zsl", [128, 2, TC], BF16),
            ("rn", [16, TC], BF16),
            ("mcol", [128, 48], F32))
        t1 = sq16
        (ps_a, ps_b, ps_pt, ps_pt2, ps_rb0, ps_hsd, ps_av, ps_av2) = psall(nc, est,
            ("ps_a", [128, 512], F32), ("ps_b", [128, 512], F32),
            ("ps_pt", [128, 384], BF16), ("ps_pt2", [128, 384], BF16),
            ("ps_rb0", [128, TC], F32), ("ps_hsd", [16, TC], F32),
            ("ps_av", [64, 128], F32), ("ps_av2", [64, 128], F32))

        r_wqkv, r_wo, r_consts, r_qsc = Res(), Res(), Res(), Res()
        r_xr = resv(NCG, "xr")
        r_xsb = resv(NCG, "xsb")
        r_qs = resv(NCG, "qs")
        r_ks = resv(NCG, "ks")
        r_qf = resv(NCG, "qf")
        r_sq = resv(NCG, "sq")
        r_t1 = r_sq
        r_kb = resv(NCG, "kb")
        r_vb = resv(6, "vb")
        r_rot = Res("rot")
        r_sb = resv(16, "sb")
        r_pf = resv(16, "pf")
        r_pT = resv(16, "pT")
        r_msk = Res("msk")
        r_av = resv(16, "av")
        r_zs = resv(2, "zs")
        r_rn = Res()
        r_mcol = resv(48, "mcol")
        (rp_m0, rp_m1, rp_m2, rp_m3, rp_rb0, rp_hs, rp_av, rp_av2) = (
            Res(), Res(), Res(), Res(), Res(), Res(), Res(), Res())
        ps_m0, ps_m1 = ps_a, ps_b
        ps_hs = ps_hsd
        ps_rb1, rp_rb1 = ps_rb0, rp_rb0

        load_wb16(p, wqkv_d, wqkv, r_wqkv, [], [], D, 3 * D)
        for j in range(16):
            p.step("gpsimd", lambda eng, sem, j=j: eng.dma_start(
                out=wo64[0:64, j, :], in_=wo_d[:, j * D:(j + 1) * D]).then_inc(sem, 16),
                writes=(r_wo,), dma=1)
        p.step("gpsimd", lambda eng, sem: eng.dma_start(
            out=onesh[:].rearrange("p a b -> p (a b)"), in_=onesh_d).then_inc(sem, 16),
            writes=(r_consts,), dma=1)
        p.step("gpsimd", lambda eng, sem: eng.dma_start(
            out=exph[0:16, :, :].rearrange("p a b -> p (a b)"), in_=exph_d).then_inc(sem, 16),
            writes=(r_consts,), dma=1)
        p.step("gpsimd", lambda eng, sem: eng.dma_start(
            out=identb[:], in_=id_d).then_inc(sem, 16),
            writes=(r_consts,), dma=1)
        colload(p, qs_d, qscol[:], r_qsc)
        p.step("gpsimd", lambda eng, sem: eng.dma_start(
            out=maskb[:].rearrange("p a b -> p (a b)"),
            in_=msk_d.rearrange("p a b -> p (a b)")).then_inc(sem, 16),
            writes=(r_msk,), dma=1)

        for b in range(B):
            for ci in range(nch):
                t0 = ci * TC
                for g in range(NCG):
                    if t0 == 0:
                        p.step("gpsimd", lambda e, g=g: e.memset(xsb[:, g, 0:1], 0.0),
                               writes=(r_xsb[g],))
                        p.step("gpsimd", lambda eng, sem, b=b, g=g: eng.dma_start(
                            out=xsb[:, g, 1:TC + 1], in_=x_in[b, g, :, 0:TC]).then_inc(sem, 16),
                            writes=(r_xsb[g],), reads=(r_wqkv, r_wo, r_consts, r_qsc), dma=1)
                    else:
                        p.step("gpsimd", lambda eng, sem, b=b, g=g, t0=t0: eng.dma_start(
                            out=xsb[:, g, :], in_=x_in[b, g, :, t0 - 1:t0 + TC]).then_inc(sem, 16),
                            writes=(r_xsb[g],), dma=1)
                    p.step("sync", lambda eng, sem, b=b, g=g, t0=t0: eng.dma_start(
                        out=xr[:, g, :], in_=x_in[b, g, :, t0:t0 + 256]).then_inc(sem, 16),
                        writes=(r_xr[g],), dma=1)

                def xv(g, cols=TC):
                    off = 1 if g < 4 else 0
                    return xsb[:, g, off:off + cols]

                if ci == 0:
                    p.step("vector", lambda e: e.memset(
                        kbuf[:].rearrange("p a t -> p (a t)"), 0.0),
                        writes=tuple(r_kb))
                    p.step("vector", lambda e: e.memset(
                        vbuf[:].rearrange("p a t -> p (a t)"), 0.0),
                        writes=tuple(r_vb))
                else:
                    def fsh(e):
                        for g in range(NCG):
                            ins = e.tensor_copy(kbuf[:, g, 0:256], kbuf[:, g, TC:TC + 256])
                        return ins
                    p.step("vector", fsh, reads=tuple(r_kb), writes=tuple(r_kb))
                    p.step("vector", lambda e: e.tensor_copy(
                        vbuf[:, 0:2, :].rearrange("p a t -> p (a t)"),
                        vbuf[:, 4:6, :].rearrange("p a t -> p (a t)")),
                        reads=(r_vb[4], r_vb[5]), writes=(r_vb[0], r_vb[1]))

                for og in range(16):
                    psb, psr = (ps_m0, rp_m0) if og % 2 == 0 else (ps_m1, rp_m1)

                    def mmq(e, og=og, psb=psb):
                        for g in range(NCG):
                            ins = e.matmul(psb[:, 0:TC], wqkv[:, g, og * 128:(og + 1) * 128],
                                           xv(g), start=(g == 0), stop=(g == NCG - 1))
                        return ins
                    p.step("tensor", mmq, reads=tuple(r_xsb) + (r_wqkv,), writes=(psr,))
                    if og < 8:
                        p.step("scalar", lambda e, og=og, psb=psb: e.copy(q_s[:, og, :], psb[:, 0:TC]),
                               reads=(psr,), writes=(r_qs[og],))
                    else:
                        p.step("scalar", lambda e, og=og, psb=psb: e.copy(k_s[:, og - 8, :], psb[:, 0:TC]),
                               reads=(psr,), writes=(r_ks[og - 8],))

                for tt in range(4):
                    slot = 2 + tt
                    for j in range(4):
                        psb, psr = (ps_m0, rp_m0) if j % 2 == 0 else (ps_m1, rp_m1)

                        def mmv(e, tt=tt, j=j, psb=psb):
                            for g in range(NCG):
                                ins = e.matmul(psb[:, 0:256], xv(g)[:, tt * 128:(tt + 1) * 128],
                                               wqkv[:, g, 2 * D + j * 256:2 * D + (j + 1) * 256],
                                               start=(g == 0), stop=(g == NCG - 1))
                            return ins
                        p.step("tensor", mmv, reads=tuple(r_xsb) + (r_wqkv,), writes=(psr,))
                        p.step("scalar", lambda e, slot=slot, j=j, psb=psb: e.copy(
                            vbuf[:, slot, j * 256:(j + 1) * 256], psb[:, 0:256]),
                            reads=(psr,), writes=(r_vb[slot],))


                for which in range(2):
                    srcb = q_s if which == 0 else k_s
                    r_src = r_qs if which == 0 else r_ks
                    p.step("gpsimd", lambda eng, sem, t0=t0, which=which: eng.dma_start(
                        out=rot[:], in_=rot_d[:, 2 * which:2 * which + 2, t0:t0 + TC]
                    ).then_inc(sem, 16), writes=(r_rot,), dma=1)
                    csq = rot[:, 0, :]
                    ssbq = rot[:, 1, :]
                    p.step("gpsimd", lambda e, srcb=srcb: e.tensor_mul(
                        sq16[:].rearrange("p a t -> p (a t)"),
                        srcb[:].rearrange("p a t -> p (a t)"),
                        srcb[:].rearrange("p a t -> p (a t)")),
                        reads=tuple(r_src), writes=tuple(r_sq))

                    def fhs(e):
                        for g in range(NCG):
                            ins = e.matmul(ps_hs[:], onesh[:, g, :], sq16[:, g, :],
                                           start=(g == 0), stop=(g == NCG - 1))
                        return ins
                    p.step("tensor", fhs, reads=tuple(r_sq) + (r_consts,), writes=(rp_hs,))
                    p.step("scalar", lambda e: e.activation(rn[:], ps_hs[:], AF.Sqrt),
                           reads=(rp_hs,), writes=(r_rn,))
                    p.step("vector", lambda e: e.tensor_scalar_max(rn[:], rn[:], 1e-12),
                           reads=(r_rn,), writes=(r_rn,))
                    def frecip(e):
                        with nc.allow_low_precision(reason="bf16 recip of l2 norm"):
                            return e.reciprocal(rn[:], rn[:])
                    p.step("vector", frecip, reads=(r_rn,), writes=(r_rn,))
                    SWMASK = [i ^ 8 for i in range(32)]
                    for g in range(NCG):
                        ps_rb, rp_rb = (ps_rb0, rp_rb0) if g % 2 == 0 else (ps_rb1, rp_rb1)
                        p.step("tensor", lambda e, g=g, ps_rb=ps_rb: e.matmul(
                            ps_rb[:], exph[:, g, :], rn[0:16, :], start=True, stop=True),
                            reads=(r_rn, r_consts), writes=(rp_rb,))
                        p.step("vector", lambda e, g=g, srcb=srcb, which=which, ps_rb=ps_rb: e.scalar_tensor_tensor(
                            t1[:, g, :], srcb[:, g, :], qscol[:, which:which + 1], ps_rb[:],
                            ALU.mult, ALU.mult),
                            reads=(r_src[g], rp_rb, r_qsc), writes=(r_t1[g],))
                        zslot = g % 2
                        p.step("vector", lambda e, g=g, zslot=zslot: e.stream_shuffle(
                            zsl[:, zslot, :], t1[:, g, :], SWMASK),
                            reads=(r_t1[g],), writes=(r_zs[zslot],))
                        p.step("gpsimd", lambda e, g=g, zslot=zslot, ssbq=ssbq: e.tensor_mul(
                            zsl[:, zslot, :], zsl[:, zslot, :], ssbq),
                            reads=(r_zs[zslot], r_rot), writes=(r_zs[zslot],))
                        dst = (qf[:, g, :] if which == 0 else kbuf[:, g, 256:256 + TC])
                        rdst = (r_qf[g] if which == 0 else r_kb[g])
                        p.step("gpsimd", lambda e, g=g, csq=csq, dst=dst: e.tensor_mul(
                            dst, t1[:, g, :], csq),
                            reads=(r_t1[g], r_rot), writes=(rdst,))
                        p.step("vector", lambda e, g=g, zslot=zslot, dst=dst: e.tensor_add(
                            dst, dst, zsl[:, zslot, :]),
                            reads=(rdst, r_zs[zslot]), writes=(rdst,))

                for wi in range(2):
                  for qt in range(2):
                    off = wi * 256 + qt * 128
                    mi = (0 if (ci > 0 or wi > 0) else (1 if qt == 0 else 2))
                    for h in range(16):
                        g = h // 2
                        hb = (h % 2) * 64
                        ps_s, rp_s = (ps_a, rp_m0) if h % 2 == 0 else (ps_b, rp_m1)
                        p.step("tensor", lambda e, g=g, hb=hb, off=off, ps_s=ps_s: e.matmul(
                            ps_s[:, 0:384], qf[hb:hb + 64, g, off:off + 128],
                            kbuf[hb:hb + 64, g, off:off + 384], start=True, stop=True),
                            reads=(r_qf[g], r_kb[g]), writes=(rp_s,))
                        p.step("vector", lambda e, h=h, mi=mi, ps_s=ps_s: e.scalar_tensor_tensor(
                            sbs[:, h, :], ps_s[:, 0:384], 1.0, maskb[:, mi, :], ALU.mult, ALU.add),
                            reads=(rp_s, r_msk), writes=(r_sb[h],))
                        p.step("vector", lambda e, h=h: e.reduce_max(
                            mcol[:, h:h + 1], sbs[:, h, :], axis=AX.X, negate=True),
                            reads=(r_sb[h],), writes=(r_mcol[h],))
                    for h in range(16):
                        p.step("scalar", lambda e, h=h: e.activation(
                            pfs[:, h, :], sbs[:, h, :], AF.Exp, bias=mcol[:, h:h + 1],
                            accum_out=mcol[:, 16 + h:17 + h]),
                            reads=(r_sb[h], r_mcol[h]), writes=(r_pf[h], r_mcol[16 + h]))
                    p.step("scalar", lambda e: e.activation(
                        mcol[:, 32:48], mcol[:, 16:32], AF.Ln),
                        reads=tuple(r_mcol[16:32]), writes=tuple(r_mcol[32:48]))
                    p.step("scalar", lambda e: e.activation(
                        mcol[:, 32:48], mcol[:, 32:48], AF.Exp, scale=-1.0),
                        reads=tuple(r_mcol[32:48]), writes=tuple(r_mcol[32:48]))
                    for h in range(16):
                        p.step("vector", lambda e, h=h: e.tensor_scalar_mul(
                            pfs[:, h, :], pfs[:, h, :], mcol[:, 32 + h:33 + h]),
                            reads=(r_pf[h], r_mcol[32 + h]), writes=(r_pf[h],))
                    for h in range(16):
                        pspt, rppt = (ps_pt, rp_m2) if h % 2 == 0 else (ps_pt2, rp_m3)

                        def ftr(e, h=h, pspt=pspt):
                            for j in range(3):
                                ins = e.transpose(pspt[:, j * 128:(j + 1) * 128],
                                                  pfs[:, h, j * 128:(j + 1) * 128], identb[:])
                            return ins
                        p.step("tensor", ftr, reads=(r_pf[h], r_consts), writes=(rppt,))
                        p.step("vector", lambda e, h=h, pspt=pspt: e.tensor_copy(
                            pTs[:, h, :], pspt[:]),
                            reads=(rppt,), writes=(r_pT[h],))
                    for h in range(16):
                        psv, rpv = (ps_av, rp_av) if h % 2 == 0 else (ps_av2, rp_av2)

                        def fav(e, h=h, qt=qt, wi=wi, psv=psv):
                            for j in range(3):
                                sv = 2 * wi + qt + j
                                ins = e.matmul(psv[:], vbuf[:, sv, h * 64:(h + 1) * 64],
                                               pTs[:, h, j * 128:(j + 1) * 128],
                                               start=(j == 0), stop=(j == 2))
                            return ins
                        p.step("tensor", fav, reads=(r_pT[h], r_vb[2 * wi + qt],
                                                     r_vb[2 * wi + qt + 1],
                                                     r_vb[2 * wi + qt + 2]), writes=(rpv,))
                        p.step("scalar", lambda e, h=h, qt=qt, psv=psv: e.copy(
                            av[:, h, qt * 128:(qt + 1) * 128], psv[:]),
                            reads=(rpv,), writes=(r_av[h],))

                    if qt == 1:
                        for og in range(NCG):
                            psb, psr = (ps_m0, rp_m0) if og % 2 == 0 else (ps_m1, rp_m1)

                            def mmo(e, og=og, psb=psb):
                                for h in range(16):
                                    ins = e.matmul(psb[:, 0:256],
                                                   wo64[:, h, og * 128:(og + 1) * 128],
                                                   av[:, h, :], start=(h == 0), stop=(h == 15))
                                return ins
                            p.step("tensor", mmo, reads=tuple(r_av) + (r_wo,), writes=(psr,))
                            p.step("vector", lambda e, og=og, psb=psb: e.tensor_add(
                                xr[:, og, :], xr[:, og, :], psb[:, 0:256]),
                                reads=(r_xr[og], psr), writes=(r_xr[og],))
                            p.step("sync", lambda eng, sem, b=b, og=og, t0=t0, wi=wi: eng.dma_start(
                                out=x_out[b, og, :, t0 + wi * 256:t0 + (wi + 1) * 256],
                                in_=xr[:, og, :]).then_inc(sem, 16),
                                reads=(r_xr[og],), dma=1)
                            if wi == 0:
                                p.step("sync", lambda eng, sem, b=b, og=og, t0=t0: eng.dma_start(
                                    out=xr[:, og, :],
                                    in_=x_in[b, og, :, t0 + 256:t0 + 512]).then_inc(sem, 16),
                                    writes=(r_xr[og],), dma=1)

        p.step("gpsimd", None, incs=0, after=[(e, c) for e, c in p.counts.items()])
    return [yin, f"f_wqkv{layer}", f"f_wo{layer}", f"f_qs{layer}", "f_rot",
            "f_onesh", "f_exph", "f_ident", "f_masks"], [yout]


def build_l1f_ffn(nc, p, kinds, layer, yin, yout, last=False):
    TC = 256
    NH1 = 44
    NH2 = 22
    x_in = nc.dram_tensor(yin, [B, NCG, 128, T], F32, kind=kinds.get(yin, "ExternalInput")).ap()
    w1_d = nc.dram_tensor(f"f_w1{layer}", [D, 5632], F32, kind="ExternalInput").ap()
    w2_d = nc.dram_tensor(f"f_w2{layer}", [2816, D], F32, kind="ExternalInput").ap()
    b1_d = nc.dram_tensor(f"f_b1{layer}", [128, NH1], F32, kind="ExternalInput").ap()
    b2_d = nc.dram_tensor(f"f_b2{layer}", [128, NCG], F32, kind="ExternalInput").ap()
    g_d = nc.dram_tensor(f"f_lng{layer}", [128, NCG], F32, kind="ExternalInput").ap()
    bl_d = nc.dram_tensor(f"f_lnb{layer}", [128, NCG], F32, kind="ExternalInput").ap()
    if last:
        x3i = nc.dram_tensor("x3s", [B, NCG, 128, T], F32, kind=kinds.get("x3s", "ExternalInput")).ap()
    x_out = nc.dram_tensor(yout, [B, NCG, 128, T], F32, kind=kinds.get(yout, "ExternalOutput")).ap()

    nch = T // TC
    with ExitStack() as est:
        (stage_t, w_1, w_2, ones128, onesb, cols, kc, xb, xn, c16, dd, hb, ev,
         xr, rowt) = sball(nc, est,
            ("stage", [128, 2816], F32), ("w_1", [128, NCG, 5632], BF16),
            ("w_2", [128, NH2, D], BF16), ("ones128", [128, 1], BF16),
            ("onesb", [1, 128], F32), ("cols", [128, NH1 + 4 * NCG], F32),
            ("kc", [128, 1], F32), ("xb", [128, NCG, 257], F32),
            ("xn", [128, NCG, TC], BF16), ("c16", [128, NCG, TC], BF16),
            ("dd", [128, NCG, TC], F32), ("hb", [128, NH2, TC], BF16),
            ("ev", [128, 2, TC], F32), ("xr", [128, 2, TC], F32),
            ("rowt", [1, 2, TC], F32))
        (ps_s1, ps_s2, ps_mu, ps_r, ps_m0, ps_m1, ps_m2, ps_m3) = psall(nc, est,
            ("ps_s1", [1, TC], F32), ("ps_s2", [1, TC], F32),
            ("ps_mu", [128, TC], F32), ("ps_r", [128, TC], F32),
            ("ps_m0", [128, TC], F32), ("ps_m1", [128, TC], F32),
            ("ps_m2", [128, TC], F32), ("ps_m3", [128, TC], F32))

        r_stage = resv(1, "stage")
        r_w1, r_w2, r_cols, r_ones, r_kc = Res(), Res(), Res(), Res(), Res()
        r_xb = resv(NCG, "xb")
        r_xn = resv(NCG, "xn")
        r_c16 = resv(NCG, "c16")
        r_dd = resv(NCG, "dd")
        r_sq = r_c16
        r_hb = resv(NH2, "hb")
        r_ev = resv(2, "ev")
        r_xr = resv(2, "xr")
        r_row = resv(2, "row")
        r_s1, r_s2, r_mu, r_r, r_m0, r_m1 = Res(), Res(), Res(), Res(), Res(), Res()
        r_m2, r_m3 = Res(), Res()

        load_wb16(p, w1_d[:, 0:2816], w_1, r_w1, [stage_t], r_stage, D, 2816)
        load_wb16(p, w1_d[:, 2816:5632], w_1, r_w1, [stage_t], r_stage, D, 2816, dcol=2816)
        load_wb16(p, w2_d, w_2, r_w2, [stage_t], r_stage, 2816, D)
        colload(p, b1_d, cols[:, 0:NH1], r_cols)
        colload(p, b2_d, cols[:, NH1:NH1 + NCG], r_cols)
        colload(p, g_d, cols[:, NH1 + NCG:NH1 + 2 * NCG], r_cols)
        colload(p, bl_d, cols[:, NH1 + 2 * NCG:NH1 + 3 * NCG], r_cols)
        p.step("vector", lambda e: e.memset(ones128[:], 1.0), writes=(r_ones,))
        p.step("vector", lambda e: e.memset(onesb[:], 1.0), writes=(r_ones,))
        p.step("vector", lambda e: e.memset(kc[:], 1e-5), writes=(r_kc,))

        def b1c(j):
            return cols[:, j:j + 1]

        def b2c(g):
            return cols[:, NH1 + g:NH1 + g + 1]

        def gamc(g):
            return cols[:, NH1 + NCG + g:NH1 + NCG + g + 1]

        def betc(g):
            return cols[:, NH1 + 2 * NCG + g:NH1 + 2 * NCG + g + 1]

        E = {"c16": c16, "r_c16": r_c16, "sq16": c16, "r_sq": r_c16, "dd": dd,
             "r_dd": r_dd, "ps_s1": ps_s1, "r_s1": r_s1, "ps_s2": ps_s2, "r_s2": r_s2,
             "ps_mu": ps_mu, "r_mu": r_mu, "ps_r": ps_r, "r_r": r_r,
             "rows": [rowt[0:1, 0, :], rowt[0:1, 1, :]], "r_row": r_row,
             "ones128": ones128, "onesb": onesb, "r_ones": r_ones,
             "kc": kc[0:1, 0:1], "r_kc": r_kc, "r_cols": r_cols}

        for b in range(B):
            for ci in range(nch):
                t0 = ci * TC
                if t0 == 0:
                    p.step("gpsimd", lambda e: e.memset(xb[:, :, 0:1], 0.0),
                           writes=tuple(r_xb))
                    p.step("sync", lambda eng, sem, b=b: eng.dma_start(
                        out=xb[:, :, 1:257],
                        in_=x_in[b, :, :, 0:256].rearrange("g p t -> p g t")
                    ).then_inc(sem, 16), writes=tuple(r_xb),
                        reads=(r_w1, r_w2, r_cols), dma=1)
                else:
                    p.step("sync", lambda eng, sem, b=b, t0=t0: eng.dma_start(
                        out=xb[:],
                        in_=x_in[b, :, :, t0 - 1:t0 + 256].rearrange("g p t -> p g t")
                    ).then_inc(sem, 16), writes=tuple(r_xb), dma=1)

                def xs(g):
                    off = 1 if g < 4 else 0
                    return xb[:, g, off:off + TC]

                emit_ln(p, E, xs, r_xb, xn, r_xn, NCG, gcol=gamc, bcol=betc)

                for og in range(NH2):
                    psa, ra = (ps_m0, r_m0) if og % 2 == 0 else (ps_m2, r_m2)
                    psg, rg = (ps_m1, r_m1) if og % 2 == 0 else (ps_m3, r_m3)

                    def mma(e, og=og, psa=psa):
                        for g in range(NCG):
                            ins = e.matmul(psa[:], w_1[:, g, og * 128:(og + 1) * 128],
                                           xn[:, g, :], start=(g == 0), stop=(g == NCG - 1))
                        return ins

                    def mmg(e, og=og, psg=psg):
                        for g in range(NCG):
                            ins = e.matmul(psg[:], w_1[:, g, 2816 + og * 128:2816 + (og + 1) * 128],
                                           xn[:, g, :], start=(g == 0), stop=(g == NCG - 1))
                        return ins
                    es = og % 2
                    p.step("tensor", mma, reads=tuple(r_xn) + (r_w1,), writes=(ra,))
                    p.step("tensor", mmg, reads=tuple(r_xn) + (r_w1,), writes=(rg,))
                    p.step("scalar", lambda e, og=og, es=es, psg=psg: e.activation(
                        ev[:, es, :], psg[:], AF.Gelu, bias=b1c(22 + og)),
                        reads=(rg, r_cols), writes=(r_ev[es],))
                    p.step("vector", lambda e, og=og, es=es, psa=psa: e.scalar_tensor_tensor(
                        hb[:, og, :], psa[:], b1c(og), ev[:, es, :], ALU.add, ALU.mult),
                        reads=(ra, r_cols, r_ev[es]), writes=(r_hb[og],))

                for og in range(NCG):
                    psb, psr = [(ps_m0, r_m0), (ps_m1, r_m1), (ps_m2, r_m2),
                                (ps_m3, r_m3)][og % 4]
                    es = og % 2

                    def mm2(e, og=og, psb=psb):
                        for kk in range(NH2):
                            ins = e.matmul(psb[:], w_2[:, kk, og * 128:(og + 1) * 128],
                                           hb[:, kk, :], start=(kk == 0), stop=(kk == NH2 - 1))
                        return ins
                    p.step("tensor", mm2, reads=tuple(r_hb) + (r_w2,), writes=(psr,))
                    p.step("scalar", lambda e, og=og, psb=psb, es=es: e.activation(
                        ev[:, es, :], psb[:], AF.Identity, bias=b2c(og)),
                        reads=(psr, r_cols), writes=(r_ev[es],))
                    p.step("vector", lambda e, og=og, es=es: e.tensor_add(
                        ev[:, es, :], ev[:, es, :], xb[:, og, 1:257]),
                        reads=(r_ev[es], r_xb[og]), writes=(r_ev[es],))
                    if last:
                        p.step("sync", lambda eng, sem, b=b, og=og, t0=t0, es=es: eng.dma_start(
                            out=xr[:, es, :], in_=x3i[b, og, :, t0:t0 + TC]).then_inc(sem, 16),
                            writes=(r_xr[es],), dma=1)
                        p.step("gpsimd", lambda e, es=es: e.tensor_add(
                            ev[:, es, :], ev[:, es, :], xr[:, es, :]),
                            reads=(r_ev[es], r_xr[es]), writes=(r_ev[es],))
                    p.step("sync", lambda eng, sem, b=b, og=og, t0=t0, es=es: eng.dma_start(
                        out=x_out[b, og, :, t0:t0 + TC], in_=ev[:, es, :]).then_inc(sem, 16),
                        reads=(r_ev[es],), dma=1)

        p.step("gpsimd", None, incs=0, after=[(e, c) for e, c in p.counts.items()])
    names = [yin, f"f_w1{layer}", f"f_w2{layer}", f"f_b1{layer}", f"f_b2{layer}",
             f"f_lng{layer}", f"f_lnb{layer}"]
    if last:
        names.append("x3s")
    return names, [yout]


DPERM = np.array([32 * ((c % 16) // 8) + 8 * (c // 16) + (c % 8) for c in range(64)])


def _rot_tables():
    DH, WINDOW = 64, 256
    half = np.arange(0, DH, 2, dtype=np.float64)
    inv_freq = 1.0 / 10000.0 ** (half / DH)
    t = np.arange(T, dtype=np.float64)
    freqs = t[:, None] * inv_freq[None, :]
    freqs = np.concatenate([freqs, freqs], -1)
    base_scale = (half + 0.4 * DH) / (1.4 * DH)
    power = (t - T // 2) / (WINDOW // 2)
    scale = base_scale[None, :] ** power[:, None]
    scale = np.concatenate([scale, scale], -1)
    cos, sin = np.cos(freqs), np.sin(freqs)
    sign = np.where(np.arange(64) < 32, -1.0, 1.0)
    QK_SCALE = 8.0
    cs = QK_SCALE * (cos * scale).T
    ssb = QK_SCALE * (sign[:, None] * (sin * scale).T)
    csi = (cos / scale).T
    ssbi = (sign[:, None] * (sin / scale).T)
    cs, ssb, csi, ssbi = cs[DPERM], ssb[DPERM], csi[DPERM], ssbi[DPERM]
    rot = np.stack([np.tile(cs, (2, 1)), np.tile(ssb, (2, 1)),
                    np.tile(csi, (2, 1)), np.tile(ssbi, (2, 1))], axis=1)
    return np.ascontiguousarray(rot.astype(np.float32))


def _consts():
    d = {}
    d["f_rot"] = _rot_tables()
    onesh = np.zeros((128, NCG, 16), np.float32)
    exph = np.zeros((16, NCG, 128), np.float32)
    for g in range(NCG):
        for pp in range(128):
            h = 2 * g + (1 if pp >= 64 else 0)
            onesh[pp, g, h] = 1.0
            exph[h, g, pp] = 1.0
    d["f_onesh"] = onesh.reshape(128, NCG * 16)
    d["f_exph"] = exph.reshape(16, NCG * 128)
    perm = np.zeros((128, 128), np.float32)
    for k in range(128):
        perm[k, k ^ 32] = 1.0
    d["f_perm"] = perm
    d["f_ident"] = np.eye(128, dtype=np.float32)
    NEG = -1e30
    pp = np.arange(128)[:, None]
    ff = np.arange(384)[None, :]
    anti = (ff < 128) & (ff >= pp)
    full = (ff >= 128) & (ff < 256)
    diag = (ff >= 256) & (ff - 256 <= pp)
    allowed = anti | full | diag
    m_std = np.where(allowed, 0.0, NEG)
    m_w0q0 = m_std.copy(); m_w0q0[:, 0:256] = NEG
    m_w0q1 = m_std.copy(); m_w0q1[:, 0:128] = NEG
    d["f_masks"] = np.ascontiguousarray(
        np.stack([m_std, m_w0q0, m_w0q1], axis=1).astype(np.float32))
    return d


def _pad_cols(a, n):
    out = np.zeros((a.shape[0], n), a.dtype)
    out[:, :a.shape[1]] = a
    return out


def host_prep(inputs, core):
    b0 = 2 * core
    x = np.ascontiguousarray(
        np.transpose(np.asarray(inputs["L0_attn_x"][b0:b0 + 2]), (0, 2, 1))
    ).reshape(B, NCG, 128, T).astype(np.float32)
    d = {"x0": x}
    d["l0_pwT"] = np.ascontiguousarray(np.asarray(inputs["L0_attn_conv_pw_w"]).T)
    d["l0_b_pw"] = np.ascontiguousarray(np.asarray(inputs["L0_attn_conv_pw_b"]).reshape(NCG, 128).T)
    d["l0_w_dw"] = np.ascontiguousarray(np.transpose(np.asarray(inputs["L0_attn_conv_dw_w"])[:, 0, :].reshape(NCG, 128, 3), (1, 2, 0)).reshape(128, 3 * NCG))
    d["l0_b_dw"] = np.ascontiguousarray(np.asarray(inputs["L0_attn_conv_dw_b"]).reshape(NCG, 128).T)
    d["l0_g_conv"] = np.ascontiguousarray(np.asarray(inputs["L0_attn_conv_norm_g"]).reshape(NCG, 128).T)
    d["l0_g_gru"] = np.ascontiguousarray(np.asarray(inputs["L0_attn_gru_norm_g"]).reshape(NCG, 128).T)
    d["l0_g_ff"] = np.ascontiguousarray(np.asarray(inputs["L0_attn_ff_norm_g"]).reshape(NCG, 128).T)
    d["l0_gruT"] = np.ascontiguousarray(np.asarray(inputs["L0_attn_gru_w"]).T)
    d["l0_ff1T"] = np.ascontiguousarray(np.asarray(inputs["L0_attn_ff_w1"]).T)
    d["l0_b_ff1"] = np.ascontiguousarray(np.asarray(inputs["L0_attn_ff_b1"]).reshape(32, 128).T)
    d["l0_ff2T"] = np.ascontiguousarray(np.asarray(inputs["L0_attn_ff_w2"]).T)
    d["l0_b_ff2"] = np.ascontiguousarray(np.asarray(inputs["L0_attn_ff_b2"]).reshape(NCG, 128).T)
    d["l0f_wgT"] = np.ascontiguousarray(np.asarray(inputs["L0_ffn_w_gate"]).T)
    d["l0f_wuT"] = np.ascontiguousarray(np.asarray(inputs["L0_ffn_w_up"]).T)
    d["l0f_wdT"] = np.ascontiguousarray(np.asarray(inputs["L0_ffn_w_down"]).T)
    d["m_w_in"] = np.ascontiguousarray(np.asarray(inputs["L1_attn_W_in"]))
    cw = np.asarray(inputs["L1_attn_conv_w"]).reshape(16, 128, 4)
    d["m_conv_w"] = np.ascontiguousarray(np.transpose(cw, (1, 2, 0)).reshape(128, 64))
    d["m_conv_b"] = np.ascontiguousarray(np.asarray(inputs["L1_attn_conv_b"]).reshape(16, 128).T)
    xp = np.asarray(inputs["L1_attn_W_xproj"])
    d["m_wxp1"] = np.ascontiguousarray(xp[:, 0:64])
    wx2 = np.zeros((2048, 64), np.float32)
    wx2[:, 0:16] = xp[:, 64:80]
    wx2[:, 32:48] = xp[:, 80:96]
    d["m_wxp2"] = wx2
    d["m_wdt"] = np.ascontiguousarray(np.asarray(inputs["L1_attn_W_dt"]))
    d["m_wout"] = np.ascontiguousarray(np.asarray(inputs["L1_attn_W_out"]))
    sel = np.zeros((64, 32, 128), np.float32)
    for n in range(16):
        sel[n, n, :] = 1.0
        sel[32 + n, 16 + n, :] = 1.0
    d["m_selb"] = sel.reshape(64, 32 * 128)
    al = np.asarray(inputs["L1_attn_A_log"]).reshape(16, 128, 16)
    d["m_alog"] = np.ascontiguousarray(np.transpose(al, (1, 0, 2)).reshape(128, 256))
    d["m_bdt"] = np.ascontiguousarray(np.asarray(inputs["L1_attn_b_dt"]).reshape(16, 128).T)
    d["m_dvec"] = np.ascontiguousarray(np.asarray(inputs["L1_attn_D"]).reshape(16, 128).T)
    d["m_gamma"] = np.ascontiguousarray(np.asarray(inputs["L1_attn_gamma"]).reshape(NCG, 128).T)
    d["m_beta"] = np.ascontiguousarray(np.asarray(inputs["L1_attn_beta"]).reshape(NCG, 128).T)
    return d


def host_prep_l1f(inputs):
    d = _consts()
    wqkv = np.asarray(inputs["L1_ffn_Wqkv"])
    wo = np.asarray(inputs["L1_ffn_Wo"])
    qs = np.asarray(inputs["L1_ffn_q_scale"])
    ks = np.asarray(inputs["L1_ffn_k_scale"])
    w1 = np.asarray(inputs["L1_ffn_W1"])
    b1 = np.asarray(inputs["L1_ffn_b1"])
    w2 = np.asarray(inputs["L1_ffn_W2"])
    b2 = np.asarray(inputs["L1_ffn_b2"])
    lng = np.asarray(inputs["L1_ffn_ln_g"])
    lnb = np.asarray(inputs["L1_ffn_ln_b"])
    qkperm = np.concatenate([h * 64 + DPERM for h in range(16)])
    for l in range(6):
        w = wqkv[l].copy()
        w[:, 0:1024] = w[:, 0:1024][:, qkperm]
        w[:, 1024:2048] = w[:, 1024:2048][:, qkperm]
        d[f"f_wqkv{l}"] = np.ascontiguousarray(w)
        d[f"f_wo{l}"] = np.ascontiguousarray(
            wo[l].reshape(16, 64, D).transpose(1, 0, 2).reshape(64, 16 * D))
        d[f"f_qs{l}"] = np.ascontiguousarray(
            np.stack([np.tile(qs[l][DPERM], 2), np.tile(ks[l][DPERM], 2)], axis=1))
        w1a = _pad_cols(w1[l][:, :2730], 2816)
        w1g = _pad_cols(w1[l][:, 2730:], 2816)
        d[f"f_w1{l}"] = np.ascontiguousarray(np.concatenate([w1a, w1g], axis=1))
        b1a = _pad_cols(b1[l][None, :2730], 2816)[0]
        b1g = _pad_cols(b1[l][None, 2730:], 2816)[0]
        d[f"f_b1{l}"] = np.ascontiguousarray(
            np.concatenate([b1a.reshape(22, 128), b1g.reshape(22, 128)], 0).T)
        w2p = np.zeros((2816, D), np.float32)
        w2p[:2730] = w2[l]
        d[f"f_w2{l}"] = w2p
        d[f"f_b2{l}"] = np.ascontiguousarray(b2[l].reshape(NCG, 128).T)
        d[f"f_lng{l}"] = np.ascontiguousarray(lng[l].reshape(NCG, 128).T)
        d[f"f_lnb{l}"] = np.ascontiguousarray(lnb[l].reshape(NCG, 128).T)
    return d


def phase_list():
    ph = [("p0a", build_p0a), ("p0b", build_p0b), ("p1a", build_p1a),
          ("p1b", build_p1b), ("p2a", build_p2a), ("p2b", build_p2b),
          ("p2c", build_p2c)]
    ynames = ["y0s", "yAs"]
    cur = "y0s"
    for l in range(6):
        nxt = "yAs" if cur != "yAs" else "yBs"
        ph.append((f"att{l}", (lambda l=l, a=cur, b=nxt:
                               lambda nc, p, kinds: build_l1f_attn(nc, p, kinds, l, a, b))()))
        cur2 = nxt
        nxt2 = "outs" if l == 5 else ("yAs" if cur2 != "yAs" else "yBs")
        ph.append((f"ffn{l}", (lambda l=l, a=cur2, b=nxt2:
                               lambda nc, p, kinds: build_l1f_ffn(nc, p, kinds, l, a, b,
                                                                  last=(l == 5)))()))
        cur = nxt2
    return ph


def host_prep_all(inputs, core):
    d = host_prep(inputs, core)
    d.update(host_prep_l1f(inputs))
    return d


FUSED = True

STREAM_KINDS = {k: "Internal" for k in
                ["y2s", "x1s", "hs", "x2s", "us", "zs", "ybs", "x3s", "y0s",
                 "yAs", "yBs"]}


def kernel(**inputs):
    core_inputs = [host_prep_all(inputs, c) for c in range(N_CORES)]
    if not FUSED:
        streams = [dict(ci) for ci in core_inputs]
        for name, build in phase_list():
            nc = new_nc()
            p = Prog(nc)
            in_names, out_names = build(nc, p, {})
            p.emit()
            in_maps = [{k: streams[c][k] for k in in_names} for c in range(N_CORES)]
            res = run_bass_kernel_spmd(nc, in_maps, core_ids=list(range(N_CORES)))
            for c in range(N_CORES):
                for k in out_names:
                    streams[c][k] = res.results[c][k]
        outs = [streams[c]["outs"] for c in range(N_CORES)]
    else:
        nc = new_nc()
        p = Prog(nc)
        all_in = []
        for name, build in phase_list():
            in_names, out_names = build(nc, p, STREAM_KINDS)
            p.barrier()
            all_in.extend(n for n in in_names if n not in STREAM_KINDS)
        p.emit()
        all_in = sorted(set(all_in))
        in_maps = [{k: core_inputs[c][k] for k in all_in} for c in range(N_CORES)]
        res = run_bass_kernel_spmd(nc, in_maps, core_ids=list(range(N_CORES)))
        outs = [res.results[c]["outs"] for c in range(N_CORES)]
    out = np.empty((16, T, D), np.float32)
    for c in range(N_CORES):
        xc = outs[c].reshape(B, D, T)
        out[2 * c:2 * c + 2] = np.transpose(xc, (0, 2, 1))
    return out


# version 08 (final)
# speedup vs baseline: 1.1994x; 1.1994x over previous
import math
from contextlib import ExitStack
import numpy as np
import concourse.bass as bass
import concourse.mybir as mybir
from concourse.bass_utils import run_bass_kernel_spmd

F32 = mybir.dt.float32
BF16 = mybir.dt.bfloat16
I32 = mybir.dt.int32
AF = mybir.ActivationFunctionType
ALU = mybir.AluOpType
AX = mybir.AxisListType

N_CORES = 8
B = 2
T = 2048
D = 1024
NCG = D // 128

ENGINES = ("sync", "tensor", "scalar", "vector", "gpsimd")


class Res:
    __slots__ = ("writer", "readers", "name")

    def __init__(self, name=""):
        self.writer = None
        self.readers = []
        self.name = name


def resv(n, name=""):
    return [Res(f"{name}{i}") for i in range(n)]


class Prog:
    def __init__(self, nc):
        self.nc = nc
        self.steps = []
        self.counts = {e: 0 for e in ENGINES}

    def step(self, engine, fn, reads=(), writes=(), after=(), dma=0, incs=None):
        deps = list(after)
        for r in reads:
            if r is not None and r.writer is not None:
                deps.append(r.writer)
        for w in writes:
            if w is None:
                continue
            if w.writer is not None:
                deps.append(w.writer)
            deps.extend(w.readers)
        waits = {}
        for h in deps:
            if h is None:
                continue
            e2, c2 = h
            if e2 == engine:
                continue
            if c2 > waits.get(e2, 0):
                waits[e2] = c2
        n = 16 * dma if dma else (1 if incs is None else incs)
        self.counts[engine] += n
        h = (engine, self.counts[engine])
        self.steps.append((engine, fn, waits, n, dma))
        for r in reads:
            if r is not None:
                r.readers.append(h)
        for w in writes:
            if w is not None:
                w.writer = h
                w.readers = []
        return h

    def fence(self):
        return [(e, c) for e, c in self.counts.items() if c > 0]

    def barrier(self):
        snap = [(e, c) for e, c in self.counts.items() if c > 0]
        for e in ENGINES:
            self.step(e, None, incs=0, after=snap)

    def emit(self):
        nc = self.nc
        sems = nc._phase_sems
        with nc.Block() as blk:
            for name in ENGINES:
                body = getattr(blk, name)

                def make(nm):
                    def run(eng):
                        self._emit_engine(nm, eng, sems)
                    return run
                body(make(name))

    def _emit_engine(self, name, eng, sems):
        waited = {e: 0 for e in ENGINES}
        sem = sems[name]
        for engine, fn, waits, incs, dma in self.steps:
            if engine != name:
                continue
            for e2, c2 in sorted(waits.items()):
                if c2 > waited[e2]:
                    eng.wait_ge(sems[e2], c2)
                    waited[e2] = c2
            if dma:
                fn(eng, sem)
            elif incs:
                last = fn(eng)
                last.then_inc(sem, incs)
            elif fn is not None:
                fn(eng)


def new_nc():
    nc = bass.Bass("TRN2", target_bir_lowering=False, debug=False)
    nc._phase_sems = {}
    nc._sem_ctxs = []
    nc._dram_cache = {}
    _orig = nc.dram_tensor

    def dram_cached(name, shape, dtype, kind="Internal"):
        if name in nc._dram_cache:
            return nc._dram_cache[name]
        h = _orig(name, shape, dtype, kind=kind)
        nc._dram_cache[name] = h
        return h
    nc.dram_tensor = dram_cached
    for e in ENGINES:
        ctx = nc.semaphore(f"sem_{e}")
        h = ctx.__enter__()
        nc._sem_ctxs.append(ctx)
        nc._phase_sems[e] = h
    return nc


def load_wb16(p, src_ap, wsb, wres, stage, stres, rows, cols, scale=None, dcol=0):
    if rows < 128:
        p.step("gpsimd", lambda eng, sem: eng.dma_start(
            out=wsb[0:rows, 0, dcol:dcol + cols], in_=src_ap).then_inc(sem, 16),
            writes=(wres,), dma=1)
        return
    k = rows // 128
    if scale is None:
        for i in range(k):
            p.step("gpsimd", lambda eng, sem, i=i: eng.dma_start(
                out=wsb[:, i, dcol:dcol + cols],
                in_=src_ap[i * 128:(i + 1) * 128, :]).then_inc(sem, 16),
                writes=(wres,), dma=1)
        return
    for i in range(k):
        sres = stres[i % len(stres)]
        st = stage[i % len(stres)]
        p.step("sync", lambda eng, sem, i=i, st=st: eng.dma_start(
            out=st[:, 0:cols], in_=src_ap[i * 128:(i + 1) * 128, :]).then_inc(sem, 16),
            writes=(sres,), dma=1)
        p.step("scalar", lambda en, i=i, st=st: en.activation(
            wsb[:, i, dcol:dcol + cols], st[:, 0:cols], AF.Copy, scale=scale),
            reads=(sres,), writes=(wres,))


def colload(p, src_ap, dst_slice, cres):
    p.step("sync", lambda eng, sem: eng.dma_start(
        out=dst_slice, in_=src_ap).then_inc(sem, 16), writes=(cres,), dma=1)


def _uname(nc, n):
    i = getattr(nc, "_uid", 0)
    nc._uid = i + 1
    return f"{n}_{i}"


def sball(nc, st, *specs):
    return [st.enter_context(nc.sbuf_tensor(_uname(nc, n), s, d)) for n, s, d in specs]


def psall(nc, st, *specs):
    return [st.enter_context(nc.psum_tensor(_uname(nc, n), s, d)) for n, s, d in specs]


def emit_ln(p, E, src, r_src, out, r_out, ng, gcol=None, bcol=None):
    if not callable(src):
        _s = src
        src = lambda g: _s[:, g, :]
    if not callable(out):
        _o = out
        out = lambda g: _o[:, g, :]
    """LayerNorm over ng*128 channels: out = (src - mu)*rstd [*g + b].
    E: env dict with c16/r_c16, sq16/r_sq, dd/r_dd, ps_s1/r_s1, ps_s2/r_s2,
    ps_mu/r_mu, ps_r/r_r, rows (2 row APs), r_row, ones128, onesb, r_ones,
    kc eps col, r_kc. src/out/dd slabs indexed [:, g, :]."""
    DN = ng * 128
    for g in range(ng):
        p.step("scalar", lambda e, g=g: e.copy(E["c16"][:, g, :], src(g)),
               reads=(r_src[g],), writes=(E["r_c16"][g],))

    def f_s1(e):
        for g in range(ng):
            ins = e.matmul(E["ps_s1"][:], E["ones128"][:], E["c16"][:, g, :],
                           start=(g == 0), stop=(g == ng - 1))
        return ins
    p.step("tensor", f_s1, reads=tuple(E["r_c16"]) + (E["r_ones"],), writes=(E["r_s1"],))
    p.step("scalar", lambda e: e.activation(E["rows"][0], E["ps_s1"][:], AF.Copy, scale=1.0 / DN),
           reads=(E["r_s1"],), writes=(E["r_row"][0],))
    p.step("tensor", lambda e: e.matmul(E["ps_mu"][:], E["onesb"][:], E["rows"][0],
                                        start=True, stop=True),
           reads=(E["r_row"][0], E["r_ones"]), writes=(E["r_mu"],))
    for g in range(ng):
        p.step("vector", lambda e, g=g: e.tensor_sub(E["dd"][:, g, :], src(g), E["ps_mu"][:]),
               reads=(r_src[g], E["r_mu"]), writes=(E["r_dd"][g],))
        p.step("scalar", lambda e, g=g: e.activation(E["sq16"][:, g, :], E["dd"][:, g, :], AF.Square),
               reads=(E["r_dd"][g],), writes=(E["r_sq"][g],))

    def f_s2(e):
        for g in range(ng):
            ins = e.matmul(E["ps_s2"][:], E["ones128"][:], E["sq16"][:, g, :],
                           start=(g == 0), stop=(g == ng - 1))
        return ins
    p.step("tensor", f_s2, reads=tuple(E["r_sq"]) + (E["r_ones"],), writes=(E["r_s2"],))
    p.step("scalar", lambda e: e.activation(E["rows"][1], E["ps_s2"][:], AF.Sqrt,
                                            scale=1.0 / DN, bias=E["kc"]),
           reads=(E["r_s2"], E["r_kc"]), writes=(E["r_row"][1],))
    p.step("vector", lambda e: e.reciprocal(E["rows"][1], E["rows"][1]),
           reads=(E["r_row"][1],), writes=(E["r_row"][1],))
    p.step("tensor", lambda e: e.matmul(E["ps_r"][:], E["onesb"][:], E["rows"][1],
                                        start=True, stop=True),
           reads=(E["r_row"][1], E["r_ones"]), writes=(E["r_r"],))
    for g in range(ng):
        if gcol is None:
            p.step("vector", lambda e, g=g: e.tensor_mul(out(g), E["dd"][:, g, :], E["ps_r"][:]),
                   reads=(E["r_dd"][g], E["r_r"]), writes=(r_out[g],))
        else:
            p.step("vector", lambda e, g=g: e.tensor_mul(E["dd"][:, g, :], E["dd"][:, g, :], E["ps_r"][:]),
                   reads=(E["r_dd"][g], E["r_r"]), writes=(E["r_dd"][g],))
            p.step("vector", lambda e, g=g: e.tensor_scalar(
                out(g), E["dd"][:, g, :], gcol(g), bcol(g), ALU.mult, ALU.add),
                reads=(E["r_dd"][g], E["r_cols"]), writes=(r_out[g],))


def build_p0a(nc, p, kinds):
    TC = 512
    x0 = nc.dram_tensor("x0", [B, NCG, 128, T], F32, kind=kinds.get("x0", "ExternalInput")).ap()
    pwT = nc.dram_tensor("l0_pwT", [D, D], F32, kind="ExternalInput").ap()
    gruT = nc.dram_tensor("l0_gruT", [D, 2 * D], F32, kind="ExternalInput").ap()
    b_pw = nc.dram_tensor("l0_b_pw", [128, NCG], F32, kind="ExternalInput").ap()
    w_dw = nc.dram_tensor("l0_w_dw", [128, 3 * NCG], F32, kind="ExternalInput").ap()
    b_dw = nc.dram_tensor("l0_b_dw", [128, NCG], F32, kind="ExternalInput").ap()
    g_conv = nc.dram_tensor("l0_g_conv", [128, NCG], F32, kind="ExternalInput").ap()
    g_gru = nc.dram_tensor("l0_g_gru", [128, NCG], F32, kind="ExternalInput").ap()
    y2o = nc.dram_tensor("y2s", [B, NCG, 128, T], F32, kind=kinds.get("y2s", "ExternalOutput")).ap()

    nch = T // TC
    with ExitStack() as est:
        (stage_t, w_pw, w_gru, ones128, onesb, cols, xb, xb16, dd, sq16, u1b,
         cvt, cv16, ev, u2b, ga, gs, ggt, gb, hsc, rowt, carry, kc) = sball(nc, est,
            ("stage", [128, 2, 2048], F32), ("w_pw", [128, NCG, D], BF16),
            ("w_gru", [128, NCG, 2 * D], BF16), ("ones128", [128, 1], BF16),
            ("onesb", [1, 128], F32), ("cols", [128, 7 * NCG], F32),
            ("xb", [128, NCG, 512], F32), ("xb16", [128, NCG, 512], BF16),
            ("dd", [128, NCG, 512], F32), ("sq16", [128, NCG, 512], BF16),
            ("u1b", [128, 2, NCG, 514], BF16), ("cvt", [128, 2, 512], F32),
            ("cv16", [128, NCG, 512], BF16), ("ev", [128, 2, 512], F32),
            ("u2b", [128, NCG, 512], BF16), ("ga", [128, 2, 512], F32),
            ("gs", [128, 2, 512], F32), ("ggt", [128, 2, 512], F32),
            ("gb", [128, 2, 512], F32), ("hsc", [128, NCG, 512], F32),
            ("rowt", [1, 2, 512], F32), ("carry", [128, NCG], F32),
            ("kc", [128, 1], F32))
        (ps_s1, ps_s2, ps_mu, ps_r, ps_m0, ps_m1, ps_m2, ps_m3) = psall(nc, est,
            ("ps_s1", [1, 512], F32), ("ps_s2", [1, 512], F32),
            ("ps_mu", [128, 512], F32), ("ps_r", [128, 512], F32),
            ("ps_m0", [128, 512], F32), ("ps_m1", [128, 512], F32),
            ("ps_m2", [128, 512], F32), ("ps_m3", [128, 512], F32))
        stage = [stage_t[:, 0, :], stage_t[:, 1, :]]
        r_stage = resv(2, "stage")
        r_wpw, r_wgru, r_cols, r_ones = Res("wpw"), Res("wgru"), Res("cols"), Res("ones")
        r_xb = resv(NCG, "xb")
        r_xb16 = resv(NCG, "xb16")
        r_dd = resv(NCG, "dd")
        r_sq = resv(NCG, "sq")
        r_u1 = [resv(NCG, f"u1s{s}") for s in range(2)]
        r_cvt = resv(2, "cvt")
        r_cv16 = resv(NCG, "cv16")
        r_ev = resv(2, "ev")
        r_u2 = resv(NCG, "u2")
        r_ga = resv(2, "ga")
        r_gs = resv(2, "gs")
        r_gg = resv(2, "gg")
        r_gb = resv(2, "gb")
        r_hsc = resv(NCG, "hsc")
        r_carry = resv(NCG, "carry")
        r_row = resv(2, "row")
        r_s1, r_s2, r_mu, r_r, r_m0, r_m1 = (Res("s1"), Res("s2"), Res("mu"),
                                             Res("r"), Res("m0"), Res("m1"))
        r_m2, r_m3 = Res("m2"), Res("m3")
        PSROT = [(ps_m0, r_m0), (ps_m1, r_m1), (ps_m2, r_m2), (ps_m3, r_m3)]

        load_wb16(p, pwT, w_pw, r_wpw, stage, r_stage, D, D)
        load_wb16(p, gruT, w_gru, r_wgru, stage, r_stage, D, 2 * D)
        p.step("vector", lambda e: e.memset(ones128[:], 1.0), writes=(r_ones,))
        p.step("vector", lambda e: e.memset(onesb[:], 1.0), writes=(r_ones,))
        colload(p, b_pw, cols[:, 0:NCG], r_cols)
        colload(p, b_dw, cols[:, NCG:2 * NCG], r_cols)
        colload(p, g_conv, cols[:, 2 * NCG:3 * NCG], r_cols)
        colload(p, g_gru, cols[:, 3 * NCG:4 * NCG], r_cols)
        colload(p, w_dw, cols[:, 4 * NCG:7 * NCG], r_cols)
        p.step("vector", lambda e: e.tensor_scalar_add(
            cols[:, 2 * NCG:4 * NCG], cols[:, 2 * NCG:4 * NCG], 1.0),
            reads=(r_cols,), writes=(r_cols,))
        p.step("vector", lambda e: e.memset(kc[:], 1e-5), writes=(r_cols,))

        def ccol(j):
            return cols[:, j:j + 1]

        rows = [rowt[0:1, 0, :], rowt[0:1, 1, :]]

        def stats_mm(src, srcres, dst, dstres, extra=()):
            def f(e):
                for g in range(NCG):
                    ins = e.matmul(dst[:], ones128[:], src[:, g, :],
                                   start=(g == 0), stop=(g == NCG - 1))
                return ins
            return p.step("tensor", f, reads=tuple(srcres) + (r_ones,) + tuple(extra),
                          writes=(dstres,))

        def bcast(rowap, rres, dst, dres):
            return p.step("tensor", lambda e: e.matmul(dst[:], onesb[:], rowap,
                                                       start=True, stop=True),
                          reads=(rres, r_ones), writes=(dres,))

        for b in range(B):
            for ci in range(nch):
                it = b * nch + ci
                t0 = ci * TC
                sl = it % 2

                p.step("sync", lambda eng, sem, b=b, t0=t0: eng.dma_start(
                    out=xb[:], in_=x0[b, :, :, t0:t0 + TC].rearrange("g p t -> p g t")
                ).then_inc(sem, 16), writes=tuple(r_xb), reads=(r_wpw, r_wgru, r_cols), dma=1)

                for g in range(NCG):
                    p.step("gpsimd", lambda e, g=g: e.tensor_copy(xb16[:, g, :], xb[:, g, :]),
                           reads=(r_xb[g],), writes=(r_xb16[g],))
                stats_mm(xb16, r_xb16, ps_s1, r_s1)
                p.step("scalar", lambda e: e.activation(rows[0], ps_s1[:], AF.Copy, scale=1.0 / D),
                       reads=(r_s1,), writes=(r_row[0],))
                bcast(rows[0], r_row[0], ps_mu, r_mu)
                for g in range(NCG):
                    p.step("vector", lambda e, g=g: e.tensor_sub(dd[:, g, :], xb[:, g, :], ps_mu[:]),
                           reads=(r_xb[g], r_mu), writes=(r_dd[g],))
                    p.step("scalar", lambda e, g=g: e.activation(sq16[:, g, :], dd[:, g, :], AF.Square),
                           reads=(r_dd[g],), writes=(r_sq[g],))
                stats_mm(sq16, r_sq, ps_s2, r_s2)
                p.step("scalar", lambda e: e.activation(rows[1], ps_s2[:], AF.Sqrt,
                                                        scale=1.0 / D, bias=kc[0:1, 0:1]),
                       reads=(r_s2, r_cols), writes=(r_row[1],))
                p.step("vector", lambda e: e.reciprocal(rows[1], rows[1]),
                       reads=(r_row[1],), writes=(r_row[1],))
                bcast(rows[1], r_row[1], ps_r, r_r)
                for g in range(NCG):
                    p.step("vector", lambda e, g=g: e.tensor_mul(dd[:, g, :], dd[:, g, :], ps_r[:]),
                           reads=(r_dd[g], r_r), writes=(r_dd[g],))

                for g in range(NCG):
                    p.step("scalar", lambda e, g=g: e.activation(sq16[:, g, :], dd[:, g, :], AF.Square),
                           reads=(r_dd[g],), writes=(r_sq[g],))
                stats_mm(sq16, r_sq, ps_s1, r_s1)
                p.step("scalar", lambda e: e.activation(rows[0], ps_s1[:], AF.Sqrt),
                       reads=(r_s1,), writes=(r_row[0],))
                p.step("vector", lambda e: e.tensor_scalar(
                    rows[0], rows[0], 1.0 / math.sqrt(D), 1e-12 / math.sqrt(D), ALU.mult, ALU.max),
                    reads=(r_row[0],), writes=(r_row[0],))
                p.step("vector", lambda e: e.reciprocal(rows[0], rows[0]),
                       reads=(r_row[0],), writes=(r_row[0],))
                bcast(rows[0], r_row[0], ps_r, r_r)
                for g in range(NCG):
                    if ci == 0:
                        p.step("gpsimd", lambda e, g=g, sl=sl: e.memset(u1b[:, sl, g, 0:2], 0.0),
                               writes=(r_u1[sl][g],))
                    else:
                        p.step("gpsimd", lambda e, g=g, sl=sl: e.tensor_copy(
                            u1b[:, sl, g, 0:2], u1b[:, 1 - sl, g, TC:TC + 2]),
                            reads=(r_u1[1 - sl][g],), writes=(r_u1[sl][g],))
                    p.step("vector", lambda e, g=g, sl=sl: e.scalar_tensor_tensor(
                        u1b[:, sl, g, 2:2 + TC], dd[:, g, :], ccol(2 * NCG + g), ps_r[:],
                        ALU.mult, ALU.mult),
                        reads=(r_dd[g], r_r, r_cols), writes=(r_u1[sl][g],))

                for g in range(NCG):
                    cs = g % 2
                    p.step("vector", lambda e, g=g, sl=sl, cs=cs: e.tensor_scalar_mul(
                        cvt[:, cs, :], u1b[:, sl, g, 0:TC], ccol(4 * NCG + g)),
                        reads=(r_u1[sl][g], r_cols), writes=(r_cvt[cs],))
                    p.step("vector", lambda e, g=g, sl=sl, cs=cs: e.scalar_tensor_tensor(
                        cvt[:, cs, :], u1b[:, sl, g, 1:1 + TC], ccol(5 * NCG + g), cvt[:, cs, :],
                        ALU.mult, ALU.add),
                        reads=(r_u1[sl][g], r_cols, r_cvt[cs]), writes=(r_cvt[cs],))
                    p.step("vector", lambda e, g=g, sl=sl, cs=cs: e.scalar_tensor_tensor(
                        cvt[:, cs, :], u1b[:, sl, g, 2:2 + TC], ccol(6 * NCG + g), cvt[:, cs, :],
                        ALU.mult, ALU.add),
                        reads=(r_u1[sl][g], r_cols, r_cvt[cs]), writes=(r_cvt[cs],))
                    p.step("gpsimd", lambda e, g=g, cs=cs: e.tensor_scalar_add(
                        cv16[:, g, :], cvt[:, cs, :], ccol(NCG + g)),
                        reads=(r_cvt[cs], r_cols), writes=(r_cv16[g],))

                for og in range(NCG):
                    psb, psr = PSROT[og % 4]
                    es = og % 2

                    def mmpw(e, og=og, psb=psb):
                        for g in range(NCG):
                            ins = e.matmul(psb[:], w_pw[:, g, og * 128:(og + 1) * 128],
                                           cv16[:, g, :], start=(g == 0), stop=(g == NCG - 1))
                        return ins
                    p.step("tensor", mmpw, reads=tuple(r_cv16) + (r_wpw,), writes=(psr,))
                    p.step("scalar", lambda e, og=og, psb=psb, es=es: e.activation(
                        ev[:, es, :], psb[:], AF.Identity, bias=ccol(og)),
                        reads=(psr, r_cols), writes=(r_ev[es],))
                    p.step("vector", lambda e, og=og, es=es: e.tensor_add(
                        dd[:, og, :], dd[:, og, :], ev[:, es, :]),
                        reads=(r_dd[og], r_ev[es]), writes=(r_dd[og],))

                for g in range(NCG):
                    p.step("scalar", lambda e, g=g: e.activation(sq16[:, g, :], dd[:, g, :], AF.Square),
                           reads=(r_dd[g],), writes=(r_sq[g],))
                stats_mm(sq16, r_sq, ps_s1, r_s1)
                p.step("scalar", lambda e: e.activation(rows[0], ps_s1[:], AF.Sqrt),
                       reads=(r_s1,), writes=(r_row[0],))
                p.step("vector", lambda e: e.tensor_scalar(
                    rows[0], rows[0], 1.0 / math.sqrt(D), 1e-12 / math.sqrt(D), ALU.mult, ALU.max),
                    reads=(r_row[0],), writes=(r_row[0],))
                p.step("vector", lambda e: e.reciprocal(rows[0], rows[0]),
                       reads=(r_row[0],), writes=(r_row[0],))
                bcast(rows[0], r_row[0], ps_r, r_r)
                for g in range(NCG):
                    p.step("vector", lambda e, g=g: e.scalar_tensor_tensor(
                        u2b[:, g, :], dd[:, g, :], ccol(3 * NCG + g), ps_r[:], ALU.mult, ALU.mult),
                        reads=(r_dd[g], r_r, r_cols), writes=(r_u2[g],))

                for g in range(NCG):
                    psh, rph = PSROT[(g % 2) * 2]
                    psg2, rpg2 = PSROT[(g % 2) * 2 + 1]

                    def mmh(e, g=g, psh=psh):
                        for kk in range(NCG):
                            ins = e.matmul(psh[:], w_gru[:, kk, g * 128:(g + 1) * 128],
                                           u2b[:, kk, :], start=(kk == 0), stop=(kk == NCG - 1))
                        return ins

                    def mmg(e, g=g, psg2=psg2):
                        for kk in range(NCG):
                            ins = e.matmul(psg2[:], w_gru[:, kk, D + g * 128:D + (g + 1) * 128],
                                           u2b[:, kk, :], start=(kk == 0), stop=(kk == NCG - 1))
                        return ins
                    p.step("tensor", mmh, reads=tuple(r_u2) + (r_wgru,), writes=(rph,))
                    p.step("tensor", mmg, reads=tuple(r_u2) + (r_wgru,), writes=(rpg2,))
                    es = g % 2
                    p.step("scalar", lambda e, es=es, psg2=psg2: e.activation(
                        ga[:, es, :], psg2[:], AF.Sigmoid, scale=-1.0),
                        reads=(rpg2,), writes=(r_ga[es],))
                    p.step("scalar", lambda e, es=es, psg2=psg2: e.activation(
                        gs[:, es, :], psg2[:], AF.Sigmoid),
                        reads=(rpg2,), writes=(r_gs[es],))
                    p.step("scalar", lambda e, es=es, psh=psh: e.activation(
                        ggt[:, es, :], psh[:], AF.Sigmoid),
                        reads=(rph,), writes=(r_gg[es],))
                    p.step("vector", lambda e, es=es, psh=psh: e.scalar_tensor_tensor(
                        ggt[:, es, :], psh[:], 0.5, ggt[:, es, :], ALU.add, ALU.max),
                        reads=(rph, r_gg[es]), writes=(r_gg[es],))
                    p.step("gpsimd", lambda e, es=es: e.tensor_mul(
                        gb[:, es, :], gs[:, es, :], ggt[:, es, :]),
                        reads=(r_gs[es], r_gg[es]), writes=(r_gb[es],))
                    init = 0.0 if ci == 0 else carry[:, g:g + 1]
                    initdeps = () if ci == 0 else (r_carry[g],)
                    p.step("vector", lambda e, g=g, es=es, init=init: e.tensor_tensor_scan(
                        hsc[:, g, :], ga[:, es, :], gb[:, es, :], init, ALU.mult, ALU.add),
                        reads=(r_ga[es], r_gb[es]) + initdeps, writes=(r_hsc[g],))
                    p.step("gpsimd", lambda e, g=g: e.tensor_copy(
                        carry[:, g:g + 1], hsc[:, g, TC - 1:TC]),
                        reads=(r_hsc[g],), writes=(r_carry[g],))
                    p.step("vector", lambda e, g=g: e.tensor_add(
                        dd[:, g, :], dd[:, g, :], hsc[:, g, :]),
                        reads=(r_dd[g], r_hsc[g]), writes=(r_dd[g],))

                p.step("sync", lambda eng, sem, b=b, t0=t0: eng.dma_start(
                    out=y2o[b, :, :, t0:t0 + TC].rearrange("g p t -> p g t"),
                    in_=xb16.tensor.ap() if False else dd[:, :, :]).then_inc(sem, 16),
                    reads=tuple(r_dd), dma=1)

        p.step("gpsimd", None, incs=0,
               after=[(e, c) for e, c in p.counts.items()])
    return ["x0", "l0_pwT", "l0_gruT", "l0_b_pw", "l0_w_dw", "l0_b_dw",
            "l0_g_conv", "l0_g_gru"], ["y2s"]


def build_p0b(nc, p, kinds):
    TC = 256
    x0 = nc.dram_tensor("x0", [B, NCG, 128, T], F32, kind=kinds.get("x0", "ExternalInput")).ap()
    y2i = nc.dram_tensor("y2s", [B, NCG, 128, T], F32, kind=kinds.get("y2s", "ExternalInput")).ap()
    ff1T = nc.dram_tensor("l0_ff1T", [D, 4 * D], F32, kind="ExternalInput").ap()
    ff2T = nc.dram_tensor("l0_ff2T", [4 * D, D], F32, kind="ExternalInput").ap()
    b1 = nc.dram_tensor("l0_b_ff1", [128, 32], F32, kind="ExternalInput").ap()
    b2 = nc.dram_tensor("l0_b_ff2", [128, NCG], F32, kind="ExternalInput").ap()
    g_ff = nc.dram_tensor("l0_g_ff", [128, NCG], F32, kind="ExternalInput").ap()
    x1o = nc.dram_tensor("x1s", [B, NCG, 128, T], F32, kind=kinds.get("x1s", "ExternalOutput")).ap()

    nch = T // TC
    with ExitStack() as est:
        (stage_t, w_f1, w_f2, ones128, onesb, cols, yb, xr, sq16, u3b, hb, ev,
         rowt) = sball(nc, est,
            ("stage", [128, 1, 4096], F32), ("w_f1", [128, NCG, 4 * D], BF16),
            ("w_f2", [128, 32, D], BF16), ("ones128", [128, 1], BF16),
            ("onesb", [1, 128], F32), ("cols", [128, 2 * NCG + 32], F32),
            ("yb", [128, NCG, 256], F32), ("xr", [128, NCG, 256], F32),
            ("sq16", [128, NCG, 256], BF16), ("u3b", [128, NCG, 256], BF16),
            ("hb", [128, 32, 256], BF16), ("ev", [128, 2, 256], F32),
            ("rowt", [1, 2, 256], F32))
        (ps_s1, ps_r, ps_m0, ps_m1, ps_m2, ps_m3) = psall(nc, est,
            ("ps_s1", [1, 256], F32), ("ps_r", [128, 256], F32),
            ("ps_m0", [128, 256], F32), ("ps_m1", [128, 256], F32),
            ("ps_m2", [128, 256], F32), ("ps_m3", [128, 256], F32))
        stage = [stage_t[:, 0, :]]
        r_stage = resv(1, "stage")
        r_w1, r_w2, r_cols, r_ones = Res(), Res(), Res(), Res()
        r_yb = resv(NCG, "yb")
        r_xr = resv(NCG, "xr")
        r_sq = resv(NCG, "sq")
        r_u3 = resv(NCG, "u3")
        r_hb = resv(32, "hb")
        r_ev = resv(2, "ev")
        r_row = resv(2, "row")
        r_s1, r_r, r_m0, r_m1, r_m2, r_m3 = Res(), Res(), Res(), Res(), Res(), Res()
        PSROT = [(ps_m0, r_m0), (ps_m1, r_m1), (ps_m2, r_m2), (ps_m3, r_m3)]

        load_wb16(p, ff1T, w_f1, r_w1, stage, r_stage, D, 4 * D)
        load_wb16(p, ff2T, w_f2, r_w2, stage, r_stage, 4 * D, D)
        p.step("vector", lambda e: e.memset(ones128[:], 1.0), writes=(r_ones,))
        p.step("vector", lambda e: e.memset(onesb[:], 1.0), writes=(r_ones,))
        colload(p, g_ff, cols[:, 0:NCG], r_cols)
        colload(p, b2, cols[:, NCG:2 * NCG], r_cols)
        colload(p, b1, cols[:, 2 * NCG:2 * NCG + 32], r_cols)
        p.step("vector", lambda e: e.tensor_scalar_add(
            cols[:, 0:NCG], cols[:, 0:NCG], 1.0), reads=(r_cols,), writes=(r_cols,))

        def ccol(j):
            return cols[:, j:j + 1]

        rows = [rowt[0:1, 0, :], rowt[0:1, 1, :]]

        for b in range(B):
            for ci in range(nch):
                t0 = ci * TC
                p.step("sync", lambda eng, sem, b=b, t0=t0: eng.dma_start(
                    out=yb[:], in_=y2i[b, :, :, t0:t0 + TC].rearrange("g p t -> p g t")
                ).then_inc(sem, 16), writes=tuple(r_yb), reads=(r_w1, r_w2, r_cols), dma=1)
                p.step("sync", lambda eng, sem, b=b, t0=t0: eng.dma_start(
                    out=xr[:], in_=x0[b, :, :, t0:t0 + TC].rearrange("g p t -> p g t")
                ).then_inc(sem, 16), writes=tuple(r_xr), dma=1)

                for g in range(NCG):
                    p.step("scalar", lambda e, g=g: e.activation(sq16[:, g, :], yb[:, g, :], AF.Square),
                           reads=(r_yb[g],), writes=(r_sq[g],))

                def f_stats(e):
                    for g in range(NCG):
                        ins = e.matmul(ps_s1[:], ones128[:], sq16[:, g, :],
                                       start=(g == 0), stop=(g == NCG - 1))
                    return ins
                p.step("tensor", f_stats, reads=tuple(r_sq) + (r_ones,), writes=(r_s1,))
                p.step("scalar", lambda e: e.activation(rows[0], ps_s1[:], AF.Sqrt),
                       reads=(r_s1,), writes=(r_row[0],))
                p.step("vector", lambda e: e.tensor_scalar(
                    rows[0], rows[0], 1.0 / math.sqrt(D), 1e-12 / math.sqrt(D), ALU.mult, ALU.max),
                    reads=(r_row[0],), writes=(r_row[0],))
                p.step("vector", lambda e: e.reciprocal(rows[0], rows[0]),
                       reads=(r_row[0],), writes=(r_row[0],))
                p.step("tensor", lambda e: e.matmul(ps_r[:], onesb[:], rows[0],
                                                    start=True, stop=True),
                       reads=(r_row[0], r_ones), writes=(r_r,))
                for g in range(NCG):
                    p.step("vector", lambda e, g=g: e.scalar_tensor_tensor(
                        u3b[:, g, :], yb[:, g, :], ccol(g), ps_r[:], ALU.mult, ALU.mult),
                        reads=(r_yb[g], r_r, r_cols), writes=(r_u3[g],))

                for og in range(32):
                    psb, psr = PSROT[og % 4]

                    def mm1(e, og=og, psb=psb):
                        for g in range(NCG):
                            ins = e.matmul(psb[:], w_f1[:, g, og * 128:(og + 1) * 128],
                                           u3b[:, g, :], start=(g == 0), stop=(g == NCG - 1))
                        return ins
                    p.step("tensor", mm1, reads=tuple(r_u3) + (r_w1,), writes=(psr,))
                    p.step("scalar", lambda e, og=og, psb=psb: e.activation(
                        hb[:, og, :], psb[:], AF.Gelu, bias=ccol(2 * NCG + og)),
                        reads=(psr, r_cols), writes=(r_hb[og],))

                for og in range(NCG):
                    psb, psr = PSROT[og % 4]
                    es = og % 2

                    def mm2(e, og=og, psb=psb):
                        for kk in range(32):
                            ins = e.matmul(psb[:], w_f2[:, kk, og * 128:(og + 1) * 128],
                                           hb[:, kk, :], start=(kk == 0), stop=(kk == 31))
                        return ins
                    p.step("tensor", mm2, reads=tuple(r_hb) + (r_w2,), writes=(psr,))
                    p.step("scalar", lambda e, og=og, psb=psb, es=es: e.activation(
                        ev[:, es, :], psb[:], AF.Identity, bias=ccol(NCG + og)),
                        reads=(psr, r_cols), writes=(r_ev[es],))
                    p.step("vector", lambda e, og=og, es=es: e.tensor_add(
                        ev[:, es, :], ev[:, es, :], yb[:, og, :]),
                        reads=(r_ev[es], r_yb[og]), writes=(r_ev[es],))
                    p.step("gpsimd", lambda e, og=og, es=es: e.tensor_add(
                        xr[:, og, :], xr[:, og, :], ev[:, es, :]),
                        reads=(r_xr[og], r_ev[es]), writes=(r_xr[og],))

                p.step("sync", lambda eng, sem, b=b, t0=t0: eng.dma_start(
                    out=x1o[b, :, :, t0:t0 + TC].rearrange("g p t -> p g t"),
                    in_=xr[:, :, :]).then_inc(sem, 16),
                    reads=tuple(r_xr), dma=1)

        p.step("gpsimd", None, incs=0, after=[(e, c) for e, c in p.counts.items()])
    return ["x0", "y2s", "l0_ff1T", "l0_ff2T", "l0_b_ff1", "l0_b_ff2", "l0_g_ff"], ["x1s"]


def build_p1a(nc, p, kinds):
    TC = 512
    NH = 24
    x1 = nc.dram_tensor("x1s", [B, NCG, 128, T], F32, kind=kinds.get("x1s", "ExternalInput")).ap()
    wgT = nc.dram_tensor("l0f_wgT", [D, 3072], I32, kind="ExternalInput").ap()
    wuT = nc.dram_tensor("l0f_wuT", [D, 3072], I32, kind="ExternalInput").ap()
    ho = nc.dram_tensor("hs", [B, NH, 128, T], BF16, kind=kinds.get("hs", "ExternalOutput")).ap()

    nch = T // TC
    with ExitStack() as est:
        (stage_t, w_g, w_u, ones128, onesb, xb, xb16, dd, sq16, xnb, hb, ev,
         rowt, kc) = sball(nc, est,
            ("stage", [128, 3072], I32), ("w_g", [128, NCG, 3072], BF16),
            ("w_u", [128, NCG, 3072], BF16), ("ones128", [128, 1], BF16),
            ("onesb", [1, 128], F32), ("xb", [128, NCG, TC], F32),
            ("xb16", [128, NCG, TC], BF16), ("dd", [128, NCG, TC], F32),
            ("sq16", [128, NCG, TC], BF16), ("xnb", [128, NCG, TC], BF16),
            ("hb", [128, NH, TC], BF16), ("ev", [128, 2, TC], F32),
            ("rowt", [1, 2, TC], F32), ("kc", [128, 1], F32))
        (ps_s1, ps_s2, ps_mu, ps_r, ps_m0, ps_m1, ps_m2, ps_m3) = psall(nc, est,
            ("ps_s1", [1, TC], F32), ("ps_s2", [1, TC], F32),
            ("ps_mu", [128, TC], F32), ("ps_r", [128, TC], F32),
            ("ps_m0", [128, TC], F32), ("ps_m1", [128, TC], F32),
            ("ps_m2", [128, TC], F32), ("ps_m3", [128, TC], F32))

        r_stage = resv(1, "stage")
        r_wg, r_wu, r_ones = Res(), Res(), Res()
        r_xb = resv(NCG, "xb")
        r_xb16 = resv(NCG, "xb16")
        r_dd = resv(NCG, "dd")
        r_sq = resv(NCG, "sq")
        r_xn = resv(NCG, "xn")
        r_hb = resv(NH, "hb")
        r_ev = resv(2, "ev")
        r_row = resv(2, "row")
        r_s1, r_s2, r_mu, r_r, r_m0, r_m1 = Res(), Res(), Res(), Res(), Res(), Res()
        r_m2, r_m3 = Res(), Res()

        load_wb16(p, wgT, w_g, r_wg, [stage_t], r_stage, D, 3072, scale=1.0 / 127.0)
        load_wb16(p, wuT, w_u, r_wu, [stage_t], r_stage, D, 3072, scale=1.0 / 127.0)
        p.step("vector", lambda e: e.memset(ones128[:], 1.0), writes=(r_ones,))
        p.step("vector", lambda e: e.memset(onesb[:], 1.0), writes=(r_ones,))
        p.step("vector", lambda e: e.memset(kc[:], 1e-5), writes=(r_ones,))
        rows = [rowt[0:1, 0, :], rowt[0:1, 1, :]]

        for b in range(B):
            for ci in range(nch):
                t0 = ci * TC
                p.step("sync", lambda eng, sem, b=b, t0=t0: eng.dma_start(
                    out=xb[:], in_=x1[b, :, :, t0:t0 + TC].rearrange("g p t -> p g t")
                ).then_inc(sem, 16), writes=tuple(r_xb), reads=(r_wg, r_wu), dma=1)
                for g in range(NCG):
                    p.step("gpsimd", lambda e, g=g: e.tensor_copy(xb16[:, g, :], xb[:, g, :]),
                           reads=(r_xb[g],), writes=(r_xb16[g],))

                def f_s1(e):
                    for g in range(NCG):
                        ins = e.matmul(ps_s1[:], ones128[:], xb16[:, g, :],
                                       start=(g == 0), stop=(g == NCG - 1))
                    return ins
                p.step("tensor", f_s1, reads=tuple(r_xb16) + (r_ones,), writes=(r_s1,))
                p.step("scalar", lambda e: e.activation(rows[0], ps_s1[:], AF.Copy, scale=1.0 / D),
                       reads=(r_s1,), writes=(r_row[0],))
                p.step("tensor", lambda e: e.matmul(ps_mu[:], onesb[:], rows[0],
                                                    start=True, stop=True),
                       reads=(r_row[0], r_ones), writes=(r_mu,))
                for g in range(NCG):
                    p.step("vector", lambda e, g=g: e.tensor_sub(dd[:, g, :], xb[:, g, :], ps_mu[:]),
                           reads=(r_xb[g], r_mu), writes=(r_dd[g],))
                    p.step("scalar", lambda e, g=g: e.activation(sq16[:, g, :], dd[:, g, :], AF.Square),
                           reads=(r_dd[g],), writes=(r_sq[g],))

                def f_s2(e):
                    for g in range(NCG):
                        ins = e.matmul(ps_s2[:], ones128[:], sq16[:, g, :],
                                       start=(g == 0), stop=(g == NCG - 1))
                    return ins
                p.step("tensor", f_s2, reads=tuple(r_sq) + (r_ones,), writes=(r_s2,))
                p.step("scalar", lambda e: e.activation(rows[1], ps_s2[:], AF.Sqrt,
                                                        scale=1.0 / D, bias=kc[0:1, 0:1]),
                       reads=(r_s2, r_ones), writes=(r_row[1],))
                p.step("vector", lambda e: e.reciprocal(rows[1], rows[1]),
                       reads=(r_row[1],), writes=(r_row[1],))
                p.step("tensor", lambda e: e.matmul(ps_r[:], onesb[:], rows[1],
                                                    start=True, stop=True),
                       reads=(r_row[1], r_ones), writes=(r_r,))
                for g in range(NCG):
                    p.step("vector", lambda e, g=g: e.tensor_mul(xnb[:, g, :], dd[:, g, :], ps_r[:]),
                           reads=(r_dd[g], r_r), writes=(r_xn[g],))

                for og in range(NH):
                    psg3, rg3 = (ps_m0, r_m0) if og % 2 == 0 else (ps_m2, r_m2)
                    psu, ru = (ps_m1, r_m1) if og % 2 == 0 else (ps_m3, r_m3)

                    def mmg_(e, og=og, psg3=psg3):
                        for g in range(NCG):
                            ins = e.matmul(psg3[:], w_g[:, g, og * 128:(og + 1) * 128],
                                           xnb[:, g, :], start=(g == 0), stop=(g == NCG - 1))
                        return ins

                    def mmu_(e, og=og, psu=psu):
                        for g in range(NCG):
                            ins = e.matmul(psu[:], w_u[:, g, og * 128:(og + 1) * 128],
                                           xnb[:, g, :], start=(g == 0), stop=(g == NCG - 1))
                        return ins
                    es = og % 2
                    p.step("tensor", mmg_, reads=tuple(r_xn) + (r_wg,), writes=(rg3,))
                    p.step("tensor", mmu_, reads=tuple(r_xn) + (r_wu,), writes=(ru,))
                    p.step("scalar", lambda e, es=es, psg3=psg3: e.activation(
                        ev[:, es, :], psg3[:], AF.Silu),
                        reads=(rg3,), writes=(r_ev[es],))
                    p.step("vector", lambda e, og=og, es=es, psu=psu: e.tensor_mul(
                        hb[:, og, :], ev[:, es, :], psu[:]),
                        reads=(r_ev[es], ru), writes=(r_hb[og],))
                p.step("sync", lambda eng, sem, b=b, t0=t0: eng.dma_start(
                    out=ho[b, :, :, t0:t0 + TC].rearrange("g p t -> p g t"),
                    in_=hb[:, :, :]).then_inc(sem, 16),
                    reads=tuple(r_hb), dma=1)

        p.step("gpsimd", None, incs=0, after=[(e, c) for e, c in p.counts.items()])
    return ["x1s", "l0f_wgT", "l0f_wuT"], ["hs"]


def build_p1b(nc, p, kinds):
    TC = 512
    NH = 24
    x1 = nc.dram_tensor("x1s", [B, NCG, 128, T], F32, kind=kinds.get("x1s", "ExternalInput")).ap()
    hi = nc.dram_tensor("hs", [B, NH, 128, T], BF16, kind=kinds.get("hs", "ExternalInput")).ap()
    wdT = nc.dram_tensor("l0f_wdT", [3072, D], I32, kind="ExternalInput").ap()
    x2o = nc.dram_tensor("x2s", [B, NCG, 128, T], F32, kind=kinds.get("x2s", "ExternalOutput")).ap()

    nch = T // TC
    with ExitStack() as est:
        (stage_t, w_d, xb, hb, ev) = sball(nc, est,
            ("stage", [128, 1024], I32), ("w_d", [128, NH, D], BF16),
            ("xb", [128, NCG, TC], F32), ("hb", [128, NH, TC], BF16),
            ("ev", [128, 2, TC], F32))
        (ps_m0, ps_m1, ps_m2, ps_m3) = psall(nc, est,
            ("ps_m0", [128, TC], F32), ("ps_m1", [128, TC], F32),
            ("ps_m2", [128, TC], F32), ("ps_m3", [128, TC], F32))
        r_stage = resv(1, "stage")
        r_wd = Res()
        r_xb = resv(NCG, "xb")
        r_hb = resv(NH, "hb")
        r_ev = resv(2, "ev")
        r_m0, r_m1, r_m2, r_m3 = Res(), Res(), Res(), Res()
        PSROT = [(ps_m0, r_m0), (ps_m1, r_m1), (ps_m2, r_m2), (ps_m3, r_m3)]

        load_wb16(p, wdT, w_d, r_wd, [stage_t], r_stage, 3072, D, scale=1.0 / 127.0)

        for b in range(B):
            for ci in range(nch):
                t0 = ci * TC
                p.step("sync", lambda eng, sem, b=b, t0=t0: eng.dma_start(
                    out=hb[:], in_=hi[b, :, :, t0:t0 + TC].rearrange("g p t -> p g t")
                ).then_inc(sem, 16), writes=tuple(r_hb), reads=(r_wd,), dma=1)
                p.step("sync", lambda eng, sem, b=b, t0=t0: eng.dma_start(
                    out=xb[:], in_=x1[b, :, :, t0:t0 + TC].rearrange("g p t -> p g t")
                ).then_inc(sem, 16), writes=tuple(r_xb), dma=1)
                for og in range(NCG):
                    psb, psr = PSROT[og % 4]
                    es = og % 2

                    def mmd(e, og=og, psb=psb):
                        for kk in range(NH):
                            ins = e.matmul(psb[:], w_d[:, kk, og * 128:(og + 1) * 128],
                                           hb[:, kk, :], start=(kk == 0), stop=(kk == NH - 1))
                        return ins
                    p.step("tensor", mmd, reads=tuple(r_hb) + (r_wd,), writes=(psr,))
                    p.step("vector", lambda e, og=og, psb=psb: e.tensor_add(
                        xb[:, og, :], xb[:, og, :], psb[:]),
                        reads=(r_xb[og], psr), writes=(r_xb[og],))
                p.step("sync", lambda eng, sem, b=b, t0=t0: eng.dma_start(
                    out=x2o[b, :, :, t0:t0 + TC].rearrange("g p t -> p g t"),
                    in_=xb[:, :, :]).then_inc(sem, 16),
                    reads=tuple(r_xb), dma=1)

        p.step("gpsimd", None, incs=0, after=[(e, c) for e, c in p.counts.items()])
    return ["x1s", "hs", "l0f_wdT"], ["x2s"]


def build_p2a(nc, p, kinds):
    TC = 256
    ND = 16
    x2 = nc.dram_tensor("x2s", [B, NCG, 128, T], F32, kind=kinds.get("x2s", "ExternalInput")).ap()
    w_in = nc.dram_tensor("m_w_in", [D, 4 * D], F32, kind="ExternalInput").ap()
    conv_w = nc.dram_tensor("m_conv_w", [128, 4 * ND], F32, kind="ExternalInput").ap()
    conv_b = nc.dram_tensor("m_conv_b", [128, ND], F32, kind="ExternalInput").ap()
    uo = nc.dram_tensor("us", [B, ND, 128, T], BF16, kind=kinds.get("us", "ExternalOutput")).ap()
    zo = nc.dram_tensor("zs", [B, ND, 128, T], BF16, kind=kinds.get("zs", "ExternalOutput")).ap()

    nch = T // TC
    with ExitStack() as est:
        (stage_t, w_i, ones128, onesb, cols, xb, c16, dd, sq16, xnb, xinb, zb,
         ub, cvt, rowt, kc) = sball(nc, est,
            ("stage", [128, 4096], F32), ("w_i", [128, NCG, 4 * D], BF16),
            ("ones128", [128, 1], BF16), ("onesb", [1, 128], F32),
            ("cols", [128, 5 * ND], F32), ("xb", [128, NCG, TC], F32),
            ("c16", [128, NCG, TC], BF16), ("dd", [128, NCG, TC], F32),
            ("sq16", [128, NCG, TC], BF16), ("xnb", [128, NCG, TC], BF16),
            ("xinb", [128, ND, TC + 3], F32), ("zb", [128, ND, TC], BF16),
            ("ub", [128, ND, TC], BF16), ("cvt", [128, 2, TC], F32),
            ("rowt", [1, 2, TC], F32), ("kc", [128, 1], F32))
        (ps_s1, ps_s2, ps_mu, ps_r, ps_m0, ps_m1, ps_m2, ps_m3) = psall(nc, est,
            ("ps_s1", [1, TC], F32), ("ps_s2", [1, TC], F32),
            ("ps_mu", [128, TC], F32), ("ps_r", [128, TC], F32),
            ("ps_m0", [128, TC], F32), ("ps_m1", [128, TC], F32),
            ("ps_m2", [128, TC], F32), ("ps_m3", [128, TC], F32))

        r_stage = resv(1, "stage")
        r_wi, r_cols, r_ones, r_kc = Res(), Res(), Res(), Res()
        r_xb = resv(NCG, "xb")
        r_c16 = resv(NCG, "c16")
        r_dd = resv(NCG, "dd")
        r_sq = resv(NCG, "sq")
        r_xn = resv(NCG, "xn")
        r_xin = resv(ND, "xin")
        r_zb = resv(ND, "zb")
        r_ub = resv(ND, "ub")
        r_cvt = resv(2, "cvt")
        r_row = resv(2, "row")
        r_s1, r_s2, r_mu, r_r, r_m0, r_m1 = Res(), Res(), Res(), Res(), Res(), Res()
        r_m2, r_m3 = Res(), Res()
        PSROT = [(ps_m0, r_m0), (ps_m1, r_m1), (ps_m2, r_m2), (ps_m3, r_m3)]

        load_wb16(p, w_in, w_i, r_wi, [stage_t], r_stage, D, 4 * D)
        p.step("vector", lambda e: e.memset(ones128[:], 1.0), writes=(r_ones,))
        p.step("vector", lambda e: e.memset(onesb[:], 1.0), writes=(r_ones,))
        p.step("vector", lambda e: e.memset(kc[:], 1e-5), writes=(r_kc,))
        colload(p, conv_w, cols[:, 0:4 * ND], r_cols)
        colload(p, conv_b, cols[:, 4 * ND:5 * ND], r_cols)

        def ccw(k, dg):
            return cols[:, k * ND + dg:k * ND + dg + 1]

        E = {"c16": c16, "r_c16": r_c16, "sq16": sq16, "r_sq": r_sq, "dd": dd,
             "r_dd": r_dd, "ps_s1": ps_s1, "r_s1": r_s1, "ps_s2": ps_s2, "r_s2": r_s2,
             "ps_mu": ps_mu, "r_mu": r_mu, "ps_r": ps_r, "r_r": r_r,
             "rows": [rowt[0:1, 0, :], rowt[0:1, 1, :]], "r_row": r_row,
             "ones128": ones128, "onesb": onesb, "r_ones": r_ones,
             "kc": kc[0:1, 0:1], "r_kc": r_kc, "r_cols": r_cols}

        for b in range(B):
            for ci in range(nch):
                t0 = ci * TC
                p.step("sync", lambda eng, sem, b=b, t0=t0: eng.dma_start(
                    out=xb[:], in_=x2[b, :, :, t0:t0 + TC].rearrange("g p t -> p g t")
                ).then_inc(sem, 16), writes=tuple(r_xb), reads=(r_wi, r_cols), dma=1)
                emit_ln(p, E, xb, r_xb, dd, r_dd, NCG)
                for g in range(NCG):
                    p.step("gpsimd", lambda e, g=g: e.tensor_copy(xnb[:, g, :], dd[:, g, :]),
                           reads=(r_dd[g],), writes=(r_xn[g],))

                for og in range(32):
                    psb, psr = PSROT[og % 4]

                    def mmw(e, og=og, psb=psb):
                        for g in range(NCG):
                            ins = e.matmul(psb[:], w_i[:, g, og * 128:(og + 1) * 128],
                                           xnb[:, g, :], start=(g == 0), stop=(g == NCG - 1))
                        return ins
                    p.step("tensor", mmw, reads=tuple(r_xn) + (r_wi,), writes=(psr,))
                    if og < ND:
                        if ci == 0:
                            p.step("gpsimd", lambda e, og=og: e.memset(xinb[:, og, 0:3], 0.0),
                                   writes=(r_xin[og],))
                        else:
                            p.step("gpsimd", lambda e, og=og: e.tensor_copy(
                                xinb[:, og, 0:3], xinb[:, og, TC:TC + 3]),
                                reads=(r_xin[og],), writes=(r_xin[og],))
                        p.step("scalar", lambda e, og=og, psb=psb: e.copy(
                            xinb[:, og, 3:3 + TC], psb[:]),
                            reads=(psr,), writes=(r_xin[og],))
                    else:
                        p.step("scalar", lambda e, og=og, psb=psb: e.copy(
                            zb[:, og - ND, :], psb[:]),
                            reads=(psr,), writes=(r_zb[og - ND],))

                for dg in range(ND):
                    cs = dg % 2
                    p.step("vector", lambda e, dg=dg, cs=cs: e.tensor_scalar_mul(
                        cvt[:, cs, :], xinb[:, dg, 0:TC], ccw(0, dg)),
                        reads=(r_xin[dg], r_cols), writes=(r_cvt[cs],))
                    for k in range(1, 4):
                        p.step("vector", lambda e, dg=dg, cs=cs, k=k: e.scalar_tensor_tensor(
                            cvt[:, cs, :], xinb[:, dg, k:k + TC], ccw(k, dg), cvt[:, cs, :],
                            ALU.mult, ALU.add),
                            reads=(r_xin[dg], r_cols, r_cvt[cs]), writes=(r_cvt[cs],))
                    p.step("scalar", lambda e, dg=dg, cs=cs: e.activation(
                        ub[:, dg, :], cvt[:, cs, :], AF.Silu, bias=ccw(4, dg)),
                        reads=(r_cvt[cs], r_cols), writes=(r_ub[dg],))

                p.step("sync", lambda eng, sem, b=b, t0=t0: eng.dma_start(
                    out=uo[b, :, :, t0:t0 + TC].rearrange("g p t -> p g t"),
                    in_=ub[:, :, :]).then_inc(sem, 16), reads=tuple(r_ub), dma=1)
                p.step("sync", lambda eng, sem, b=b, t0=t0: eng.dma_start(
                    out=zo[b, :, :, t0:t0 + TC].rearrange("g p t -> p g t"),
                    in_=zb[:, :, :]).then_inc(sem, 16), reads=tuple(r_zb), dma=1)

        p.step("gpsimd", None, incs=0, after=[(e, c) for e, c in p.counts.items()])
    return ["x2s", "m_w_in", "m_conv_w", "m_conv_b"], ["us", "zs"]


def build_p2b(nc, p, kinds):
    TC = 512
    ND = 16
    NS = 16
    ui = nc.dram_tensor("us", [B, ND, 128, T], BF16, kind=kinds.get("us", "ExternalInput")).ap()
    zi = nc.dram_tensor("zs", [B, ND, 128, T], BF16, kind=kinds.get("zs", "ExternalInput")).ap()
    wxp1 = nc.dram_tensor("m_wxp1", [2 * D, 64], F32, kind="ExternalInput").ap()
    wxp2 = nc.dram_tensor("m_wxp2", [2 * D, 64], F32, kind="ExternalInput").ap()
    wdt = nc.dram_tensor("m_wdt", [64, 2 * D], F32, kind="ExternalInput").ap()
    selb_d = nc.dram_tensor("m_selb", [64, 2 * NS * 128], F32, kind="ExternalInput").ap()
    alog_d = nc.dram_tensor("m_alog", [128, ND * NS], F32, kind="ExternalInput").ap()
    bdt_d = nc.dram_tensor("m_bdt", [128, ND], F32, kind="ExternalInput").ap()
    dvec_d = nc.dram_tensor("m_dvec", [128, ND], F32, kind="ExternalInput").ap()
    ybo = nc.dram_tensor("ybs", [B, ND, 128, T], BF16, kind=kinds.get("ybs", "ExternalOutput")).ap()

    nch = T // TC
    with ExitStack() as est:
        (stage_t, w_x1, w_x2, w_dt, selb, a_sb, ones128, onesb, cols, kc,
         u_s, z_s, dtb, dub, dAs, bsc, hbuf, yacc, bbs, dl16, bc16, cbs,
         sp, state) = sball(nc, est,
            ("stage", [128, 2048], F32), ("w_x1", [128, ND, 64], BF16),
            ("w_x2", [128, ND, 64], BF16), ("w_dt", [64, 1, 2 * D], BF16),
            ("selb", [64, 2 * NS, 128], BF16),
            ("a_sb", [128, ND * NS], F32), ("ones128", [128, 1], BF16),
            ("onesb", [1, 128], F32), ("cols", [128, 2 * ND], F32),
            ("kc", [128, 2], F32),
            ("u_s", [128, ND, TC], BF16),
            ("z_s", [128, ND, TC], BF16), ("dtb", [128, ND, TC], BF16),
            ("dub", [128, ND, TC], BF16), ("dAs", [128, ND, TC], BF16),
            ("bsc", [128, 2, ND, TC], BF16), ("hbuf", [128, ND, TC], BF16),
            ("yacc", [128, ND, TC], F32), ("bbs", [128, 2, TC], BF16),
            ("dl16", [64, TC], BF16), ("bc16", [64, TC], BF16),
            ("cbs", [128, 2, TC], BF16),
            ("sp", [128, 4, TC], F32),
            ("state", [128, ND * NS], F32))
        (ps_m0, ps_m1, ps_dt, ps_bb, ps_cb0, ps_cb1) = psall(nc, est,
            ("ps_m0", [128, TC], F32), ("ps_m1", [128, TC], F32),
            ("ps_dt", [64, TC], F32), ("ps_bb", [128, TC], F32),
            ("ps_cb0", [128, TC], F32), ("ps_cb1", [128, TC], F32))

        r_stage = resv(1, "stage")
        r_wx1, r_wx2, r_wdt, r_selb, r_asb = (Res(), Res(), Res(), Res(), Res())
        r_cols, r_ones, r_kc = Res(), Res(), Res()
        r_u = resv(ND, "u")
        r_z = resv(ND, "z")
        r_dt = resv(ND, "dt")
        r_du = resv(ND, "du")
        r_dA = resv(ND, "dA")
        r_bsc = [resv(ND, "bscA"), resv(ND, "bscB")]
        r_h = resv(ND, "h")
        r_y = resv(ND, "y")
        r_bbs = resv(2, "bbs")
        r_cbs = resv(2, "cbs")
        r_dl, r_bc = Res(), Res()
        r_sp = resv(4, "sp")
        r_state = [resv(ND, f"st{n}") for n in range(NS)]
        (r_m0, r_m1, r_psdt, r_psbb, r_pscb0, r_pscb1) = (Res(), Res(), Res(), Res(), Res(), Res())

        load_wb16(p, wxp1, w_x1, r_wx1, [stage_t], r_stage, 2 * D, 64)
        load_wb16(p, wxp2, w_x2, r_wx2, [stage_t], r_stage, 2 * D, 64)
        load_wb16(p, wdt, w_dt, r_wdt, [stage_t], r_stage, 64, 2 * D)
        for sh_ in range(2):
            p.step("sync", lambda eng, sem, sh_=sh_: eng.dma_start(
                out=stage_t[0:64, 0:NS * 128],
                in_=selb_d[:, sh_ * NS * 128:(sh_ + 1) * NS * 128]).then_inc(sem, 16),
                writes=(r_stage[0],), dma=1)
            p.step("vector", lambda e, sh_=sh_: e.tensor_copy(
                selb[:, sh_ * NS:(sh_ + 1) * NS, :].rearrange("p a b -> p (a b)"),
                stage_t[0:64, 0:NS * 128]),
                reads=(r_stage[0],), writes=(r_selb,))
        p.step("sync", lambda eng, sem: eng.dma_start(
            out=stage_t[:, 0:ND * NS], in_=alog_d).then_inc(sem, 16),
            writes=(r_stage[0],), dma=1)
        p.step("scalar", lambda e: e.activation(a_sb[:], stage_t[:, 0:ND * NS], AF.Exp),
               reads=(r_stage[0],), writes=(r_asb,))
        p.step("vector", lambda e: e.tensor_scalar_mul(a_sb[:], a_sb[:], -1.0),
               reads=(r_asb,), writes=(r_asb,))
        colload(p, bdt_d, cols[:, 0:ND], r_cols)
        colload(p, dvec_d, cols[:, ND:2 * ND], r_cols)
        p.step("vector", lambda e: e.memset(ones128[:], 1.0), writes=(r_ones,))
        p.step("vector", lambda e: e.memset(onesb[:], 1.0), writes=(r_ones,))
        p.step("vector", lambda e: e.memset(kc[:, 0:1], 1e-5), writes=(r_kc,))
        p.step("vector", lambda e: e.memset(kc[:, 1:2], 1.0), writes=(r_kc,))

        def bdtc(dg):
            return cols[:, dg:dg + 1]

        def dvc(dg):
            return cols[:, ND + dg:ND + dg + 1]

        for b in range(B):
            for ci in range(nch):
                t0 = ci * TC
                p.step("sync", lambda eng, sem, b=b, t0=t0: eng.dma_start(
                    out=u_s[:], in_=ui[b, :, :, t0:t0 + TC].rearrange("g p t -> p g t")
                ).then_inc(sem, 16), writes=tuple(r_u),
                    reads=(r_wx1, r_wx2, r_wdt, r_selb, r_asb, r_cols), dma=1)
                p.step("sync", lambda eng, sem, b=b, t0=t0: eng.dma_start(
                    out=z_s[:], in_=zi[b, :, :, t0:t0 + TC].rearrange("g p t -> p g t")
                ).then_inc(sem, 16), writes=tuple(r_z), dma=1)

                def mmx1(e):
                    for dg in range(ND):
                        ins = e.matmul(ps_dt[:], w_x1[:, dg, :], u_s[:, dg, :],
                                       start=(dg == 0), stop=(dg == ND - 1))
                    return ins
                p.step("tensor", mmx1, reads=tuple(r_u) + (r_wx1,), writes=(r_psdt,))
                p.step("scalar", lambda e: e.copy(dl16[:], ps_dt[:]),
                       reads=(r_psdt,), writes=(r_dl,))

                def mmx2(e):
                    for dg in range(ND):
                        ins = e.matmul(ps_dt[:], w_x2[:, dg, :], u_s[:, dg, :],
                                       start=(dg == 0), stop=(dg == ND - 1))
                    return ins
                p.step("tensor", mmx2, reads=tuple(r_u) + (r_wx2, r_dl), writes=(r_psdt,))
                p.step("scalar", lambda e: e.copy(bc16[:], ps_dt[:]),
                       reads=(r_psdt,), writes=(r_bc,))

                for dg in range(ND):
                    psb, psr = (ps_m0, r_m0) if dg % 2 == 0 else (ps_m1, r_m1)
                    p.step("tensor", lambda e, dg=dg, psb=psb: e.matmul(
                        psb[:], w_dt[0:64, 0, dg * 128:(dg + 1) * 128], dl16[:],
                        start=True, stop=True),
                        reads=(r_dl, r_wdt), writes=(psr,))
                    p.step("scalar", lambda e, dg=dg, psb=psb: e.activation(
                        dAs[:, dg, :], psb[:], AF.Abs, bias=bdtc(dg)),
                        reads=(psr, r_cols), writes=(r_dA[dg],))
                    p.step("scalar", lambda e, dg=dg, psb=psb: e.activation(
                        hbuf[:, dg, :], psb[:], AF.Relu, bias=bdtc(dg)),
                        reads=(psr, r_cols), writes=(r_h[dg],))
                p.step("scalar", lambda e: e.activation(
                    dAs[:].rearrange("p a t -> p (a t)"),
                    dAs[:].rearrange("p a t -> p (a t)"), AF.Exp, scale=-1.0),
                    reads=tuple(r_dA), writes=tuple(r_dA))
                p.step("scalar", lambda e: e.activation(
                    dAs[:].rearrange("p a t -> p (a t)"),
                    dAs[:].rearrange("p a t -> p (a t)"), AF.Ln, bias=kc[:, 1:2]),
                    reads=tuple(r_dA) + (r_kc,), writes=tuple(r_dA))
                p.step("vector", lambda e: e.tensor_add(
                    dtb[:].rearrange("p a t -> p (a t)"),
                    dAs[:].rearrange("p a t -> p (a t)"),
                    hbuf[:].rearrange("p a t -> p (a t)")),
                    reads=tuple(r_dA) + tuple(r_h), writes=tuple(r_dt))

                p.step("vector", lambda e: e.tensor_mul(
                    dub[:].rearrange("p a t -> p (a t)"), dtb[:].rearrange("p a t -> p (a t)"),
                    u_s[:].rearrange("p a t -> p (a t)")),
                    reads=tuple(r_dt) + tuple(r_u), writes=tuple(r_du))

                for n in range(NS):
                    bs = n % 2
                    ps_cb, r_pscb = (ps_cb0, r_pscb0) if bs == 0 else (ps_cb1, r_pscb1)
                    ps_bbx, r_psbbx = (ps_bb, r_psbb) if bs == 0 else (ps_m0, r_m0)
                    p.step("tensor", lambda e, n=n, ps_bbx=ps_bbx: e.matmul(
                        ps_bbx[:], selb[:, n, :], bc16[:], start=True, stop=True),
                        reads=(r_selb, r_bc), writes=(r_psbbx,))
                    p.step("tensor", lambda e, n=n, ps_cb=ps_cb: e.matmul(
                        ps_cb[:], selb[:, NS + n, :], bc16[:], start=True, stop=True),
                        reads=(r_selb, r_bc), writes=(r_pscb,))
                    p.step("scalar", lambda e, bs=bs, ps_bbx=ps_bbx: e.copy(
                        bbs[:, bs, :], ps_bbx[:]),
                        reads=(r_psbbx,), writes=(r_bbs[bs],))
                    p.step("vector", lambda e, bs=bs: e.tensor_mul(
                        bsc[:, bs], dub[:],
                        bbs[:, bs, :].rearrange("p (a t) -> p a t", a=1)
                        .broadcast_to([128, ND, TC])),
                        reads=tuple(r_du) + (r_bbs[bs],), writes=tuple(r_bsc[bs]))
                    for dg in range(ND):
                        p.step("scalar", lambda e, n=n, dg=dg: e.activation(
                            dAs[:, dg, :], dtb[:, dg, :], AF.Exp,
                            scale=a_sb[:, dg * NS + n:dg * NS + n + 1]),
                            reads=(r_dt[dg], r_asb), writes=(r_dA[dg],))
                    for dg in range(ND):
                        if ci == 0:
                            init = 0.0
                            ideps = ()
                        else:
                            init = state[:, n * ND + dg:n * ND + dg + 1]
                            ideps = (r_state[n][dg],)
                        p.step("vector", lambda e, n=n, dg=dg, init=init, bs=bs: e.tensor_tensor_scan(
                            hbuf[:, dg, :], dAs[:, dg, :], bsc[:, bs, dg, :], init,
                            ALU.mult, ALU.add),
                            reads=(r_dA[dg], r_bsc[bs][dg]) + ideps, writes=(r_h[dg],))
                    p.step("vector", lambda e, n=n: e.tensor_copy(
                        state[:, n * ND:(n + 1) * ND],
                        hbuf[:, 0:ND, TC - 1:TC].rearrange("p a b -> p (a b)")),
                        reads=tuple(r_h), writes=tuple(r_state[n]))
                    p.step("scalar", lambda e, bs=bs, ps_cb=ps_cb: e.copy(
                        cbs[:, bs, :], ps_cb[:]),
                        reads=(r_pscb,), writes=(r_cbs[bs],))
                    p.step("vector", lambda e, bs=bs: e.tensor_mul(
                        bsc[:, bs], hbuf[:],
                        cbs[:, bs, :].rearrange("p (a t) -> p a t", a=1)
                        .broadcast_to([128, ND, TC])),
                        reads=tuple(r_h) + (r_cbs[bs],), writes=tuple(r_bsc[bs]))
                    if n == 0:
                        p.step("gpsimd", lambda e, bs=bs: e.tensor_copy(
                            yacc[:].rearrange("p a t -> p (a t)"),
                            bsc[:, bs].rearrange("p a t -> p (a t)")),
                            reads=tuple(r_bsc[bs]), writes=tuple(r_y))
                    elif n % 2 == 1:
                        p.step("vector", lambda e, bs=bs: e.tensor_add(
                            yacc[:].rearrange("p a t -> p (a t)"),
                            yacc[:].rearrange("p a t -> p (a t)"),
                            bsc[:, bs].rearrange("p a t -> p (a t)")),
                            reads=tuple(r_bsc[bs]) + tuple(r_y), writes=tuple(r_y))
                    else:
                        p.step("gpsimd", lambda e, bs=bs: e.tensor_add(
                            yacc[:].rearrange("p a t -> p (a t)"),
                            yacc[:].rearrange("p a t -> p (a t)"),
                            bsc[:, bs].rearrange("p a t -> p (a t)")),
                            reads=tuple(r_bsc[bs]) + tuple(r_y), writes=tuple(r_y))

                for dg in range(ND):
                    p.step("vector", lambda e, dg=dg: e.scalar_tensor_tensor(
                        yacc[:, dg, :], u_s[:, dg, :], dvc(dg), yacc[:, dg, :],
                        ALU.mult, ALU.add),
                        reads=(r_u[dg], r_cols, r_y[dg]), writes=(r_y[dg],))
                p.step("scalar", lambda e: e.activation(
                    dub[:].rearrange("p a t -> p (a t)"),
                    z_s[:].rearrange("p a t -> p (a t)"), AF.Silu),
                    reads=tuple(r_z), writes=tuple(r_du))
                p.step("gpsimd", lambda e: e.tensor_mul(
                    u_s[:].rearrange("p a t -> p (a t)"),
                    yacc[:].rearrange("p a t -> p (a t)"),
                    dub[:].rearrange("p a t -> p (a t)")),
                    reads=tuple(r_y) + tuple(r_du), writes=tuple(r_u))

                p.step("sync", lambda eng, sem, b=b, t0=t0: eng.dma_start(
                    out=ybo[b, :, :, t0:t0 + TC].rearrange("g p t -> p g t"),
                    in_=u_s[:, :, :]).then_inc(sem, 16), reads=tuple(r_u), dma=1)

        p.step("gpsimd", None, incs=0, after=[(e, c) for e, c in p.counts.items()])
    return ["us", "zs", "m_wxp1", "m_wxp2", "m_wdt", "m_selb",
            "m_alog", "m_bdt", "m_dvec"], ["ybs"]




def build_p2c(nc, p, kinds):
    TC = 512
    ND = 16
    x2 = nc.dram_tensor("x2s", [B, NCG, 128, T], F32, kind=kinds.get("x2s", "ExternalInput")).ap()
    ybi = nc.dram_tensor("ybs", [B, ND, 128, T], BF16, kind=kinds.get("ybs", "ExternalInput")).ap()
    wout = nc.dram_tensor("m_wout", [2 * D, D], F32, kind="ExternalInput").ap()
    gam_d = nc.dram_tensor("m_gamma", [128, NCG], F32, kind="ExternalInput").ap()
    bet_d = nc.dram_tensor("m_beta", [128, NCG], F32, kind="ExternalInput").ap()
    x3o = nc.dram_tensor("x3s", [B, NCG, 128, T], F32, kind=kinds.get("x3s", "ExternalOutput")).ap()
    y0o = nc.dram_tensor("y0s", [B, NCG, 128, T], F32, kind=kinds.get("y0s", "ExternalOutput")).ap()

    nch = T // TC
    with ExitStack() as est:
        (stage_t, w_o, ones128, onesb, cols, kc, ybt, xb, c16, dd, sq16, resb,
         rowt) = sball(nc, est,
            ("stage", [128, 1024], F32), ("w_o", [128, ND, D], BF16),
            ("ones128", [128, 1], BF16), ("onesb", [1, 128], F32),
            ("cols", [128, 2 * NCG], F32), ("kc", [128, 1], F32),
            ("ybt", [128, ND, TC], BF16), ("xb", [128, NCG, TC], F32),
            ("c16", [128, NCG, TC], BF16), ("dd", [128, NCG, TC], F32),
            ("sq16", [128, NCG, TC], BF16), ("resb", [128, NCG, TC], F32),
            ("rowt", [1, 2, TC], F32))
        (ps_s1, ps_s2, ps_mu, ps_r, ps_m0, ps_m1, ps_m2, ps_m3) = psall(nc, est,
            ("ps_s1", [1, TC], F32), ("ps_s2", [1, TC], F32),
            ("ps_mu", [128, TC], F32), ("ps_r", [128, TC], F32),
            ("ps_m0", [128, TC], F32), ("ps_m1", [128, TC], F32),
            ("ps_m2", [128, TC], F32), ("ps_m3", [128, TC], F32))
        r_stage = resv(1, "stage")
        r_wo, r_cols, r_ones, r_kc = Res(), Res(), Res(), Res()
        r_yb = resv(ND, "yb")
        r_xb = resv(NCG, "xb")
        r_c16 = resv(NCG, "c16")
        r_dd = resv(NCG, "dd")
        r_sq = resv(NCG, "sq")
        r_res = resv(NCG, "res")
        r_row = resv(2, "row")
        r_s1, r_s2, r_mu, r_r, r_m0, r_m1 = Res(), Res(), Res(), Res(), Res(), Res()
        r_m2, r_m3 = Res(), Res()
        PSROT = [(ps_m0, r_m0), (ps_m1, r_m1), (ps_m2, r_m2), (ps_m3, r_m3)]

        load_wb16(p, wout, w_o, r_wo, [stage_t], r_stage, 2 * D, D)
        colload(p, gam_d, cols[:, 0:NCG], r_cols)
        colload(p, bet_d, cols[:, NCG:2 * NCG], r_cols)
        p.step("vector", lambda e: e.memset(ones128[:], 1.0), writes=(r_ones,))
        p.step("vector", lambda e: e.memset(onesb[:], 1.0), writes=(r_ones,))
        p.step("vector", lambda e: e.memset(kc[:], 1e-5), writes=(r_kc,))

        def gamc(g):
            return cols[:, g:g + 1]

        def betc(g):
            return cols[:, NCG + g:NCG + g + 1]

        E = {"c16": c16, "r_c16": r_c16, "sq16": sq16, "r_sq": r_sq, "dd": dd,
             "r_dd": r_dd, "ps_s1": ps_s1, "r_s1": r_s1, "ps_s2": ps_s2, "r_s2": r_s2,
             "ps_mu": ps_mu, "r_mu": r_mu, "ps_r": ps_r, "r_r": r_r,
             "rows": [rowt[0:1, 0, :], rowt[0:1, 1, :]], "r_row": r_row,
             "ones128": ones128, "onesb": onesb, "r_ones": r_ones,
             "kc": kc[0:1, 0:1], "r_kc": r_kc, "r_cols": r_cols}

        for b in range(B):
            for ci in range(nch):
                t0 = ci * TC
                p.step("sync", lambda eng, sem, b=b, t0=t0: eng.dma_start(
                    out=ybt[:], in_=ybi[b, :, :, t0:t0 + TC].rearrange("g p t -> p g t")
                ).then_inc(sem, 16), writes=tuple(r_yb), reads=(r_wo, r_cols), dma=1)
                p.step("sync", lambda eng, sem, b=b, t0=t0: eng.dma_start(
                    out=xb[:], in_=x2[b, :, :, t0:t0 + TC].rearrange("g p t -> p g t")
                ).then_inc(sem, 16), writes=tuple(r_xb), dma=1)

                emit_ln(p, E, xb, r_xb, dd, r_dd, NCG)
                for og in range(NCG):
                    psb, psr = PSROT[og % 4]

                    def mmo(e, og=og, psb=psb):
                        for kk in range(ND):
                            ins = e.matmul(psb[:], w_o[:, kk, og * 128:(og + 1) * 128],
                                           ybt[:, kk, :], start=(kk == 0), stop=(kk == ND - 1))
                        return ins
                    p.step("tensor", mmo, reads=tuple(r_yb) + (r_wo,), writes=(psr,))
                    p.step("vector", lambda e, og=og, psb=psb: e.tensor_add(
                        resb[:, og, :], dd[:, og, :], psb[:]),
                        reads=(r_dd[og], psr), writes=(r_res[og],))

                emit_ln(p, E, resb, r_res, resb, r_res, NCG, gcol=gamc, bcol=betc)
                p.step("gpsimd", lambda e: e.tensor_add(
                    xb[:].rearrange("p a t -> p (a t)"), xb[:].rearrange("p a t -> p (a t)"),
                    resb[:].rearrange("p a t -> p (a t)")),
                    reads=tuple(r_res) + tuple(r_xb), writes=tuple(r_xb))
                p.step("sync", lambda eng, sem, b=b, t0=t0: eng.dma_start(
                    out=x3o[b, :, :, t0:t0 + TC].rearrange("g p t -> p g t"),
                    in_=xb[:, :, :]).then_inc(sem, 16), reads=tuple(r_xb), dma=1)
                emit_ln(p, E, xb, r_xb, dd, r_dd, NCG)
                p.step("sync", lambda eng, sem, b=b, t0=t0: eng.dma_start(
                    out=y0o[b, :, :, t0:t0 + TC].rearrange("g p t -> p g t"),
                    in_=dd[:, :, :]).then_inc(sem, 16), reads=tuple(r_dd), dma=1)

        p.step("gpsimd", None, incs=0, after=[(e, c) for e, c in p.counts.items()])
    return ["x2s", "ybs", "m_wout", "m_gamma", "m_beta"], ["x3s", "y0s"]

def build_l1f_attn(nc, p, kinds, layer, yin, yout):
    TC = 512
    x_in = nc.dram_tensor(yin, [B, NCG, 128, T], F32, kind=kinds.get(yin, "ExternalInput")).ap()
    wqkv_d = nc.dram_tensor(f"f_wqkv{layer}", [D, 3 * D], F32, kind="ExternalInput").ap()
    wo_d = nc.dram_tensor(f"f_wo{layer}", [64, 16 * D], F32, kind="ExternalInput").ap()
    qs_d = nc.dram_tensor(f"f_qs{layer}", [128, 2], F32, kind="ExternalInput").ap()
    rot_d = nc.dram_tensor("f_rot", [128, 4, T], F32, kind="ExternalInput").ap()
    onesh_d = nc.dram_tensor("f_onesh", [128, NCG * 16], F32, kind="ExternalInput").ap()
    exph_d = nc.dram_tensor("f_exph", [16, NCG * 128], F32, kind="ExternalInput").ap()
    id_d = nc.dram_tensor("f_ident", [128, 128], F32, kind="ExternalInput").ap()
    msk_d = nc.dram_tensor("f_masks", [128, 3, 384], F32, kind="ExternalInput").ap()
    x_out = nc.dram_tensor(yout, [B, NCG, 128, T], F32, kind=kinds.get(yout, "ExternalOutput")).ap()
    nch = T // TC
    with ExitStack() as est:
        (wqkv, wo64, onesh, exph, identb, qscol, xr, xsb, q_s,
         k_s, qf, sq16, kbuf, vbuf, rot, sbs, pfs, pTs, maskb, av, zsl,
         rn, mcol) = sball(nc, est,
            ("wqkv", [128, NCG, 3 * D], BF16),
            ("wo64", [64, 16, D], BF16), ("onesh", [128, NCG, 16], BF16),
            ("exph", [16, NCG, 128], BF16),
            ("identb", [128, 128], BF16), ("qscol", [128, 2], F32),
            ("xr", [128, NCG, 256], F32), ("xsb", [128, NCG, TC + 1], BF16),
            ("q_s", [128, NCG, TC], BF16), ("k_s", [128, NCG, TC], BF16),
            ("qf", [128, NCG, TC], BF16),
            ("sq16", [128, NCG, TC], BF16), ("kbuf", [128, NCG, 256 + TC], BF16),
            ("vbuf", [128, 6, 1024], BF16), ("rot", [128, 2, TC], BF16),
            ("sbs", [128, 16, 384], BF16), ("pfs", [128, 16, 384], BF16),
            ("pTs", [128, 16, 384], BF16), ("maskb", [128, 3, 384], BF16),
            ("av", [64, 16, 256], BF16), ("zsl", [128, 2, TC], BF16),
            ("rn", [16, TC], BF16),
            ("mcol", [128, 48], F32))
        t1 = sq16
        (ps_a, ps_b, ps_pt, ps_pt2, ps_rb0, ps_hsd, ps_av, ps_av2) = psall(nc, est,
            ("ps_a", [128, 512], F32), ("ps_b", [128, 512], F32),
            ("ps_pt", [128, 384], BF16), ("ps_pt2", [128, 384], BF16),
            ("ps_rb0", [128, TC], F32), ("ps_hsd", [16, TC], F32),
            ("ps_av", [64, 128], F32), ("ps_av2", [64, 128], F32))

        r_wqkv, r_wo, r_consts, r_qsc = Res(), Res(), Res(), Res()
        r_xr = resv(NCG, "xr")
        r_xsb = resv(NCG, "xsb")
        r_qs = resv(NCG, "qs")
        r_ks = resv(NCG, "ks")
        r_qf = resv(NCG, "qf")
        r_sq = resv(NCG, "sq")
        r_t1 = r_sq
        r_kb = resv(NCG, "kb")
        r_vb = resv(6, "vb")
        r_rot = Res("rot")
        r_sb = resv(16, "sb")
        r_pf = resv(16, "pf")
        r_pT = resv(16, "pT")
        r_msk = Res("msk")
        r_av = resv(16, "av")
        r_zs = resv(2, "zs")
        r_rn = Res()
        r_mcol = resv(48, "mcol")
        (rp_m0, rp_m1, rp_m2, rp_m3, rp_rb0, rp_hs, rp_av, rp_av2) = (
            Res(), Res(), Res(), Res(), Res(), Res(), Res(), Res())
        ps_m0, ps_m1 = ps_a, ps_b
        ps_hs = ps_hsd
        ps_rb1, rp_rb1 = ps_rb0, rp_rb0

        load_wb16(p, wqkv_d, wqkv, r_wqkv, [], [], D, 3 * D)
        for j in range(16):
            p.step("gpsimd", lambda eng, sem, j=j: eng.dma_start(
                out=wo64[0:64, j, :], in_=wo_d[:, j * D:(j + 1) * D]).then_inc(sem, 16),
                writes=(r_wo,), dma=1)
        p.step("gpsimd", lambda eng, sem: eng.dma_start(
            out=onesh[:].rearrange("p a b -> p (a b)"), in_=onesh_d).then_inc(sem, 16),
            writes=(r_consts,), dma=1)
        p.step("gpsimd", lambda eng, sem: eng.dma_start(
            out=exph[0:16, :, :].rearrange("p a b -> p (a b)"), in_=exph_d).then_inc(sem, 16),
            writes=(r_consts,), dma=1)
        p.step("gpsimd", lambda eng, sem: eng.dma_start(
            out=identb[:], in_=id_d).then_inc(sem, 16),
            writes=(r_consts,), dma=1)
        colload(p, qs_d, qscol[:], r_qsc)
        p.step("gpsimd", lambda eng, sem: eng.dma_start(
            out=maskb[:].rearrange("p a b -> p (a b)"),
            in_=msk_d.rearrange("p a b -> p (a b)")).then_inc(sem, 16),
            writes=(r_msk,), dma=1)

        for b in range(B):
            for ci in range(nch):
                t0 = ci * TC
                for g in range(NCG):
                    if t0 == 0:
                        p.step("gpsimd", lambda e, g=g: e.memset(xsb[:, g, 0:1], 0.0),
                               writes=(r_xsb[g],))
                        p.step("gpsimd", lambda eng, sem, b=b, g=g: eng.dma_start(
                            out=xsb[:, g, 1:TC + 1], in_=x_in[b, g, :, 0:TC]).then_inc(sem, 16),
                            writes=(r_xsb[g],), reads=(r_wqkv, r_wo, r_consts, r_qsc), dma=1)
                    else:
                        p.step("gpsimd", lambda eng, sem, b=b, g=g, t0=t0: eng.dma_start(
                            out=xsb[:, g, :], in_=x_in[b, g, :, t0 - 1:t0 + TC]).then_inc(sem, 16),
                            writes=(r_xsb[g],), dma=1)
                    p.step("sync", lambda eng, sem, b=b, g=g, t0=t0: eng.dma_start(
                        out=xr[:, g, :], in_=x_in[b, g, :, t0:t0 + 256]).then_inc(sem, 16),
                        writes=(r_xr[g],), dma=1)

                def xv(g, cols=TC):
                    off = 1 if g < 4 else 0
                    return xsb[:, g, off:off + cols]

                if ci == 0:
                    p.step("vector", lambda e: e.memset(
                        kbuf[:].rearrange("p a t -> p (a t)"), 0.0),
                        writes=tuple(r_kb))
                    p.step("vector", lambda e: e.memset(
                        vbuf[:].rearrange("p a t -> p (a t)"), 0.0),
                        writes=tuple(r_vb))
                else:
                    def fsh(e):
                        for g in range(NCG):
                            ins = e.tensor_copy(kbuf[:, g, 0:256], kbuf[:, g, TC:TC + 256])
                        return ins
                    p.step("vector", fsh, reads=tuple(r_kb), writes=tuple(r_kb))
                    p.step("vector", lambda e: e.tensor_copy(
                        vbuf[:, 0:2, :].rearrange("p a t -> p (a t)"),
                        vbuf[:, 4:6, :].rearrange("p a t -> p (a t)")),
                        reads=(r_vb[4], r_vb[5]), writes=(r_vb[0], r_vb[1]))

                for og in range(16):
                    psb, psr = (ps_m0, rp_m0) if og % 2 == 0 else (ps_m1, rp_m1)

                    def mmq(e, og=og, psb=psb):
                        for g in range(NCG):
                            ins = e.matmul(psb[:, 0:TC], wqkv[:, g, og * 128:(og + 1) * 128],
                                           xv(g), start=(g == 0), stop=(g == NCG - 1))
                        return ins
                    p.step("tensor", mmq, reads=tuple(r_xsb) + (r_wqkv,), writes=(psr,))
                    if og < 8:
                        p.step("scalar", lambda e, og=og, psb=psb: e.copy(q_s[:, og, :], psb[:, 0:TC]),
                               reads=(psr,), writes=(r_qs[og],))
                    else:
                        p.step("scalar", lambda e, og=og, psb=psb: e.copy(k_s[:, og - 8, :], psb[:, 0:TC]),
                               reads=(psr,), writes=(r_ks[og - 8],))

                for tt in range(4):
                    slot = 2 + tt
                    for j in range(4):
                        psb, psr = (ps_m0, rp_m0) if j % 2 == 0 else (ps_m1, rp_m1)

                        def mmv(e, tt=tt, j=j, psb=psb):
                            for g in range(NCG):
                                ins = e.matmul(psb[:, 0:256], xv(g)[:, tt * 128:(tt + 1) * 128],
                                               wqkv[:, g, 2 * D + j * 256:2 * D + (j + 1) * 256],
                                               start=(g == 0), stop=(g == NCG - 1))
                            return ins
                        p.step("tensor", mmv, reads=tuple(r_xsb) + (r_wqkv,), writes=(psr,))
                        p.step("scalar", lambda e, slot=slot, j=j, psb=psb: e.copy(
                            vbuf[:, slot, j * 256:(j + 1) * 256], psb[:, 0:256]),
                            reads=(psr,), writes=(r_vb[slot],))


                for which in range(2):
                    srcb = q_s if which == 0 else k_s
                    r_src = r_qs if which == 0 else r_ks
                    p.step("gpsimd", lambda eng, sem, t0=t0, which=which: eng.dma_start(
                        out=rot[:], in_=rot_d[:, 2 * which:2 * which + 2, t0:t0 + TC]
                    ).then_inc(sem, 16), writes=(r_rot,), dma=1)
                    csq = rot[:, 0, :]
                    ssbq = rot[:, 1, :]
                    p.step("gpsimd", lambda e, srcb=srcb: e.tensor_mul(
                        sq16[:].rearrange("p a t -> p (a t)"),
                        srcb[:].rearrange("p a t -> p (a t)"),
                        srcb[:].rearrange("p a t -> p (a t)")),
                        reads=tuple(r_src), writes=tuple(r_sq))

                    def fhs(e):
                        for g in range(NCG):
                            ins = e.matmul(ps_hs[:], onesh[:, g, :], sq16[:, g, :],
                                           start=(g == 0), stop=(g == NCG - 1))
                        return ins
                    p.step("tensor", fhs, reads=tuple(r_sq) + (r_consts,), writes=(rp_hs,))
                    p.step("scalar", lambda e: e.activation(rn[:], ps_hs[:], AF.Sqrt),
                           reads=(rp_hs,), writes=(r_rn,))
                    p.step("vector", lambda e: e.tensor_scalar_max(rn[:], rn[:], 1e-12),
                           reads=(r_rn,), writes=(r_rn,))
                    def frecip(e):
                        with nc.allow_low_precision(reason="bf16 recip of l2 norm"):
                            return e.reciprocal(rn[:], rn[:])
                    p.step("vector", frecip, reads=(r_rn,), writes=(r_rn,))
                    SWMASK = [i ^ 8 for i in range(32)]
                    for g in range(NCG):
                        ps_rb, rp_rb = (ps_rb0, rp_rb0) if g % 2 == 0 else (ps_rb1, rp_rb1)
                        p.step("tensor", lambda e, g=g, ps_rb=ps_rb: e.matmul(
                            ps_rb[:], exph[:, g, :], rn[0:16, :], start=True, stop=True),
                            reads=(r_rn, r_consts), writes=(rp_rb,))
                        p.step("vector", lambda e, g=g, srcb=srcb, which=which, ps_rb=ps_rb: e.scalar_tensor_tensor(
                            t1[:, g, :], srcb[:, g, :], qscol[:, which:which + 1], ps_rb[:],
                            ALU.mult, ALU.mult),
                            reads=(r_src[g], rp_rb, r_qsc), writes=(r_t1[g],))
                        zslot = g % 2
                        p.step("vector", lambda e, g=g, zslot=zslot: e.stream_shuffle(
                            zsl[:, zslot, :], t1[:, g, :], SWMASK),
                            reads=(r_t1[g],), writes=(r_zs[zslot],))
                        p.step("gpsimd", lambda e, g=g, zslot=zslot, ssbq=ssbq: e.tensor_mul(
                            zsl[:, zslot, :], zsl[:, zslot, :], ssbq),
                            reads=(r_zs[zslot], r_rot), writes=(r_zs[zslot],))
                        dst = (qf[:, g, :] if which == 0 else kbuf[:, g, 256:256 + TC])
                        rdst = (r_qf[g] if which == 0 else r_kb[g])
                        p.step("gpsimd", lambda e, g=g, csq=csq, dst=dst: e.tensor_mul(
                            dst, t1[:, g, :], csq),
                            reads=(r_t1[g], r_rot), writes=(rdst,))
                        p.step("vector", lambda e, g=g, zslot=zslot, dst=dst: e.tensor_add(
                            dst, dst, zsl[:, zslot, :]),
                            reads=(rdst, r_zs[zslot]), writes=(rdst,))

                for wi in range(2):
                  for qt in range(2):
                    off = wi * 256 + qt * 128
                    mi = (0 if (ci > 0 or wi > 0) else (1 if qt == 0 else 2))
                    for h in range(16):
                        g = h // 2
                        hb = (h % 2) * 64
                        ps_s, rp_s = (ps_a, rp_m0) if h % 2 == 0 else (ps_b, rp_m1)
                        p.step("tensor", lambda e, g=g, hb=hb, off=off, ps_s=ps_s: e.matmul(
                            ps_s[:, 0:384], qf[hb:hb + 64, g, off:off + 128],
                            kbuf[hb:hb + 64, g, off:off + 384], start=True, stop=True),
                            reads=(r_qf[g], r_kb[g]), writes=(rp_s,))
                        p.step("vector", lambda e, h=h, mi=mi, ps_s=ps_s: e.scalar_tensor_tensor(
                            sbs[:, h, :], ps_s[:, 0:384], 1.0, maskb[:, mi, :], ALU.mult, ALU.add),
                            reads=(rp_s, r_msk), writes=(r_sb[h],))
                        p.step("vector", lambda e, h=h: e.reduce_max(
                            mcol[:, h:h + 1], sbs[:, h, :], axis=AX.X, negate=True),
                            reads=(r_sb[h],), writes=(r_mcol[h],))
                    for h in range(16):
                        p.step("scalar", lambda e, h=h: e.activation(
                            pfs[:, h, :], sbs[:, h, :], AF.Exp, bias=mcol[:, h:h + 1],
                            accum_out=mcol[:, 16 + h:17 + h]),
                            reads=(r_sb[h], r_mcol[h]), writes=(r_pf[h], r_mcol[16 + h]))
                    p.step("scalar", lambda e: e.activation(
                        mcol[:, 32:48], mcol[:, 16:32], AF.Ln),
                        reads=tuple(r_mcol[16:32]), writes=tuple(r_mcol[32:48]))
                    p.step("scalar", lambda e: e.activation(
                        mcol[:, 32:48], mcol[:, 32:48], AF.Exp, scale=-1.0),
                        reads=tuple(r_mcol[32:48]), writes=tuple(r_mcol[32:48]))
                    for h in range(16):
                        p.step("vector", lambda e, h=h: e.tensor_scalar_mul(
                            pfs[:, h, :], pfs[:, h, :], mcol[:, 32 + h:33 + h]),
                            reads=(r_pf[h], r_mcol[32 + h]), writes=(r_pf[h],))
                    for h in range(16):
                        pspt, rppt = (ps_pt, rp_m2) if h % 2 == 0 else (ps_pt2, rp_m3)

                        def ftr(e, h=h, pspt=pspt):
                            for j in range(3):
                                ins = e.transpose(pspt[:, j * 128:(j + 1) * 128],
                                                  pfs[:, h, j * 128:(j + 1) * 128], identb[:])
                            return ins
                        p.step("tensor", ftr, reads=(r_pf[h], r_consts), writes=(rppt,))
                        p.step("vector", lambda e, h=h, pspt=pspt: e.tensor_copy(
                            pTs[:, h, :], pspt[:]),
                            reads=(rppt,), writes=(r_pT[h],))
                    for h in range(16):
                        psv, rpv = (ps_av, rp_av) if h % 2 == 0 else (ps_av2, rp_av2)

                        def fav(e, h=h, qt=qt, wi=wi, psv=psv):
                            for j in range(3):
                                sv = 2 * wi + qt + j
                                ins = e.matmul(psv[:], vbuf[:, sv, h * 64:(h + 1) * 64],
                                               pTs[:, h, j * 128:(j + 1) * 128],
                                               start=(j == 0), stop=(j == 2))
                            return ins
                        p.step("tensor", fav, reads=(r_pT[h], r_vb[2 * wi + qt],
                                                     r_vb[2 * wi + qt + 1],
                                                     r_vb[2 * wi + qt + 2]), writes=(rpv,))
                        p.step("scalar", lambda e, h=h, qt=qt, psv=psv: e.copy(
                            av[:, h, qt * 128:(qt + 1) * 128], psv[:]),
                            reads=(rpv,), writes=(r_av[h],))

                    if qt == 1:
                        for og in range(NCG):
                            psb, psr = (ps_m0, rp_m0) if og % 2 == 0 else (ps_m1, rp_m1)

                            def mmo(e, og=og, psb=psb):
                                for h in range(16):
                                    ins = e.matmul(psb[:, 0:256],
                                                   wo64[:, h, og * 128:(og + 1) * 128],
                                                   av[:, h, :], start=(h == 0), stop=(h == 15))
                                return ins
                            p.step("tensor", mmo, reads=tuple(r_av) + (r_wo,), writes=(psr,))
                            p.step("vector", lambda e, og=og, psb=psb: e.tensor_add(
                                xr[:, og, :], xr[:, og, :], psb[:, 0:256]),
                                reads=(r_xr[og], psr), writes=(r_xr[og],))
                            p.step("sync", lambda eng, sem, b=b, og=og, t0=t0, wi=wi: eng.dma_start(
                                out=x_out[b, og, :, t0 + wi * 256:t0 + (wi + 1) * 256],
                                in_=xr[:, og, :]).then_inc(sem, 16),
                                reads=(r_xr[og],), dma=1)
                            if wi == 0:
                                p.step("sync", lambda eng, sem, b=b, og=og, t0=t0: eng.dma_start(
                                    out=xr[:, og, :],
                                    in_=x_in[b, og, :, t0 + 256:t0 + 512]).then_inc(sem, 16),
                                    writes=(r_xr[og],), dma=1)

        p.step("gpsimd", None, incs=0, after=[(e, c) for e, c in p.counts.items()])
    return [yin, f"f_wqkv{layer}", f"f_wo{layer}", f"f_qs{layer}", "f_rot",
            "f_onesh", "f_exph", "f_ident", "f_masks"], [yout]


def build_l1f_ffn(nc, p, kinds, layer, yin, yout, last=False):
    TC = 256
    NH1 = 44
    NH2 = 22
    x_in = nc.dram_tensor(yin, [B, NCG, 128, T], F32, kind=kinds.get(yin, "ExternalInput")).ap()
    w1_d = nc.dram_tensor(f"f_w1{layer}", [D, 5632], F32, kind="ExternalInput").ap()
    w2_d = nc.dram_tensor(f"f_w2{layer}", [2816, D], F32, kind="ExternalInput").ap()
    b1_d = nc.dram_tensor(f"f_b1{layer}", [128, NH1], F32, kind="ExternalInput").ap()
    b2_d = nc.dram_tensor(f"f_b2{layer}", [128, NCG], F32, kind="ExternalInput").ap()
    g_d = nc.dram_tensor(f"f_lng{layer}", [128, NCG], F32, kind="ExternalInput").ap()
    bl_d = nc.dram_tensor(f"f_lnb{layer}", [128, NCG], F32, kind="ExternalInput").ap()
    if last:
        x3i = nc.dram_tensor("x3s", [B, NCG, 128, T], F32, kind=kinds.get("x3s", "ExternalInput")).ap()
    x_out = nc.dram_tensor(yout, [B, NCG, 128, T], F32, kind=kinds.get(yout, "ExternalOutput")).ap()

    nch = T // TC
    with ExitStack() as est:
        (stage_t, w_1, w_2, ones128, onesb, cols, kc, xb, xn, c16, dd, hb, ev,
         xr, rowt) = sball(nc, est,
            ("stage", [128, 2816], F32), ("w_1", [128, NCG, 5632], BF16),
            ("w_2", [128, NH2, D], BF16), ("ones128", [128, 1], BF16),
            ("onesb", [1, 128], F32), ("cols", [128, NH1 + 4 * NCG], F32),
            ("kc", [128, 1], F32), ("xb", [128, NCG, 257], F32),
            ("xn", [128, NCG, TC], BF16), ("c16", [128, NCG, TC], BF16),
            ("dd", [128, NCG, TC], F32), ("hb", [128, NH2, TC], BF16),
            ("ev", [128, 2, TC], F32), ("xr", [128, 2, TC], F32),
            ("rowt", [1, 2, TC], F32))
        (ps_s1, ps_s2, ps_mu, ps_r, ps_m0, ps_m1, ps_m2, ps_m3) = psall(nc, est,
            ("ps_s1", [1, TC], F32), ("ps_s2", [1, TC], F32),
            ("ps_mu", [128, TC], F32), ("ps_r", [128, TC], F32),
            ("ps_m0", [128, TC], F32), ("ps_m1", [128, TC], F32),
            ("ps_m2", [128, TC], F32), ("ps_m3", [128, TC], F32))

        r_stage = resv(1, "stage")
        r_w1, r_w2, r_cols, r_ones, r_kc = Res(), Res(), Res(), Res(), Res()
        r_xb = resv(NCG, "xb")
        r_xn = resv(NCG, "xn")
        r_c16 = resv(NCG, "c16")
        r_dd = resv(NCG, "dd")
        r_sq = r_c16
        r_hb = resv(NH2, "hb")
        r_ev = resv(2, "ev")
        r_xr = resv(2, "xr")
        r_row = resv(2, "row")
        r_s1, r_s2, r_mu, r_r, r_m0, r_m1 = Res(), Res(), Res(), Res(), Res(), Res()
        r_m2, r_m3 = Res(), Res()

        load_wb16(p, w1_d[:, 0:2816], w_1, r_w1, [stage_t], r_stage, D, 2816)
        load_wb16(p, w1_d[:, 2816:5632], w_1, r_w1, [stage_t], r_stage, D, 2816, dcol=2816)
        load_wb16(p, w2_d, w_2, r_w2, [stage_t], r_stage, 2816, D)
        colload(p, b1_d, cols[:, 0:NH1], r_cols)
        colload(p, b2_d, cols[:, NH1:NH1 + NCG], r_cols)
        colload(p, g_d, cols[:, NH1 + NCG:NH1 + 2 * NCG], r_cols)
        colload(p, bl_d, cols[:, NH1 + 2 * NCG:NH1 + 3 * NCG], r_cols)
        p.step("vector", lambda e: e.memset(ones128[:], 1.0), writes=(r_ones,))
        p.step("vector", lambda e: e.memset(onesb[:], 1.0), writes=(r_ones,))
        p.step("vector", lambda e: e.memset(kc[:], 1e-5), writes=(r_kc,))

        def b1c(j):
            return cols[:, j:j + 1]

        def b2c(g):
            return cols[:, NH1 + g:NH1 + g + 1]

        def gamc(g):
            return cols[:, NH1 + NCG + g:NH1 + NCG + g + 1]

        def betc(g):
            return cols[:, NH1 + 2 * NCG + g:NH1 + 2 * NCG + g + 1]

        E = {"c16": c16, "r_c16": r_c16, "sq16": c16, "r_sq": r_c16, "dd": dd,
             "r_dd": r_dd, "ps_s1": ps_s1, "r_s1": r_s1, "ps_s2": ps_s2, "r_s2": r_s2,
             "ps_mu": ps_mu, "r_mu": r_mu, "ps_r": ps_r, "r_r": r_r,
             "rows": [rowt[0:1, 0, :], rowt[0:1, 1, :]], "r_row": r_row,
             "ones128": ones128, "onesb": onesb, "r_ones": r_ones,
             "kc": kc[0:1, 0:1], "r_kc": r_kc, "r_cols": r_cols}

        for b in range(B):
            for ci in range(nch):
                t0 = ci * TC
                if t0 == 0:
                    p.step("gpsimd", lambda e: e.memset(xb[:, :, 0:1], 0.0),
                           writes=tuple(r_xb))
                    p.step("sync", lambda eng, sem, b=b: eng.dma_start(
                        out=xb[:, :, 1:257],
                        in_=x_in[b, :, :, 0:256].rearrange("g p t -> p g t")
                    ).then_inc(sem, 16), writes=tuple(r_xb),
                        reads=(r_w1, r_w2, r_cols), dma=1)
                else:
                    p.step("sync", lambda eng, sem, b=b, t0=t0: eng.dma_start(
                        out=xb[:],
                        in_=x_in[b, :, :, t0 - 1:t0 + 256].rearrange("g p t -> p g t")
                    ).then_inc(sem, 16), writes=tuple(r_xb), dma=1)

                def xs(g):
                    off = 1 if g < 4 else 0
                    return xb[:, g, off:off + TC]

                emit_ln(p, E, xs, r_xb, xn, r_xn, NCG, gcol=gamc, bcol=betc)

                for og in range(NH2):
                    psa, ra = (ps_m0, r_m0) if og % 2 == 0 else (ps_m2, r_m2)
                    psg, rg = (ps_m1, r_m1) if og % 2 == 0 else (ps_m3, r_m3)

                    def mma(e, og=og, psa=psa):
                        for g in range(NCG):
                            ins = e.matmul(psa[:], w_1[:, g, og * 128:(og + 1) * 128],
                                           xn[:, g, :], start=(g == 0), stop=(g == NCG - 1))
                        return ins

                    def mmg(e, og=og, psg=psg):
                        for g in range(NCG):
                            ins = e.matmul(psg[:], w_1[:, g, 2816 + og * 128:2816 + (og + 1) * 128],
                                           xn[:, g, :], start=(g == 0), stop=(g == NCG - 1))
                        return ins
                    es = og % 2
                    p.step("tensor", mma, reads=tuple(r_xn) + (r_w1,), writes=(ra,))
                    p.step("tensor", mmg, reads=tuple(r_xn) + (r_w1,), writes=(rg,))
                    p.step("scalar", lambda e, og=og, es=es, psg=psg: e.activation(
                        ev[:, es, :], psg[:], AF.Gelu, bias=b1c(22 + og)),
                        reads=(rg, r_cols), writes=(r_ev[es],))
                    p.step("vector", lambda e, og=og, es=es, psa=psa: e.scalar_tensor_tensor(
                        hb[:, og, :], psa[:], b1c(og), ev[:, es, :], ALU.add, ALU.mult),
                        reads=(ra, r_cols, r_ev[es]), writes=(r_hb[og],))

                for og in range(NCG):
                    psb, psr = [(ps_m0, r_m0), (ps_m1, r_m1), (ps_m2, r_m2),
                                (ps_m3, r_m3)][og % 4]
                    es = og % 2

                    def mm2(e, og=og, psb=psb):
                        for kk in range(NH2):
                            ins = e.matmul(psb[:], w_2[:, kk, og * 128:(og + 1) * 128],
                                           hb[:, kk, :], start=(kk == 0), stop=(kk == NH2 - 1))
                        return ins
                    p.step("tensor", mm2, reads=tuple(r_hb) + (r_w2,), writes=(psr,))
                    p.step("scalar", lambda e, og=og, psb=psb, es=es: e.activation(
                        ev[:, es, :], psb[:], AF.Identity, bias=b2c(og)),
                        reads=(psr, r_cols), writes=(r_ev[es],))
                    p.step("vector", lambda e, og=og, es=es: e.tensor_add(
                        ev[:, es, :], ev[:, es, :], xb[:, og, 1:257]),
                        reads=(r_ev[es], r_xb[og]), writes=(r_ev[es],))
                    if last:
                        p.step("sync", lambda eng, sem, b=b, og=og, t0=t0, es=es: eng.dma_start(
                            out=xr[:, es, :], in_=x3i[b, og, :, t0:t0 + TC]).then_inc(sem, 16),
                            writes=(r_xr[es],), dma=1)
                        p.step("gpsimd", lambda e, es=es: e.tensor_add(
                            ev[:, es, :], ev[:, es, :], xr[:, es, :]),
                            reads=(r_ev[es], r_xr[es]), writes=(r_ev[es],))
                    p.step("sync", lambda eng, sem, b=b, og=og, t0=t0, es=es: eng.dma_start(
                        out=x_out[b, og, :, t0:t0 + TC], in_=ev[:, es, :]).then_inc(sem, 16),
                        reads=(r_ev[es],), dma=1)

        p.step("gpsimd", None, incs=0, after=[(e, c) for e, c in p.counts.items()])
    names = [yin, f"f_w1{layer}", f"f_w2{layer}", f"f_b1{layer}", f"f_b2{layer}",
             f"f_lng{layer}", f"f_lnb{layer}"]
    if last:
        names.append("x3s")
    return names, [yout]


DPERM = np.array([32 * ((c % 16) // 8) + 8 * (c // 16) + (c % 8) for c in range(64)])


def _rot_tables():
    DH, WINDOW = 64, 256
    half = np.arange(0, DH, 2, dtype=np.float64)
    inv_freq = 1.0 / 10000.0 ** (half / DH)
    t = np.arange(T, dtype=np.float64)
    freqs = t[:, None] * inv_freq[None, :]
    freqs = np.concatenate([freqs, freqs], -1)
    base_scale = (half + 0.4 * DH) / (1.4 * DH)
    power = (t - T // 2) / (WINDOW // 2)
    scale = base_scale[None, :] ** power[:, None]
    scale = np.concatenate([scale, scale], -1)
    cos, sin = np.cos(freqs), np.sin(freqs)
    sign = np.where(np.arange(64) < 32, -1.0, 1.0)
    QK_SCALE = 8.0
    cs = QK_SCALE * (cos * scale).T
    ssb = QK_SCALE * (sign[:, None] * (sin * scale).T)
    csi = (cos / scale).T
    ssbi = (sign[:, None] * (sin / scale).T)
    cs, ssb, csi, ssbi = cs[DPERM], ssb[DPERM], csi[DPERM], ssbi[DPERM]
    rot = np.stack([np.tile(cs, (2, 1)), np.tile(ssb, (2, 1)),
                    np.tile(csi, (2, 1)), np.tile(ssbi, (2, 1))], axis=1)
    return np.ascontiguousarray(rot.astype(np.float32))


def _consts():
    d = {}
    d["f_rot"] = _rot_tables()
    onesh = np.zeros((128, NCG, 16), np.float32)
    exph = np.zeros((16, NCG, 128), np.float32)
    for g in range(NCG):
        for pp in range(128):
            h = 2 * g + (1 if pp >= 64 else 0)
            onesh[pp, g, h] = 1.0
            exph[h, g, pp] = 1.0
    d["f_onesh"] = onesh.reshape(128, NCG * 16)
    d["f_exph"] = exph.reshape(16, NCG * 128)
    perm = np.zeros((128, 128), np.float32)
    for k in range(128):
        perm[k, k ^ 32] = 1.0
    d["f_perm"] = perm
    d["f_ident"] = np.eye(128, dtype=np.float32)
    NEG = -1e30
    pp = np.arange(128)[:, None]
    ff = np.arange(384)[None, :]
    anti = (ff < 128) & (ff >= pp)
    full = (ff >= 128) & (ff < 256)
    diag = (ff >= 256) & (ff - 256 <= pp)
    allowed = anti | full | diag
    m_std = np.where(allowed, 0.0, NEG)
    m_w0q0 = m_std.copy(); m_w0q0[:, 0:256] = NEG
    m_w0q1 = m_std.copy(); m_w0q1[:, 0:128] = NEG
    d["f_masks"] = np.ascontiguousarray(
        np.stack([m_std, m_w0q0, m_w0q1], axis=1).astype(np.float32))
    return d


def _pad_cols(a, n):
    out = np.zeros((a.shape[0], n), a.dtype)
    out[:, :a.shape[1]] = a
    return out


def host_prep(inputs, core):
    b0 = 2 * core
    x = np.ascontiguousarray(
        np.transpose(np.asarray(inputs["L0_attn_x"][b0:b0 + 2]), (0, 2, 1))
    ).reshape(B, NCG, 128, T).astype(np.float32)
    d = {"x0": x}
    d["l0_pwT"] = np.ascontiguousarray(np.asarray(inputs["L0_attn_conv_pw_w"]).T)
    d["l0_b_pw"] = np.ascontiguousarray(np.asarray(inputs["L0_attn_conv_pw_b"]).reshape(NCG, 128).T)
    d["l0_w_dw"] = np.ascontiguousarray(np.transpose(np.asarray(inputs["L0_attn_conv_dw_w"])[:, 0, :].reshape(NCG, 128, 3), (1, 2, 0)).reshape(128, 3 * NCG))
    d["l0_b_dw"] = np.ascontiguousarray(np.asarray(inputs["L0_attn_conv_dw_b"]).reshape(NCG, 128).T)
    d["l0_g_conv"] = np.ascontiguousarray(np.asarray(inputs["L0_attn_conv_norm_g"]).reshape(NCG, 128).T)
    d["l0_g_gru"] = np.ascontiguousarray(np.asarray(inputs["L0_attn_gru_norm_g"]).reshape(NCG, 128).T)
    d["l0_g_ff"] = np.ascontiguousarray(np.asarray(inputs["L0_attn_ff_norm_g"]).reshape(NCG, 128).T)
    d["l0_gruT"] = np.ascontiguousarray(np.asarray(inputs["L0_attn_gru_w"]).T)
    d["l0_ff1T"] = np.ascontiguousarray(np.asarray(inputs["L0_attn_ff_w1"]).T)
    d["l0_b_ff1"] = np.ascontiguousarray(np.asarray(inputs["L0_attn_ff_b1"]).reshape(32, 128).T)
    d["l0_ff2T"] = np.ascontiguousarray(np.asarray(inputs["L0_attn_ff_w2"]).T)
    d["l0_b_ff2"] = np.ascontiguousarray(np.asarray(inputs["L0_attn_ff_b2"]).reshape(NCG, 128).T)
    d["l0f_wgT"] = np.ascontiguousarray(np.asarray(inputs["L0_ffn_w_gate"]).T)
    d["l0f_wuT"] = np.ascontiguousarray(np.asarray(inputs["L0_ffn_w_up"]).T)
    d["l0f_wdT"] = np.ascontiguousarray(np.asarray(inputs["L0_ffn_w_down"]).T)
    d["m_w_in"] = np.ascontiguousarray(np.asarray(inputs["L1_attn_W_in"]))
    cw = np.asarray(inputs["L1_attn_conv_w"]).reshape(16, 128, 4)
    d["m_conv_w"] = np.ascontiguousarray(np.transpose(cw, (1, 2, 0)).reshape(128, 64))
    d["m_conv_b"] = np.ascontiguousarray(np.asarray(inputs["L1_attn_conv_b"]).reshape(16, 128).T)
    xp = np.asarray(inputs["L1_attn_W_xproj"])
    d["m_wxp1"] = np.ascontiguousarray(xp[:, 0:64])
    wx2 = np.zeros((2048, 64), np.float32)
    wx2[:, 0:16] = xp[:, 64:80]
    wx2[:, 32:48] = xp[:, 80:96]
    d["m_wxp2"] = wx2
    d["m_wdt"] = np.ascontiguousarray(np.asarray(inputs["L1_attn_W_dt"]))
    d["m_wout"] = np.ascontiguousarray(np.asarray(inputs["L1_attn_W_out"]))
    sel = np.zeros((64, 32, 128), np.float32)
    for n in range(16):
        sel[n, n, :] = 1.0
        sel[32 + n, 16 + n, :] = 1.0
    d["m_selb"] = sel.reshape(64, 32 * 128)
    al = np.asarray(inputs["L1_attn_A_log"]).reshape(16, 128, 16)
    d["m_alog"] = np.ascontiguousarray(np.transpose(al, (1, 0, 2)).reshape(128, 256))
    d["m_bdt"] = np.ascontiguousarray(np.asarray(inputs["L1_attn_b_dt"]).reshape(16, 128).T)
    d["m_dvec"] = np.ascontiguousarray(np.asarray(inputs["L1_attn_D"]).reshape(16, 128).T)
    d["m_gamma"] = np.ascontiguousarray(np.asarray(inputs["L1_attn_gamma"]).reshape(NCG, 128).T)
    d["m_beta"] = np.ascontiguousarray(np.asarray(inputs["L1_attn_beta"]).reshape(NCG, 128).T)
    return d


def host_prep_l1f(inputs):
    d = _consts()
    wqkv = np.asarray(inputs["L1_ffn_Wqkv"])
    wo = np.asarray(inputs["L1_ffn_Wo"])
    qs = np.asarray(inputs["L1_ffn_q_scale"])
    ks = np.asarray(inputs["L1_ffn_k_scale"])
    w1 = np.asarray(inputs["L1_ffn_W1"])
    b1 = np.asarray(inputs["L1_ffn_b1"])
    w2 = np.asarray(inputs["L1_ffn_W2"])
    b2 = np.asarray(inputs["L1_ffn_b2"])
    lng = np.asarray(inputs["L1_ffn_ln_g"])
    lnb = np.asarray(inputs["L1_ffn_ln_b"])
    qkperm = np.concatenate([h * 64 + DPERM for h in range(16)])
    for l in range(6):
        w = wqkv[l].copy()
        w[:, 0:1024] = w[:, 0:1024][:, qkperm]
        w[:, 1024:2048] = w[:, 1024:2048][:, qkperm]
        d[f"f_wqkv{l}"] = np.ascontiguousarray(w)
        d[f"f_wo{l}"] = np.ascontiguousarray(
            wo[l].reshape(16, 64, D).transpose(1, 0, 2).reshape(64, 16 * D))
        d[f"f_qs{l}"] = np.ascontiguousarray(
            np.stack([np.tile(qs[l][DPERM], 2), np.tile(ks[l][DPERM], 2)], axis=1))
        w1a = _pad_cols(w1[l][:, :2730], 2816)
        w1g = _pad_cols(w1[l][:, 2730:], 2816)
        d[f"f_w1{l}"] = np.ascontiguousarray(np.concatenate([w1a, w1g], axis=1))
        b1a = _pad_cols(b1[l][None, :2730], 2816)[0]
        b1g = _pad_cols(b1[l][None, 2730:], 2816)[0]
        d[f"f_b1{l}"] = np.ascontiguousarray(
            np.concatenate([b1a.reshape(22, 128), b1g.reshape(22, 128)], 0).T)
        w2p = np.zeros((2816, D), np.float32)
        w2p[:2730] = w2[l]
        d[f"f_w2{l}"] = w2p
        d[f"f_b2{l}"] = np.ascontiguousarray(b2[l].reshape(NCG, 128).T)
        d[f"f_lng{l}"] = np.ascontiguousarray(lng[l].reshape(NCG, 128).T)
        d[f"f_lnb{l}"] = np.ascontiguousarray(lnb[l].reshape(NCG, 128).T)
    return d


def phase_list():
    ph = [("p0a", build_p0a), ("p0b", build_p0b), ("p1a", build_p1a),
          ("p1b", build_p1b), ("p2a", build_p2a), ("p2b", build_p2b),
          ("p2c", build_p2c)]
    ynames = ["y0s", "yAs"]
    cur = "y0s"
    for l in range(6):
        nxt = "yAs" if cur != "yAs" else "yBs"
        ph.append((f"att{l}", (lambda l=l, a=cur, b=nxt:
                               lambda nc, p, kinds: build_l1f_attn(nc, p, kinds, l, a, b))()))
        cur2 = nxt
        nxt2 = "outs" if l == 5 else ("yAs" if cur2 != "yAs" else "yBs")
        ph.append((f"ffn{l}", (lambda l=l, a=cur2, b=nxt2:
                               lambda nc, p, kinds: build_l1f_ffn(nc, p, kinds, l, a, b,
                                                                  last=(l == 5)))()))
        cur = nxt2
    return ph


def host_prep_all(inputs, core):
    d = host_prep(inputs, core)
    d.update(host_prep_l1f(inputs))
    return d


FUSED = True

STREAM_KINDS = {k: "Internal" for k in
                ["y2s", "x1s", "hs", "x2s", "us", "zs", "ybs", "x3s", "y0s",
                 "yAs", "yBs"]}


def kernel(**inputs):
    core_inputs = [host_prep_all(inputs, c) for c in range(N_CORES)]
    if not FUSED:
        streams = [dict(ci) for ci in core_inputs]
        for name, build in phase_list():
            nc = new_nc()
            p = Prog(nc)
            in_names, out_names = build(nc, p, {})
            p.emit()
            in_maps = [{k: streams[c][k] for k in in_names} for c in range(N_CORES)]
            res = run_bass_kernel_spmd(nc, in_maps, core_ids=list(range(N_CORES)))
            for c in range(N_CORES):
                for k in out_names:
                    streams[c][k] = res.results[c][k]
        outs = [streams[c]["outs"] for c in range(N_CORES)]
    else:
        nc = new_nc()
        p = Prog(nc)
        all_in = []
        for name, build in phase_list():
            in_names, out_names = build(nc, p, STREAM_KINDS)
            p.barrier()
            all_in.extend(n for n in in_names if n not in STREAM_KINDS)
        p.emit()
        all_in = sorted(set(all_in))
        in_maps = [{k: core_inputs[c][k] for k in all_in} for c in range(N_CORES)]
        res = run_bass_kernel_spmd(nc, in_maps, core_ids=list(range(N_CORES)))
        outs = [res.results[c]["outs"] for c in range(N_CORES)]
    out = np.empty((16, T, D), np.float32)
    for c in range(N_CORES):
        xc = outs[c].reshape(B, D, T)
        out[2 * c:2 * c + 2] = np.transpose(xc, (0, 2, 1))
    return out
